# version 54
# baseline (speedup 1.0000x reference)
import contextlib
import numpy as np
import concourse.bass as bass
import concourse.mybir as mybir
from concourse.bass_utils import run_bass_kernel_spmd

F32 = mybir.dt.float32
F16 = mybir.dt.float16
BF16 = mybir.dt.bfloat16
AF = mybir.ActivationFunctionType
OP = mybir.AluOpType

D = 1024
DI = 2048
NS = 16
DTR = 64
T = 256
TS = 512
NFC = D // 128
NDC = DI // 128
EPS = 1e-5

ENGS = ("pe", "act", "dve", "pool", "sq")
DMA_SEMS = ("xin0", "xin1", "xin2", "xin3", "xout0", "xout1", "xout2", "xout3",
            "win0", "win1", "win2", "key0", "key1", "key2", "val0", "val1", "val2",
            "stgi0", "stgi1", "scr0", "scr1", "dir",
            "rso0", "rso1", "rsi0", "rsi1", "rbo0", "rbo1", "rbi0", "rbi1",
            "rco0", "rco1", "rci0", "rci1")


class Dep:
    __slots__ = ("w", "r")

    def __init__(self):
        self.w = None
        self.r = []


class Sched:

    def __init__(self):
        self.ops = {e: [] for e in ENGS}
        self.cnt = {}
        self.waited = {e: {} for e in ENGS}
        self.tag = "init"

    def emit(self, eng, fn, reads=(), writes=(), dma=None, extra_waits=()):
        need = {}

        def add(e, c):
            if c > need.get(e, 0):
                need[e] = c

        for d in reads:
            if d.w:
                add(*d.w)
        for d in writes:
            if d.w:
                add(*d.w)
            for (e, c) in d.r:
                add(e, c)
        for (e, c) in extra_waits:
            add(e, c)
        waits = []
        w = self.waited[eng]
        for e, c in need.items():
            if e == eng:
                c = min(c, self.cnt.get(eng, 0))
            if c > w.get(e, 0):
                waits.append((e, c))
                w[e] = c
        if dma is not None:
            self.cnt[dma] = self.cnt.get(dma, 0) + 16
            me = (dma, self.cnt[dma])
            self.ops[eng].append((fn, waits, dma, self.tag))
        else:
            self.cnt[eng] = self.cnt.get(eng, 0) + 1
            me = (eng, self.cnt[eng])
            self.ops[eng].append((fn, waits, eng, self.tag))
        for d in writes:
            d.w = me
            d.r = []
        for d in reads:
            d.r.append(me)


def build_program(L=4096):
    NPAIR = L // TS
    nc = bass.Bass("TRN2", target_bir_lowering=False, debug=False)

    x_in_d = nc.dram_tensor("x_f", [D, L], F32, kind="ExternalInput").ap()
    out_d = nc.dram_tensor("out_f", [D, L], F32, kind="ExternalOutput").ap()
    w_in_d = nc.dram_tensor("w_in", [D, 2 * DI], F32, kind="ExternalInput").ap()
    keys_d = nc.dram_tensor("keys_r", [16, 128, 1024], F32, kind="ExternalInput").ap()
    vals_d = nc.dram_tensor("vals_r", [16, 128, 1024], F32, kind="ExternalInput").ap()
    wout_d = nc.dram_tensor("wout_r", [DI, D], F32, kind="ExternalInput").ap()
    convd_d = nc.dram_tensor("convd_r", [NDC, 128, 512], F32, kind="ExternalInput").ap()
    xproj_d = nc.dram_tensor("xproj_r", [DI, 128], F32, kind="ExternalInput").ap()
    wdt_d = nc.dram_tensor("wdt_r", [DTR + 1, DI], F32, kind="ExternalInput").ap()
    selw_d = nc.dram_tensor("selw_r", [D, 16], F32, kind="ExternalInput").ap()
    alog_d = nc.dram_tensor("alog_r", [128, 256], F32, kind="ExternalInput").ap()
    dsk_d = nc.dram_tensor("dsk_r", [128, 16], F32, kind="ExternalInput").ap()
    cb_d = nc.dram_tensor("cb_r", [128, 16], F32, kind="ExternalInput").ap()
    bdt_d = nc.dram_tensor("bdt_r", [128, 16], F32, kind="ExternalInput").ap()

    win_bf = nc.dram_tensor("win_bf", [D, 2 * DI], BF16, kind="Internal").ap()
    keys_bf = nc.dram_tensor("keys_bf", [16, 128, 1024], BF16, kind="Internal").ap()
    vals_bf = nc.dram_tensor("vals_bf", [16, 128, 1024], BF16, kind="Internal").ap()
    wout_bf = nc.dram_tensor("wout_bf", [DI, D], BF16, kind="Internal").ap()
    convd_bf = nc.dram_tensor("convd_bf", [NDC, 128, 512], BF16, kind="Internal").ap()
    selscr = nc.dram_tensor("selscr", [2, 16, T], BF16, kind="Internal").ap()
    bscr = nc.dram_tensor("bscr", [2, 16, T], BF16, kind="Internal").ap()
    cscr = nc.dram_tensor("cscr", [2, 16, T], BF16, kind="Internal").ap()

    es = contextlib.ExitStack()

    def sb(name, shape, dt):
        return es.enter_context(nc.sbuf_tensor(name, shape, dt))

    xB = [sb(f"x{i}", [128, NFC, T], F32) for i in range(4)]
    xn2 = [sb(f"xn{i}", [128, NFC, T], BF16) for i in range(2)]
    x_inb = sb("x_inb", [128, 8, T + 3], BF16)
    tails = sb("tails", [128, NDC, 3], BF16)
    uB = sb("uB", [128, NDC, TS], BF16)
    zB = sb("zB", [128, NDC, TS], BF16)
    deltaB = sb("deltaB", [128, NDC, TS], BF16)
    ygB = sb("ygB", [128, NDC, TS], BF16)
    BrepB = sb("BrepB", [128, NS, TS], BF16)
    CrepB = sb("CrepB", [128, NS, TS], BF16)
    Hbig = sb("Hbig", [128, NS, TS], BF16)
    dA_s = [sb(f"dA{i}", [128, TS], F16) for i in range(8)]
    du_s = [sb("du0", [128, TS], BF16)]
    y_s = du_s[0]
    uX = [sb(f"uX{i}", [128, 2, TS], BF16) for i in range(2)]
    hcar = sb("hcar", [128, 256], BF16)
    A_sc = sb("A_sc", [128, 256], F32)
    dsk_sb = sb("dsk_sb", [128, 16], F32)
    cb_sb = sb("cb_sb", [128, 16], F32)
    smT = sb("smT", [128, 4, T], F32)
    sm = [smT[:, i, :] for i in range(4)]
    tres = sm[2]
    selrep = sb("selrep", [128, 16, T], BF16)
    xb16 = selrep[:, 0:2 * NFC:2, :]
    x2b = selrep[:, 1:2 * NFC:2, :]

    scr_sb = [sb(f"scr{i}", [128, T], BF16) for i in range(2)]
    dtlr_sb = sb("dtlr_sb", [DTR + 1, T], BF16)
    bcs_sb = sb("bcs_sb", [80, T], BF16)
    Bsb = bcs_sb[0:16, :]
    Csb = bcs_sb[32:48, :]
    sel_sb = bcs_sb[64:80, :]
    ones_bf = sb("ones_bf", [128, 128], BF16)
    cpool = sb("cpool", [128, 8], F32)
    xproj_sb = sb("xproj_sb", [128, 16 * 128], BF16)
    wdt_sb = sb("wdt_sb", [DTR + 1, DI], BF16)
    selw_sb = sb("selw_sb", [128, NFC * 16], BF16)
    win_db = [sb(f"windb{i}", [128, 1024], BF16) for i in range(3)]
    key_db = [sb(f"keydb{i}", [128, 1024], BF16) for i in range(2)]
    val_db = [sb(f"valdb{i}", [128, 1024], BF16) for i in range(2)]
    stg32 = [ygB[:, 0:2, :].bitcast(F32).rearrange("p a b -> p (a b)"),
             ygB[:, 2:4, :].bitcast(F32).rearrange("p a b -> p (a b)")]
    stgb = [Hbig[:, 0, :], CrepB[:, 0, :]]

    pbank = [es.enter_context(nc.psum_tensor(f"pb{i}", [128, 512], F32)) for i in range(8)]
    pslot = [pbank[i][:, 0:T] for i in range(8)]

    CONSTS = [EPS, 1.0, -1.0, -0.5, 1.0 / D, 0.5, 0.0, 2.0]
    for ci, cv in enumerate(CONSTS):
        nc.const_aps.register(F32, cv, cpool[:, ci:ci + 1])

    sems = {e: es.enter_context(nc.semaphore(f"sem_{e}")) for e in ENGS}
    for s in DMA_SEMS:
        sems[s] = es.enter_context(nc.semaphore(f"sem_{s}"))

    S = Sched()
    E = S.emit

    d_x = [Dep() for _ in range(4)]
    d_xn2 = [Dep(), Dep()]
    d_xin_body = Dep(); d_xin_tail = Dep()
    d_u = [Dep() for _ in range(NDC)]
    d_z = [Dep() for _ in range(NDC)]
    d_delta = [Dep() for _ in range(NDC)]
    d_yg = [Dep() for _ in range(NDC)]
    d_Brep = Dep(); d_Crep = Dep(); d_H = Dep()
    d_dA = [Dep() for _ in range(8)]
    d_du = [Dep()]
    d_hcar = Dep()
    d_sm = [Dep() for _ in range(4)]
    d_tres = d_sm[2]
    d_selrep = [Dep() for _ in range(16)]; d_selsb = Dep()
    d_scr = [Dep() for _ in range(2)]
    d_dtlr = Dep(); d_Bsb = Dep(); d_Csb = Dep()
    d_slot = [Dep() for _ in range(8)]
    d_windb = [Dep() for _ in range(3)]
    d_keydb = [Dep() for _ in range(2)]
    d_valdb = [Dep() for _ in range(2)]
    d_xin8 = [Dep() for _ in range(8)]
    d_tails = Dep()
    d_stg32 = [[d_yg[0], d_yg[1]], [d_yg[2], d_yg[3]]]
    d_stgb = [[d_H], [d_Crep]]
    d_const = Dep()
    d_dsk = Dep(); d_cb = Dep(); d_bdt = Dep(); d_Asc = Dep()
    d_xpw = Dep(); d_wdtw = Dep(); d_selww = Dep()
    d_selscr = [Dep(), Dep()]; d_bscr = [Dep(), Dep()]; d_cscr = [Dep(), Dep()]

    for ci, cv in enumerate(CONSTS):
        E("dve", lambda e, ci=ci, cv=cv: e.memset(cpool[:, ci:ci + 1], cv), writes=[d_const])
    E("dve", lambda e: e.memset(hcar[:], 0.0), writes=[d_hcar])
    E("dve", lambda e: e.memset(tails[:], 0.0), writes=[d_tails])
    E("dve", lambda e: e.memset(ones_bf[:], 1.0), writes=[d_const])
    E("dve", lambda e: e.memset(dtlr_sb[64:65, :], 1.0), writes=[d_const])
    for _e in ("pe", "act", "pool", "sq"):
        E(_e, lambda e: e.nop(), reads=[d_const])

    E("sq", lambda e: e.dma_start(out=dsk_sb[:], in_=dsk_d), writes=[d_dsk], dma="dir")
    E("sq", lambda e: e.dma_start(out=cb_sb[:], in_=cb_d), writes=[d_cb], dma="dir")
    E("sq", lambda e: e.dma_start(out=sm[0][:], in_=alog_d), writes=[d_sm[0]], dma="dir")
    dir_total = ("dir", S.cnt["dir"])
    for d in (d_dsk, d_cb, d_sm[0]):
        d.w = dir_total
    E("act", lambda e: e.activation(out=sm[1][:], in_=sm[0][:], func=AF.Exp),
      reads=[d_sm[0]], writes=[d_sm[1]])
    E("pool", lambda e: e.tensor_scalar_mul(out=A_sc[:], in0=sm[1][:], scalar1=-1.0),
      reads=[d_sm[1], d_const], writes=[d_Asc])

    rr = [0]

    def conv_round(src_ap, dst_sb=None, dst_sb_dep=None, dst_dram=None, nparts=128):
        s = rr[0] % 2
        rr[0] += 1
        wdt = src_ap.shape[-1]
        E("sq", lambda e, a=src_ap, s=s, p=nparts, w=wdt: e.dma_start(
            out=stg32[s][0:p, :w], in_=a),
          writes=d_stg32[s], dma=f"stgi{s}")
        if dst_sb is not None:
            E("dve", lambda e, d=dst_sb, s=s, p=nparts, w=wdt: e.tensor_copy(
                out=d, in_=stg32[s][0:p, :w]),
              reads=d_stg32[s], writes=[dst_sb_dep])
        else:
            E("dve", lambda e, s=s, p=nparts, w=wdt: e.tensor_copy(
                out=stgb[s][0:p, :w], in_=stg32[s][0:p, :w]),
              reads=d_stg32[s], writes=d_stgb[s])
            E("sq", lambda e, a=dst_dram, s=s, p=nparts, w=wdt: e.dma_start(
                out=a, in_=stgb[s][0:p, :w]),
              reads=d_stgb[s], dma=f"scr{s}")

    for kc in range(NFC):
        for j in range(8):
            conv_round(w_in_d[kc * 128:(kc + 1) * 128, j * 512:(j + 1) * 512],
                       dst_dram=win_bf[kc * 128:(kc + 1) * 128, j * 512:(j + 1) * 512])
    for ei in range(16):
        for j in range(2):
            conv_round(keys_d[ei, :, j * 512:(j + 1) * 512],
                       dst_dram=keys_bf[ei, :, j * 512:(j + 1) * 512])
    for ei in range(16):
        for j in range(2):
            conv_round(vals_d[ei, :, j * 512:(j + 1) * 512],
                       dst_dram=vals_bf[ei, :, j * 512:(j + 1) * 512])
    for kc in range(NDC):
        for j in range(2):
            conv_round(wout_d[kc * 128:(kc + 1) * 128, j * 512:(j + 1) * 512],
                       dst_dram=wout_bf[kc * 128:(kc + 1) * 128, j * 512:(j + 1) * 512])
    for dc in range(NDC):
        conv_round(convd_d[dc], dst_dram=convd_bf[dc])
    for kc in range(NDC):
        conv_round(xproj_d[kc * 128:(kc + 1) * 128, :],
                   dst_sb=xproj_sb[:, kc * 128:(kc + 1) * 128], dst_sb_dep=d_xpw)
    for j in range(4):
        conv_round(wdt_d[:, j * 512:(j + 1) * 512],
                   dst_sb=wdt_sb[:, j * 512:(j + 1) * 512], dst_sb_dep=d_wdtw, nparts=DTR + 1)
    for kc in range(NFC):
        conv_round(selw_d[kc * 128:(kc + 1) * 128, :],
                   dst_sb=selw_sb[:, kc * 16:(kc + 1) * 16], dst_sb_dep=d_selww)

    SCR_TOTALS = [("scr0", S.cnt.get("scr0", 0)), ("scr1", S.cnt.get("scr1", 0))]

    x_src = x_in_d.rearrange("(fc p) L -> p fc L", p=128)
    out_dst = out_d.rearrange("(fc p) L -> p fc L", p=128)

    def dma_x_in(c):
        xb = xB[c % 4]
        E("sq", lambda e, c=c, xb=xb: e.dma_start(out=xb[:], in_=x_src[:, :, c * T:(c + 1) * T]),
          writes=[d_x[c % 4]], dma=f"xin{c % 4}")

    def dma_x_out(c):
        xb = xB[c % 4]
        E("sq", lambda e, c=c, xb=xb: e.dma_start(out=out_dst[:, :, c * T:(c + 1) * T], in_=xb[:]),
          reads=[d_x[c % 4]], dma=f"xout{c % 4}")

    def ln(c):
        xb = xB[c % 4]
        dx = d_x[c % 4]
        E("pool", lambda e, xb=xb: e.tensor_copy(out=xb16, in_=xb[:]),
          reads=[dx], writes=d_selrep[0::2])
        E("act", lambda e, xb=xb: e.activation(out=x2b, in_=xb[:], func=AF.Square),
          reads=[dx], writes=d_selrep[1::2])
        psq = pbank[4][:, 0:512]
        for fc in range(NFC):
            E("pe", lambda e, fc=fc, psq=psq: e.matmul(
                out=psq, lhsT=ones_bf[:],
                rhs=selrep[:, 2 * fc:2 * fc + 2, :].rearrange("p a b -> p (a b)"),
                start=(fc == 0), stop=(fc == NFC - 1)),
              reads=[d_selrep[2 * fc], d_selrep[2 * fc + 1], d_const], writes=[d_slot[4]])
        E("act", lambda e, psq=psq: e.activation(out=sm[0][:], in_=psq[:, 0:T], func=AF.Copy, scale=1.0 / D),
          reads=[d_slot[4]], writes=[d_sm[0]])
        E("act", lambda e, psq=psq: e.activation(out=sm[1][:], in_=psq[:, T:2 * T], func=AF.Copy, scale=1.0 / D),
          reads=[d_slot[4]], writes=[d_sm[1]])
        E("pool", lambda e: e.tensor_tensor(out=sm[2][:], in0=sm[0][:], in1=sm[0][:], op=OP.mult),
          reads=[d_sm[0]], writes=[d_sm[2]])
        E("pool", lambda e: e.tensor_tensor(out=sm[1][:], in0=sm[1][:], in1=sm[2][:], op=OP.subtract),
          reads=[d_sm[2]], writes=[d_sm[1]])
        E("act", lambda e: e.activation(out=sm[1][:], in_=sm[1][:], func=AF.Ln, bias=EPS),
          writes=[d_sm[1]])
        E("act", lambda e: e.activation(out=sm[1][:], in_=sm[1][:], func=AF.Exp, scale=-0.5),
          writes=[d_sm[1]])
        E("pool", lambda e, xb=xb: e.tensor_tensor(
            out=xn2[c % 2][:], in0=xb[:],
            in1=sm[0][:].unsqueeze(1).broadcast_to([128, NFC, T]), op=OP.subtract),
          reads=[dx, d_sm[0]], writes=[d_xn2[c % 2]])
        E("pool", lambda e: e.tensor_tensor(
            out=xn2[c % 2][:], in0=xn2[c % 2][:],
            in1=sm[1][:].unsqueeze(1).broadcast_to([128, NFC, T]), op=OP.mult),
          reads=[d_sm[1]], writes=[d_xn2[c % 2]])

    def ffn_pieces(c, scale):
        xnc = xn2[c % 2]
        d_xnc = d_xn2[c % 2]
        xb = xB[c % 4]
        dx = d_x[c % 4]

        def p_sel():
            S.tag = "ffn_sel"
            psel = pslot[5]
            for kc in range(NFC):
                E("pe", lambda e, kc=kc, psel=psel: e.matmul(
                    out=psel[0:16, :], lhsT=selw_sb[:, kc * 16:(kc + 1) * 16], rhs=xnc[:, kc, :],
                    start=(kc == 0), stop=(kc == NFC - 1)),
                  reads=[d_xnc, d_selww], writes=[d_slot[5]])
            E("act", lambda e, psel=psel: e.activation(out=bcs_sb[64:80, :], in_=psel[0:16, :], func=AF.Sigmoid),
              reads=[d_slot[5]], writes=[d_selsb])
            q = c % 2
            E("pool", lambda e, q=q: e.dma_start(out=selscr[q], in_=sel_sb[:]),
              reads=[d_selsb], writes=[d_selscr[q]], dma=f"rso{q}")
            E("pool", lambda e, q=q: e.dma_start(
                out=selrep[:], in_=selscr[q].unsqueeze(0).broadcast_to([128, 16, T])),
              reads=[d_selscr[q]], writes=d_selrep, dma=f"rsi{q}")

        def mk_p1(e0):
            def go():
                S.tag = "ffn_p1"
                for ei in (e0, e0 + 1):
                    pk = ei % 2
                    p = ei % 2
                    sl = 6 + (ei % 2)
                    E("sq", lambda e, ei=ei, pk=pk: e.dma_start(out=key_db[pk][:], in_=keys_bf[ei]),
                      writes=[d_keydb[pk]], dma=f"key{pk}", extra_waits=SCR_TOTALS)
                    for kc in range(NFC):
                        E("pe", lambda e, kc=kc, sl=sl, pk=pk: e.matmul(
                            out=pslot[sl], lhsT=key_db[pk][:, kc * 128:(kc + 1) * 128], rhs=xnc[:, kc, :],
                            start=(kc == 0), stop=(kc == NFC - 1)),
                          reads=[d_keydb[pk], d_xnc], writes=[d_slot[sl]])
                    E("act", lambda e, sl=sl, p=p: e.activation(out=scr_sb[p][:], in_=pslot[sl], func=AF.Relu),
                      reads=[d_slot[sl]], writes=[d_scr[p]])
                    E("pool", lambda e, ei=ei, p=p: e.tensor_tensor(out=selrep[:, ei, :], in0=scr_sb[p][:],
                                                                    in1=selrep[:, ei, :], op=OP.mult),
                      reads=[d_scr[p]], writes=[d_selrep[ei]])
            return go

        def mk_p2(e0):
            def go():
                S.tag = "ffn_p2"
                for ei in (e0, e0 + 1):
                    p = ei % 2
                    E("sq", lambda e, ei=ei, p=p: e.dma_start(out=val_db[p][:], in_=vals_bf[ei]),
                      writes=[d_valdb[p]], dma=f"val{p}", extra_waits=SCR_TOTALS)
                    for vc in range(8):
                        E("pe", lambda e, ei=ei, vc=vc, p=p: e.matmul(
                            out=pslot[vc], lhsT=val_db[p][:, vc * 128:(vc + 1) * 128],
                            rhs=selrep[:, ei, :],
                            start=(ei == 0), stop=(ei == 15)),
                          reads=[d_valdb[p], d_selrep[ei]], writes=[d_slot[vc]])
            return go

        def p_evac():
            S.tag = "ffn_evac"
            for vc in range(8):
                tr = sm[2 + vc % 2]
                dtr = d_sm[2 + vc % 2]
                E("act", lambda e, vc=vc, scale=scale, tr=tr: e.activation(
                    out=tr[:], in_=pslot[vc], func=AF.Copy, scale=scale),
                  reads=[d_slot[vc]], writes=[dtr])
                E("pool", lambda e, vc=vc, xb=xb, tr=tr: e.tensor_tensor(
                    out=xb[:, vc, :], in0=xb[:, vc, :], in1=tr[:], op=OP.add),
                  reads=[dtr, dx], writes=[dx])

        return ([(0, p_sel)] + [(0, mk_p1(e0)) for e0 in range(0, 16, 2)]
                + [(0, mk_p2(e0)) for e0 in range(0, 16, 2)] + [(0, p_evac)])

    def win_pieces(c, half):
        cip = c % 2
        xnc = xn2[c % 2]
        d_xnc = d_xn2[c % 2]

        def mk(h, q, inline_ev=False):
            def mms():
                for kc in range(NFC):
                    p = (q * NFC + kc) % 3
                    cols = h * 2048 + q * 1024
                    E("sq", lambda e, kc=kc, p=p, cols=cols: e.dma_start(
                        out=win_db[p][:], in_=win_bf[kc * 128:(kc + 1) * 128, cols:cols + 1024]),
                      writes=[d_windb[p]], dma=f"win{p}", extra_waits=SCR_TOTALS)
                    for m in range(8):
                        E("pe", lambda e, kc=kc, m=m, p=p: e.matmul(
                            out=pslot[m], lhsT=win_db[p][:, m * 128:(m + 1) * 128], rhs=xnc[:, kc, :],
                            start=(kc == 0), stop=(kc == NFC - 1)),
                          reads=[d_windb[p], d_xnc], writes=[d_slot[m]])
                if inline_ev:
                    evs()

            def evs():
                for m in range(8):
                    mc = q * 8 + m
                    if h == 0:
                        E("act", lambda e, m=m, mc=mc: e.activation(
                            out=x_inb[:, mc % 8, 3:3 + T], in_=pslot[m], func=AF.Copy),
                          reads=[d_slot[m]], writes=[d_xin8[mc % 8]])
                    else:
                        E("act", lambda e, m=m, mc=mc, cip=cip: e.activation(
                            out=zB[:, mc, cip * T:(cip + 1) * T], in_=pslot[m], func=AF.Silu),
                          reads=[d_slot[m]], writes=[d_z[mc]])
            return mms, evs
        return mk

    def conv_pieces(c):
        cip = c % 2
        par = (c // 2) % 2

        def mk(d0):
            def go():
                s0 = d0 % 8
                xds = [d_xin8[(d0 + i) % 8] for i in range(4)]
                E("act", lambda e, d0=d0, s0=s0: e.activation(
                    out=x_inb[:, s0:s0 + 4, 0:3], in_=tails[:, d0:d0 + 4, :], func=AF.Copy),
                  reads=[d_tails], writes=xds)
                for dc in range(d0, d0 + 4):
                    s8 = dc % 8
                    p = dc % 2
                    E("sq", lambda e, dc=dc, p=p: e.dma_start(out=key_db[p][:, 0:512], in_=convd_bf[dc]),
                      writes=[d_keydb[p]], dma=f"key{p}", extra_waits=SCR_TOTALS)
                    sl = 6 + (dc % 2)
                    for k in range(4):
                        E("pe", lambda e, dc=dc, k=k, p=p, sl=sl, s8=s8: e.matmul(
                            out=pslot[sl], lhsT=key_db[p][:, k * 128:(k + 1) * 128],
                            rhs=x_inb[:, s8, k:k + T],
                            start=(k == 0), stop=(k == 3)),
                          reads=[d_keydb[p], d_xin8[s8]], writes=[d_slot[sl]])
                    udst = (uB[:, dc, cip * T:(cip + 1) * T] if dc < 14
                            else uX[par][:, dc - 14, cip * T:(cip + 1) * T])
                    E("act", lambda e, dc=dc, sl=sl, udst=udst: e.activation(
                        out=udst, in_=pslot[sl], func=AF.Silu,
                        bias=cb_sb[:, dc:dc + 1]),
                      reads=[d_slot[sl], d_cb], writes=[d_u[dc]])
                E("act", lambda e, d0=d0, s0=s0: e.activation(
                    out=tails[:, d0:d0 + 4, :], in_=x_inb[:, s0:s0 + 4, T:T + 3], func=AF.Copy),
                  reads=xds, writes=[d_tails])
            return go
        return [(d0 + 3, mk(d0)) for d0 in (0, 4, 8)] + [(13, mk(12))]

    def xproj_pieces(c):
        cip = c % 2

        def p_xproj():
            S.tag = "xproj"
            px = pslot[5]
            par = (c // 2) % 2
            for kc in range(NDC):
                usrc = (uB[:, kc, cip * T:(cip + 1) * T] if kc < 14
                        else uX[par][:, kc - 14, cip * T:(cip + 1) * T])
                E("pe", lambda e, kc=kc, px=px, usrc=usrc: e.matmul(
                    out=px, lhsT=xproj_sb[:, kc * 128:(kc + 1) * 128],
                    rhs=usrc,
                    start=(kc == 0), stop=(kc == NDC - 1)),
                  reads=[d_u[kc], d_xpw], writes=[d_slot[5]])
            E("act", lambda e, px=px: e.activation(out=dtlr_sb[0:64, :], in_=px[64:128, :], func=AF.Copy),
              reads=[d_slot[5]], writes=[d_dtlr])
            E("act", lambda e, px=px: e.activation(out=Bsb[:], in_=px[0:16, :], func=AF.Copy),
              reads=[d_slot[5]], writes=[d_Bsb])
            E("act", lambda e, px=px: e.activation(out=bcs_sb[32:48, :], in_=px[32:48, :], func=AF.Copy),
              reads=[d_slot[5]], writes=[d_Csb])

        def p_bounce_out():
            q = c % 2
            E("pool", lambda e, q=q: e.dma_start(out=bscr[q], in_=Bsb[:]),
              reads=[d_Bsb], writes=[d_bscr[q]], dma=f"rbo{q}")
            E("pool", lambda e, q=q: e.dma_start(out=cscr[q], in_=Csb[:]),
              reads=[d_Csb], writes=[d_cscr[q]], dma=f"rco{q}")

        def p_bcastB():
            q = c % 2
            E("pool", lambda e, q=q, cip=cip: e.dma_start(
                out=BrepB[:, :, cip * T:(cip + 1) * T],
                in_=bscr[q].unsqueeze(0).broadcast_to([128, 16, T])),
              reads=[d_bscr[q]], writes=[d_Brep], dma=f"rbi{q}")

        def p_bcastC():
            q = c % 2
            E("pool", lambda e, q=q, cip=cip: e.dma_start(
                out=CrepB[:, :, cip * T:(cip + 1) * T],
                in_=cscr[q].unsqueeze(0).broadcast_to([128, 16, T])),
              reads=[d_cscr[q]], writes=[d_Crep], dma=f"rci{q}")

        pend = []

        def emit_dt_stt():
            mc, bk, lvd = pend.pop(0)
            E("dve", lambda e, mc=mc, cip=cip, bk=bk, lvd=lvd: e.scalar_tensor_tensor(
                out=deltaB[:, mc:mc + 2, cip * T:(cip + 1) * T],
                in0=pbank[bk][:, 0:2 * T], scalar=0.0, in1=smT[:, lvd:lvd + 2, :],
                op0=OP.max, op1=OP.add),
              reads=[d_slot[bk], d_sm[lvd], d_sm[lvd + 1], d_const],
              writes=[d_delta[mc], d_delta[mc + 1]])

        def mk_dt(m0):
            def go():
                S.tag = "dt"
                for mc in range(m0, m0 + 4, 2):
                    if len(pend) >= 2:
                        emit_dt_stt()
                    pr = (mc // 2) % 2
                    bk = 6 + pr
                    lvd = 2 * pr
                    full = pbank[bk][:, 0:2 * T]
                    lv = smT[:, lvd:lvd + 2, :].rearrange("p a b -> p (a b)")
                    E("pe", lambda e, mc=mc, bk=bk: e.matmul(
                        out=pbank[bk][:, 0:T], lhsT=wdt_sb[:, mc * 128:(mc + 1) * 128], rhs=dtlr_sb[:],
                        start=True, stop=True),
                      reads=[d_dtlr, d_wdtw], writes=[d_slot[bk]])
                    E("pe", lambda e, mc=mc, bk=bk: e.matmul(
                        out=pbank[bk][:, T:2 * T], lhsT=wdt_sb[:, (mc + 1) * 128:(mc + 2) * 128], rhs=dtlr_sb[:],
                        start=True, stop=True),
                      reads=[d_dtlr, d_wdtw], writes=[d_slot[bk]])
                    E("act", lambda e, bk=bk, lv=lv, full=full: e.activation(out=lv, in_=full, func=AF.Abs),
                      reads=[d_slot[bk]], writes=[d_sm[lvd], d_sm[lvd + 1]])
                    E("act", lambda e, lv=lv: e.activation(out=lv, in_=lv, func=AF.Exp, scale=-1.0),
                      writes=[d_sm[lvd], d_sm[lvd + 1]])
                    E("act", lambda e, lv=lv: e.activation(out=lv, in_=lv, func=AF.Ln, bias=1.0),
                      writes=[d_sm[lvd], d_sm[lvd + 1]])
                    pend.append((mc, bk, lvd))
                if m0 + 4 >= NDC - 4:
                    while pend:
                        emit_dt_stt()
            return go
        return ([(13, p_xproj), (13, p_bounce_out)]
                + [(13, mk_dt(m0)) for m0 in (0, 4, 8)] + [(15, mk_dt(12))]
                + [(15, p_bcastB), (15, p_bcastC)])

    def scan_dc_q(P, dc, qtr):
        S.tag = "scan"
        usrc = uB[:, dc, :] if dc < 14 else uX[P % 2][:, dc - 14, :]
        if qtr == 0:
            E("dve", lambda e, dc=dc, usrc=usrc: e.tensor_tensor(
                out=du_s[0][:], in0=deltaB[:, dc, :], in1=usrc, op=OP.mult),
              reads=[d_delta[dc], d_u[dc]], writes=[d_du[0]])
            E("dve", lambda e, dc=dc: e.tensor_tensor(
                out=Hbig[:], in0=du_s[0][:].unsqueeze(1).broadcast_to([128, NS, TS]),
                in1=BrepB[:], op=OP.mult),
              reads=[d_du[0], d_Brep], writes=[d_H])
        for n in range(4 * qtr, 4 * qtr + 4):
            ap_ = dA_s[n % 8]
            E("dve", lambda e, dc=dc, n=n, ap_=ap_: e.tensor_tensor_scan(
                out=Hbig[:, n, :], data0=ap_[:], data1=Hbig[:, n, :],
                initial=hcar[:, dc * NS + n:dc * NS + n + 1], op0=OP.mult, op1=OP.add),
              reads=[d_dA[n % 8], d_hcar], writes=[d_H])
        if qtr == 3:
            E("dve", lambda e, dc=dc: e.tensor_copy(
                out=hcar[:, dc * NS:(dc + 1) * NS], in_=Hbig[:, :, TS - 1:TS].squeeze(2)),
              reads=[d_H], writes=[d_hcar])
            E("dve", lambda e: e.tensor_tensor(out=Hbig[:], in0=Hbig[:], in1=CrepB[:], op=OP.mult),
              reads=[d_H, d_Crep], writes=[d_H])
            E("dve", lambda e: e.tensor_tensor(out=Hbig[:, 0:8, :], in0=Hbig[:, 0:8, :],
                                               in1=Hbig[:, 8:16, :], op=OP.add),
              reads=[d_H], writes=[d_H])
            E("dve", lambda e: e.tensor_tensor(out=Hbig[:, 0:4, :], in0=Hbig[:, 0:4, :],
                                               in1=Hbig[:, 4:8, :], op=OP.add),
              reads=[d_H], writes=[d_H])
            E("dve", lambda e: e.tensor_tensor(out=Hbig[:, 0:2, :], in0=Hbig[:, 0:2, :],
                                               in1=Hbig[:, 2:4, :], op=OP.add),
              reads=[d_H], writes=[d_H])
            E("dve", lambda e: e.tensor_tensor(out=y_s[:], in0=Hbig[:, 0, :],
                                               in1=Hbig[:, 1, :], op=OP.add),
              reads=[d_H], writes=[d_du[0]])
            E("dve", lambda e, dc=dc, usrc=usrc: e.scalar_tensor_tensor(
                out=y_s[:], in0=usrc, scalar=dsk_sb[:, dc:dc + 1], in1=y_s[:],
                op0=OP.mult, op1=OP.add),
              reads=[d_u[dc], d_dsk, d_du[0]], writes=[d_du[0]])
            E("dve", lambda e, dc=dc: e.tensor_tensor(
                out=ygB[:, dc, :], in0=y_s[:], in1=zB[:, dc, :], op=OP.mult),
              reads=[d_du[0], d_z[dc]], writes=[d_yg[dc]])
        for j in range(4):
            nn = 4 * qtr + 8 + j
            gg = P * NDC + dc + nn // NS
            n2 = nn % NS
            if gg >= NPAIR * NDC:
                continue
            d2 = gg % NDC
            ap_ = dA_s[n2 % 8]
            E("act", lambda e, d2=d2, n2=n2, ap_=ap_: e.activation(
                out=ap_[:], in_=deltaB[:, d2, :], func=AF.Exp,
                scale=A_sc[:, d2 * NS + n2:d2 * NS + n2 + 1]),
              reads=[d_delta[d2], d_Asc], writes=[d_dA[n2 % 8]])

    def wout_pieces(P):
        mm = []

        def mk_dc(dc):
            def go():
                S.tag = "wout"
                p = dc % 3
                E("sq", lambda e, dc=dc, p=p: e.dma_start(
                    out=win_db[p][:], in_=wout_bf[dc * 128:(dc + 1) * 128, :]),
                  writes=[d_windb[p]], dma=f"win{p}", extra_waits=SCR_TOTALS)
                for vc in range(8):
                    E("pe", lambda e, dc=dc, vc=vc, p=p: e.matmul(
                        out=pbank[vc][:, 0:512], lhsT=win_db[p][:, vc * 128:(vc + 1) * 128],
                        rhs=ygB[:, dc, :],
                        start=(dc == 0), stop=(dc == NDC - 1)),
                      reads=[d_windb[p], d_yg[dc]], writes=[d_slot[vc]])
            return go

        def mk_evac(h):
            def go():
                S.tag = "wout"
                c = 2 * P + h
                xb = xB[c % 4]
                dx = d_x[c % 4]
                for vc in range(8):
                    tr = sm[2 + vc % 2]
                    dtr = d_sm[2 + vc % 2]
                    E("act", lambda e, vc=vc, h=h, tr=tr: e.activation(
                        out=tr[:], in_=pbank[vc][:, h * T:(h + 1) * T], func=AF.Copy),
                      reads=[d_slot[vc]], writes=[dtr])
                    E("pool", lambda e, vc=vc, xb=xb, dx=dx, tr=tr: e.tensor_tensor(
                        out=xb[:, vc, :], in0=xb[:, vc, :], in1=tr[:], op=OP.add),
                      reads=[dtr, dx], writes=[dx])
            return go

        for dc in range(NDC):
            mm.append((0, mk_dc(dc)))
        ev = [(0, mk_evac(0)), (0, mk_evac(1))]
        return mm, ev

    def front_pieces(c):
        def p_dma():
            dma_x_in(c)

        def p_ln():
            S.tag = "ln"
            ln(c)

        def p_ln2():
            ln(c)
        wmk = win_pieces(c, 0)
        cps = conv_pieces(c)
        m00, e00 = wmk(0, 0)
        m01, e01 = wmk(0, 1)
        m10, _ = wmk(1, 0, inline_ev=True)
        m11, _ = wmk(1, 1, inline_ev=True)
        kx = (lambda k: k) if c % 2 == 0 else (lambda k: 13)
        return ([(0, p_dma), (0, p_ln)] + ffn_pieces(c, 0.5) + [(0, p_ln2)]
                + [(kx(1), m00), (kx(2), e00), (kx(cps[0][0]), cps[0][1]), (kx(cps[1][0]), cps[1][1]),
                   (kx(7), m01), (kx(8), e01), (kx(cps[2][0]), cps[2][1]), (kx(cps[3][0]), cps[3][1]),
                   (14, m10), (15, m11)]
                + xproj_pieces(c))

    def tail_pieces(c):
        def p_ln():
            S.tag = "ln"
            ln(c)

        def p_out():
            dma_x_out(c)
        return [(0, p_ln)] + ffn_pieces(c, 0.5) + [(0, p_out)]

    for c in (0, 1):
        for _mdc, pc in front_pieces(c):
            pc()
    S.tag = "scan"
    for n in range(8):
        E("act", lambda e, n=n, ap_=dA_s[n % 8]: e.activation(
            out=ap_[:], in_=deltaB[:, 0, :], func=AF.Exp,
            scale=A_sc[:, n:n + 1]),
          reads=[d_delta[0], d_Asc], writes=[d_dA[n % 8]])
    for P in range(NPAIR):
        pieces = []
        wparts = {}
        if P >= 1:
            mm, ev = wout_pieces(P - 1)
            wparts[(0, 0)] = mm
            wparts[(0, 2)] = [ev[0]]
            wparts[(0, 3)] = [ev[1]]
            pieces += tail_pieces(2 * P - 2) + tail_pieces(2 * P - 1)
        if P + 1 <= NPAIR - 1:
            pieces += front_pieces(2 * P + 2) + front_pieces(2 * P + 3)
        pieces.sort(key=lambda t: t[0])
        n = len(pieces)
        k = (n + 4 * NDC - 1) // (4 * NDC)
        j = 0
        for dc in range(NDC):
            for qtr in range(4):
                scan_dc_q(P, dc, qtr)
                for _m, pc in wparts.pop((dc, qtr), ()):
                    pc()
                budget = k
                while j < n and (pieces[j][0] < dc or (pieces[j][0] == dc and qtr == 3)) and (budget > 0 or (dc == NDC - 1 and qtr == 3)):
                    pieces[j][1]()
                    j += 1
                    budget -= 1
        while j < n:
            pieces[j][1]()
            j += 1
    mm, ev = wout_pieces(NPAIR - 1)
    for _mdc, pc in mm + ev:
        pc()
    for c in (2 * NPAIR - 2, 2 * NPAIR - 1):
        for _mdc, pc in tail_pieces(c):
            pc()

    final_waits = [(s, S.cnt.get(s, 0)) for s in ("xout0", "xout1", "xout2", "xout3")
                   if S.cnt.get(s, 0) > 0]

    with nc.Block() as block:
        def make(eng_name):
            def body(eng):
                for fn, waits, sem_name, _tag in S.ops[eng_name]:
                    for (e2, c2) in waits:
                        eng.wait_ge(sems[e2], c2)
                    inst = fn(eng)
                    inst.then_inc(sems[sem_name], 16 if sem_name in DMA_SEMS else 1)
                if eng_name == "sq":
                    for (s, c2) in final_waits:
                        eng.wait_ge(sems[s], c2)
            return body
        block.tensor(make("pe"))
        block.scalar(make("act"))
        block.vector(make("dve"))
        block.gpsimd(make("pool"))
        block.sync(make("sq"))

    es.close()
    return nc


def _host_inputs(inputs, L=4096):
    W_in = np.ascontiguousarray(inputs["L0_attn_W_in"], dtype=np.float32)
    conv_w = np.asarray(inputs["L0_attn_conv_w"], dtype=np.float32)[:, 0, :]
    conv_b = np.asarray(inputs["L0_attn_conv_b"], dtype=np.float32)
    W_xp = np.asarray(inputs["L0_attn_W_xproj"], dtype=np.float32)
    W_dt0 = np.asarray(inputs["L0_attn_W_dt"], dtype=np.float32)
    b_dt0 = np.asarray(inputs["L0_attn_b_dt"], dtype=np.float32)
    W_dt = np.ascontiguousarray(np.concatenate([W_dt0, b_dt0[None, :]], axis=0))
    b_dt = np.asarray(inputs["L0_attn_b_dt"], dtype=np.float32)
    A_log = np.asarray(inputs["L0_attn_A_log"], dtype=np.float32)
    D_skip = np.asarray(inputs["L0_attn_D_skip"], dtype=np.float32)
    W_out = np.ascontiguousarray(inputs["L0_attn_W_out"], dtype=np.float32)
    selw = np.asarray(inputs["L0_ffn_expert_sel"], dtype=np.float32)
    keys = np.asarray(inputs["L0_ffn_keys"], dtype=np.float32)
    values = np.asarray(inputs["L0_ffn_values"], dtype=np.float32)

    keys_r = np.ascontiguousarray(
        keys.reshape(16, NFC, 128, 128).transpose(0, 2, 1, 3).reshape(16, 128, 1024))
    vals_r = np.ascontiguousarray(values)
    xproj_r = np.zeros((DI, 128), np.float32)
    xproj_r[:, 0:16] = W_xp[:, 64:80]
    xproj_r[:, 32:48] = W_xp[:, 80:96]
    xproj_r[:, 64:128] = W_xp[:, 0:64]
    selw_r = np.ascontiguousarray(selw.T)

    def di_layout(v):
        return np.ascontiguousarray(v.reshape(NDC, 128).T)
    alog_r = np.ascontiguousarray(
        A_log.reshape(NDC, 128, NS).transpose(1, 0, 2).reshape(128, NDC * NS))
    convd_r = np.zeros((NDC, 128, 4, 128), np.float32)
    idx = np.arange(128)
    for dc in range(NDC):
        for k in range(4):
            convd_r[dc, idx, k, idx] = conv_w[dc * 128 + idx, k]
    convd_r = np.ascontiguousarray(convd_r.reshape(NDC, 128, 512))
    shared = dict(
        w_in=W_in, keys_r=keys_r, vals_r=vals_r, wout_r=W_out,
        convd_r=convd_r, xproj_r=xproj_r, wdt_r=W_dt, selw_r=selw_r, alog_r=alog_r,
        dsk_r=di_layout(D_skip), cb_r=di_layout(conv_b), bdt_r=di_layout(b_dt),
    )
    x = np.asarray(inputs["L0_attn_x"], dtype=np.float32)
    xs = [np.ascontiguousarray(x[b, :L, :].T) for b in range(x.shape[0])]
    return shared, xs


def kernel(**inputs):
    L = 4096
    shared, xs = _host_inputs(inputs, L)
    nc = build_program(L)
    in_maps = [dict(shared, x_f=xs[b]) for b in range(8)]
    res = run_bass_kernel_spmd(nc, in_maps, core_ids=list(range(8)))
    out = np.empty((8, L, D), np.float32)
    for b in range(8):
        out[b] = np.asarray(res.results[b]["out_f"], dtype=np.float32).T
    return out


# version 55
# speedup vs baseline: 1.0102x; 1.0102x over previous
import contextlib
import numpy as np
import concourse.bass as bass
import concourse.mybir as mybir
from concourse.bass_utils import run_bass_kernel_spmd

F32 = mybir.dt.float32
F16 = mybir.dt.float16
BF16 = mybir.dt.bfloat16
AF = mybir.ActivationFunctionType
OP = mybir.AluOpType

D = 1024
DI = 2048
NS = 16
DTR = 64
T = 256
TS = 512
NFC = D // 128
NDC = DI // 128
EPS = 1e-5

ENGS = ("pe", "act", "dve", "pool", "sq")
DMA_SEMS = ("xin0", "xin1", "xin2", "xin3", "xout0", "xout1", "xout2", "xout3",
            "win0", "win1", "win2", "key0", "key1", "key2", "val0", "val1", "val2",
            "stgi0", "stgi1", "scr0", "scr1", "dir",
            "rso0", "rso1", "rsi0", "rsi1", "rbo0", "rbo1", "rbi0", "rbi1",
            "rco0", "rco1", "rci0", "rci1")


class Dep:
    __slots__ = ("w", "r")

    def __init__(self):
        self.w = None
        self.r = []


class Sched:

    def __init__(self):
        self.ops = {e: [] for e in ENGS}
        self.cnt = {}
        self.waited = {e: {} for e in ENGS}
        self.tag = "init"

    def emit(self, eng, fn, reads=(), writes=(), dma=None, extra_waits=()):
        need = {}

        def add(e, c):
            if c > need.get(e, 0):
                need[e] = c

        for d in reads:
            if d.w:
                add(*d.w)
        for d in writes:
            if d.w:
                add(*d.w)
            for (e, c) in d.r:
                add(e, c)
        for (e, c) in extra_waits:
            add(e, c)
        waits = []
        w = self.waited[eng]
        for e, c in need.items():
            if e == eng:
                c = min(c, self.cnt.get(eng, 0))
            if c > w.get(e, 0):
                waits.append((e, c))
                w[e] = c
        if dma is not None:
            self.cnt[dma] = self.cnt.get(dma, 0) + 16
            me = (dma, self.cnt[dma])
            self.ops[eng].append((fn, waits, dma, self.tag))
        else:
            self.cnt[eng] = self.cnt.get(eng, 0) + 1
            me = (eng, self.cnt[eng])
            self.ops[eng].append((fn, waits, eng, self.tag))
        for d in writes:
            d.w = me
            d.r = []
        for d in reads:
            d.r.append(me)


def build_program(L=4096):
    NPAIR = L // TS
    nc = bass.Bass("TRN2", target_bir_lowering=False, debug=False)

    x_in_d = nc.dram_tensor("x_f", [D, L], F32, kind="ExternalInput").ap()
    out_d = nc.dram_tensor("out_f", [D, L], F32, kind="ExternalOutput").ap()
    w_in_d = nc.dram_tensor("w_in", [D, 2 * DI], F32, kind="ExternalInput").ap()
    keys_d = nc.dram_tensor("keys_r", [16, 128, 1024], F32, kind="ExternalInput").ap()
    vals_d = nc.dram_tensor("vals_r", [16, 128, 1024], F32, kind="ExternalInput").ap()
    wout_d = nc.dram_tensor("wout_r", [DI, D], F32, kind="ExternalInput").ap()
    convd_d = nc.dram_tensor("convd_r", [NDC, 128, 512], F32, kind="ExternalInput").ap()
    xproj_d = nc.dram_tensor("xproj_r", [DI, 128], F32, kind="ExternalInput").ap()
    wdt_d = nc.dram_tensor("wdt_r", [DTR + 1, DI], F32, kind="ExternalInput").ap()
    selw_d = nc.dram_tensor("selw_r", [D, 16], F32, kind="ExternalInput").ap()
    alog_d = nc.dram_tensor("alog_r", [128, 256], F32, kind="ExternalInput").ap()
    dsk_d = nc.dram_tensor("dsk_r", [128, 16], F32, kind="ExternalInput").ap()
    cb_d = nc.dram_tensor("cb_r", [128, 16], F32, kind="ExternalInput").ap()
    bdt_d = nc.dram_tensor("bdt_r", [128, 16], F32, kind="ExternalInput").ap()

    win_bf = nc.dram_tensor("win_bf", [D, 2 * DI], BF16, kind="Internal").ap()
    keys_bf = nc.dram_tensor("keys_bf", [16, 128, 1024], BF16, kind="Internal").ap()
    vals_bf = nc.dram_tensor("vals_bf", [16, 128, 1024], BF16, kind="Internal").ap()
    wout_bf = nc.dram_tensor("wout_bf", [DI, D], BF16, kind="Internal").ap()
    convd_bf = nc.dram_tensor("convd_bf", [NDC, 128, 512], BF16, kind="Internal").ap()
    selscr = nc.dram_tensor("selscr", [2, 16, T], BF16, kind="Internal").ap()
    bscr = nc.dram_tensor("bscr", [2, 16, T], BF16, kind="Internal").ap()
    cscr = nc.dram_tensor("cscr", [2, 16, T], BF16, kind="Internal").ap()

    es = contextlib.ExitStack()

    def sb(name, shape, dt):
        return es.enter_context(nc.sbuf_tensor(name, shape, dt))

    xB = [sb(f"x{i}", [128, NFC, T], F32) for i in range(4)]
    xn2 = [sb(f"xn{i}", [128, NFC, T], BF16) for i in range(2)]
    x_inb = sb("x_inb", [128, 8, T + 3], BF16)
    tails = sb("tails", [128, NDC, 3], BF16)
    uB = sb("uB", [128, NDC, TS], BF16)
    zB = sb("zB", [128, NDC, TS], BF16)
    deltaB = sb("deltaB", [128, NDC, TS], BF16)
    ygB = sb("ygB", [128, NDC, TS], BF16)
    BrepB = sb("BrepB", [128, NS, TS], BF16)
    CrepB = sb("CrepB", [128, NS, TS], BF16)
    Hbig = sb("Hbig", [128, NS, TS], BF16)
    dA_s = [sb(f"dA{i}", [128, TS], F16) for i in range(8)]
    du_s = [sb("du0", [128, TS], BF16)]
    y_s = du_s[0]
    uX = [sb(f"uX{i}", [128, 2, TS], BF16) for i in range(2)]
    hcar = sb("hcar", [128, 256], BF16)
    A_sc = sb("A_sc", [128, 256], F32)
    dsk_sb = sb("dsk_sb", [128, 16], F32)
    cb_sb = sb("cb_sb", [128, 16], F32)
    smT = sb("smT", [128, 4, T], F32)
    sm = [smT[:, i, :] for i in range(4)]
    tres = sm[2]
    selrep = sb("selrep", [128, 16, T], BF16)
    xb16 = selrep[:, 0:2 * NFC:2, :]
    x2b = selrep[:, 1:2 * NFC:2, :]

    scr_sb = [sb(f"scr{i}", [128, T], BF16) for i in range(2)]
    dtlr_sb = sb("dtlr_sb", [DTR + 1, T], BF16)
    bcs_sb = sb("bcs_sb", [80, T], BF16)
    Bsb = bcs_sb[0:16, :]
    Csb = bcs_sb[32:48, :]
    sel_sb = bcs_sb[64:80, :]
    ones_bf = sb("ones_bf", [128, 128], BF16)
    cpool = sb("cpool", [128, 8], F32)
    xproj_sb = sb("xproj_sb", [128, 16 * 128], BF16)
    wdt_sb = sb("wdt_sb", [DTR + 1, DI], BF16)
    selw_sb = sb("selw_sb", [128, NFC * 16], BF16)
    win_db = [sb(f"windb{i}", [128, 1024], BF16) for i in range(3)]
    key_db = [sb(f"keydb{i}", [128, 1024], BF16) for i in range(2)]
    val_db = [sb(f"valdb{i}", [128, 1024], BF16) for i in range(2)]
    stg32 = [ygB[:, 0:2, :].bitcast(F32).rearrange("p a b -> p (a b)"),
             ygB[:, 2:4, :].bitcast(F32).rearrange("p a b -> p (a b)")]
    stgb = [Hbig[:, 0, :], CrepB[:, 0, :]]

    pbank = [es.enter_context(nc.psum_tensor(f"pb{i}", [128, 512], F32)) for i in range(8)]
    pslot = [pbank[i][:, 0:T] for i in range(8)]

    CONSTS = [EPS, 1.0, -1.0, -0.5, 1.0 / D, 0.5, 0.0, 2.0]
    for ci, cv in enumerate(CONSTS):
        nc.const_aps.register(F32, cv, cpool[:, ci:ci + 1])

    sems = {e: es.enter_context(nc.semaphore(f"sem_{e}")) for e in ENGS}
    for s in DMA_SEMS:
        sems[s] = es.enter_context(nc.semaphore(f"sem_{s}"))

    S = Sched()
    E = S.emit

    d_x = [Dep() for _ in range(4)]
    d_xn2 = [Dep(), Dep()]
    d_xin_body = Dep(); d_xin_tail = Dep()
    d_u = [Dep() for _ in range(NDC)]
    d_z = [Dep() for _ in range(NDC)]
    d_delta = [Dep() for _ in range(NDC)]
    d_yg = [Dep() for _ in range(NDC)]
    d_Brep = Dep(); d_Crep = Dep(); d_H = Dep()
    d_dA = [Dep() for _ in range(8)]
    d_du = [Dep()]
    d_hcar = Dep()
    d_sm = [Dep() for _ in range(4)]
    d_tres = d_sm[2]
    d_selrep = [Dep() for _ in range(16)]; d_selsb = Dep()
    d_scr = [Dep() for _ in range(2)]
    d_dtlr = Dep(); d_Bsb = Dep(); d_Csb = Dep()
    d_slot = [Dep() for _ in range(8)]
    d_windb = [Dep() for _ in range(3)]
    d_keydb = [Dep() for _ in range(2)]
    d_valdb = [Dep() for _ in range(2)]
    d_xin8 = [Dep() for _ in range(8)]
    d_tails = Dep()
    d_stg32 = [[d_yg[0], d_yg[1]], [d_yg[2], d_yg[3]]]
    d_stgb = [[d_H], [d_Crep]]
    d_const = Dep()
    d_dsk = Dep(); d_cb = Dep(); d_bdt = Dep(); d_Asc = Dep()
    d_xpw = Dep(); d_wdtw = Dep(); d_selww = Dep()
    d_selscr = [Dep(), Dep()]; d_bscr = [Dep(), Dep()]; d_cscr = [Dep(), Dep()]

    for ci, cv in enumerate(CONSTS):
        E("dve", lambda e, ci=ci, cv=cv: e.memset(cpool[:, ci:ci + 1], cv), writes=[d_const])
    E("dve", lambda e: e.memset(hcar[:], 0.0), writes=[d_hcar])
    E("dve", lambda e: e.memset(tails[:], 0.0), writes=[d_tails])
    E("dve", lambda e: e.memset(ones_bf[:], 1.0), writes=[d_const])
    E("dve", lambda e: e.memset(dtlr_sb[64:65, :], 1.0), writes=[d_const])
    for _e in ("pe", "act", "pool", "sq"):
        E(_e, lambda e: e.nop(), reads=[d_const])

    E("sq", lambda e: e.dma_start(out=dsk_sb[:], in_=dsk_d), writes=[d_dsk], dma="dir")
    E("sq", lambda e: e.dma_start(out=cb_sb[:], in_=cb_d), writes=[d_cb], dma="dir")
    E("sq", lambda e: e.dma_start(out=sm[0][:], in_=alog_d), writes=[d_sm[0]], dma="dir")
    dir_total = ("dir", S.cnt["dir"])
    for d in (d_dsk, d_cb, d_sm[0]):
        d.w = dir_total
    E("act", lambda e: e.activation(out=sm[1][:], in_=sm[0][:], func=AF.Exp),
      reads=[d_sm[0]], writes=[d_sm[1]])
    E("pool", lambda e: e.tensor_scalar_mul(out=A_sc[:], in0=sm[1][:], scalar1=-1.0),
      reads=[d_sm[1], d_const], writes=[d_Asc])

    rr = [0]

    def conv_round(src_ap, dst_sb=None, dst_sb_dep=None, dst_dram=None, nparts=128):
        s = rr[0] % 2
        rr[0] += 1
        wdt = src_ap.shape[-1]
        E("sq", lambda e, a=src_ap, s=s, p=nparts, w=wdt: e.dma_start(
            out=stg32[s][0:p, :w], in_=a),
          writes=d_stg32[s], dma=f"stgi{s}")
        if dst_sb is not None:
            E("dve", lambda e, d=dst_sb, s=s, p=nparts, w=wdt: e.tensor_copy(
                out=d, in_=stg32[s][0:p, :w]),
              reads=d_stg32[s], writes=[dst_sb_dep])
        else:
            E("dve", lambda e, s=s, p=nparts, w=wdt: e.tensor_copy(
                out=stgb[s][0:p, :w], in_=stg32[s][0:p, :w]),
              reads=d_stg32[s], writes=d_stgb[s])
            E("sq", lambda e, a=dst_dram, s=s, p=nparts, w=wdt: e.dma_start(
                out=a, in_=stgb[s][0:p, :w]),
              reads=d_stgb[s], dma=f"scr{s}")

    for kc in range(NFC):
        for j in range(8):
            conv_round(w_in_d[kc * 128:(kc + 1) * 128, j * 512:(j + 1) * 512],
                       dst_dram=win_bf[kc * 128:(kc + 1) * 128, j * 512:(j + 1) * 512])
    for ei in range(16):
        for j in range(2):
            conv_round(keys_d[ei, :, j * 512:(j + 1) * 512],
                       dst_dram=keys_bf[ei, :, j * 512:(j + 1) * 512])
    for ei in range(16):
        for j in range(2):
            conv_round(vals_d[ei, :, j * 512:(j + 1) * 512],
                       dst_dram=vals_bf[ei, :, j * 512:(j + 1) * 512])
    for kc in range(NDC):
        for j in range(2):
            conv_round(wout_d[kc * 128:(kc + 1) * 128, j * 512:(j + 1) * 512],
                       dst_dram=wout_bf[kc * 128:(kc + 1) * 128, j * 512:(j + 1) * 512])
    for dc in range(NDC):
        conv_round(convd_d[dc], dst_dram=convd_bf[dc])
    for kc in range(NDC):
        conv_round(xproj_d[kc * 128:(kc + 1) * 128, :],
                   dst_sb=xproj_sb[:, kc * 128:(kc + 1) * 128], dst_sb_dep=d_xpw)
    for j in range(4):
        conv_round(wdt_d[:, j * 512:(j + 1) * 512],
                   dst_sb=wdt_sb[:, j * 512:(j + 1) * 512], dst_sb_dep=d_wdtw, nparts=DTR + 1)
    for kc in range(NFC):
        conv_round(selw_d[kc * 128:(kc + 1) * 128, :],
                   dst_sb=selw_sb[:, kc * 16:(kc + 1) * 16], dst_sb_dep=d_selww)

    SCR_TOTALS = [("scr0", S.cnt.get("scr0", 0)), ("scr1", S.cnt.get("scr1", 0))]

    x_src = x_in_d.rearrange("(fc p) L -> p fc L", p=128)
    out_dst = out_d.rearrange("(fc p) L -> p fc L", p=128)

    def dma_x_in(c):
        xb = xB[c % 4]
        E("sq", lambda e, c=c, xb=xb: e.dma_start(out=xb[:], in_=x_src[:, :, c * T:(c + 1) * T]),
          writes=[d_x[c % 4]], dma=f"xin{c % 4}")

    def dma_x_out(c):
        xb = xB[c % 4]
        E("sq", lambda e, c=c, xb=xb: e.dma_start(out=out_dst[:, :, c * T:(c + 1) * T], in_=xb[:]),
          reads=[d_x[c % 4]], dma=f"xout{c % 4}")

    def ln(c):
        xb = xB[c % 4]
        dx = d_x[c % 4]
        E("pool", lambda e, xb=xb: e.tensor_copy(out=xb16, in_=xb[:]),
          reads=[dx], writes=d_selrep[0::2])
        E("act", lambda e, xb=xb: e.activation(out=x2b, in_=xb[:], func=AF.Square),
          reads=[dx], writes=d_selrep[1::2])
        psq = pbank[4][:, 0:512]
        for fc in range(NFC):
            E("pe", lambda e, fc=fc, psq=psq: e.matmul(
                out=psq, lhsT=ones_bf[:],
                rhs=selrep[:, 2 * fc:2 * fc + 2, :].rearrange("p a b -> p (a b)"),
                start=(fc == 0), stop=(fc == NFC - 1)),
              reads=[d_selrep[2 * fc], d_selrep[2 * fc + 1], d_const], writes=[d_slot[4]])
        E("act", lambda e, psq=psq: e.activation(out=sm[0][:], in_=psq[:, 0:T], func=AF.Copy, scale=1.0 / D),
          reads=[d_slot[4]], writes=[d_sm[0]])
        E("act", lambda e, psq=psq: e.activation(out=sm[1][:], in_=psq[:, T:2 * T], func=AF.Copy, scale=1.0 / D),
          reads=[d_slot[4]], writes=[d_sm[1]])
        E("pool", lambda e: e.tensor_tensor(out=sm[2][:], in0=sm[0][:], in1=sm[0][:], op=OP.mult),
          reads=[d_sm[0]], writes=[d_sm[2]])
        E("pool", lambda e: e.tensor_tensor(out=sm[1][:], in0=sm[1][:], in1=sm[2][:], op=OP.subtract),
          reads=[d_sm[2]], writes=[d_sm[1]])
        E("act", lambda e: e.activation(out=sm[1][:], in_=sm[1][:], func=AF.Ln, bias=EPS),
          writes=[d_sm[1]])
        E("act", lambda e: e.activation(out=sm[1][:], in_=sm[1][:], func=AF.Exp, scale=-0.5),
          writes=[d_sm[1]])
        E("pool", lambda e, xb=xb: e.tensor_tensor(
            out=xn2[c % 2][:], in0=xb[:],
            in1=sm[0][:].unsqueeze(1).broadcast_to([128, NFC, T]), op=OP.subtract),
          reads=[dx, d_sm[0]], writes=[d_xn2[c % 2]])
        E("pool", lambda e: e.tensor_tensor(
            out=xn2[c % 2][:], in0=xn2[c % 2][:],
            in1=sm[1][:].unsqueeze(1).broadcast_to([128, NFC, T]), op=OP.mult),
          reads=[d_sm[1]], writes=[d_xn2[c % 2]])

    def ffn_pieces(c, scale):
        xnc = xn2[c % 2]
        d_xnc = d_xn2[c % 2]
        xb = xB[c % 4]
        dx = d_x[c % 4]

        def p_sel():
            S.tag = "ffn_sel"
            psel = pslot[5]
            for kc in range(NFC):
                E("pe", lambda e, kc=kc, psel=psel: e.matmul(
                    out=psel[0:16, :], lhsT=selw_sb[:, kc * 16:(kc + 1) * 16], rhs=xnc[:, kc, :],
                    start=(kc == 0), stop=(kc == NFC - 1)),
                  reads=[d_xnc, d_selww], writes=[d_slot[5]])
            E("act", lambda e, psel=psel: e.activation(out=bcs_sb[64:80, :], in_=psel[0:16, :], func=AF.Sigmoid),
              reads=[d_slot[5]], writes=[d_selsb])
            q = c % 2
            E("pool", lambda e, q=q: e.dma_start(out=selscr[q], in_=sel_sb[:]),
              reads=[d_selsb], writes=[d_selscr[q]], dma=f"rso{q}")
            E("pool", lambda e, q=q: e.dma_start(
                out=selrep[:], in_=selscr[q].unsqueeze(0).broadcast_to([128, 16, T])),
              reads=[d_selscr[q]], writes=d_selrep, dma=f"rsi{q}")

        def mk_p1(e0):
            def go():
                S.tag = "ffn_p1"
                for ei in (e0, e0 + 1):
                    pk = ei % 2
                    p = ei % 2
                    sl = 6 + (ei % 2)
                    E("sq", lambda e, ei=ei, pk=pk: e.dma_start(out=key_db[pk][:], in_=keys_bf[ei]),
                      writes=[d_keydb[pk]], dma=f"key{pk}", extra_waits=SCR_TOTALS)
                    for kc in range(NFC):
                        E("pe", lambda e, kc=kc, sl=sl, pk=pk: e.matmul(
                            out=pslot[sl], lhsT=key_db[pk][:, kc * 128:(kc + 1) * 128], rhs=xnc[:, kc, :],
                            start=(kc == 0), stop=(kc == NFC - 1)),
                          reads=[d_keydb[pk], d_xnc], writes=[d_slot[sl]])
                    E("act", lambda e, sl=sl, p=p: e.activation(out=scr_sb[p][:], in_=pslot[sl], func=AF.Relu),
                      reads=[d_slot[sl]], writes=[d_scr[p]])
                    E("pool", lambda e, ei=ei, p=p: e.tensor_tensor(out=selrep[:, ei, :], in0=scr_sb[p][:],
                                                                    in1=selrep[:, ei, :], op=OP.mult),
                      reads=[d_scr[p]], writes=[d_selrep[ei]])
            return go

        def mk_p2(e0):
            def go():
                S.tag = "ffn_p2"
                for ei in (e0, e0 + 1):
                    p = ei % 2
                    E("sq", lambda e, ei=ei, p=p: e.dma_start(out=val_db[p][:], in_=vals_bf[ei]),
                      writes=[d_valdb[p]], dma=f"val{p}", extra_waits=SCR_TOTALS)
                    for vc in range(8):
                        E("pe", lambda e, ei=ei, vc=vc, p=p: e.matmul(
                            out=pslot[vc], lhsT=val_db[p][:, vc * 128:(vc + 1) * 128],
                            rhs=selrep[:, ei, :],
                            start=(ei == 0), stop=(ei == 15)),
                          reads=[d_valdb[p], d_selrep[ei]], writes=[d_slot[vc]])
            return go

        def p_evac():
            S.tag = "ffn_evac"
            for vc in range(8):
                tr = sm[2 + vc % 2]
                dtr = d_sm[2 + vc % 2]
                E("act", lambda e, vc=vc, scale=scale, tr=tr: e.activation(
                    out=tr[:], in_=pslot[vc], func=AF.Copy, scale=scale),
                  reads=[d_slot[vc]], writes=[dtr])
                E("pool", lambda e, vc=vc, xb=xb, tr=tr: e.tensor_tensor(
                    out=xb[:, vc, :], in0=xb[:, vc, :], in1=tr[:], op=OP.add),
                  reads=[dtr, dx], writes=[dx])

        return ([(0, p_sel)] + [(0, mk_p1(e0)) for e0 in range(0, 16, 2)]
                + [(0, mk_p2(e0)) for e0 in range(0, 16, 2)] + [(0, p_evac)])

    def win_pieces(c, half):
        cip = c % 2
        xnc = xn2[c % 2]
        d_xnc = d_xn2[c % 2]

        def mk(h, q, inline_ev=False):
            def mms():
                for kc in range(NFC):
                    p = (q * NFC + kc) % 3
                    cols = h * 2048 + q * 1024
                    E("sq", lambda e, kc=kc, p=p, cols=cols: e.dma_start(
                        out=win_db[p][:], in_=win_bf[kc * 128:(kc + 1) * 128, cols:cols + 1024]),
                      writes=[d_windb[p]], dma=f"win{p}", extra_waits=SCR_TOTALS)
                    for m in range(8):
                        E("pe", lambda e, kc=kc, m=m, p=p: e.matmul(
                            out=pslot[m], lhsT=win_db[p][:, m * 128:(m + 1) * 128], rhs=xnc[:, kc, :],
                            start=(kc == 0), stop=(kc == NFC - 1)),
                          reads=[d_windb[p], d_xnc], writes=[d_slot[m]])
                if inline_ev:
                    evs()

            def evs():
                for m in range(8):
                    mc = q * 8 + m
                    if h == 0:
                        E("act", lambda e, m=m, mc=mc: e.activation(
                            out=x_inb[:, mc % 8, 3:3 + T], in_=pslot[m], func=AF.Copy),
                          reads=[d_slot[m]], writes=[d_xin8[mc % 8]])
                    else:
                        E("act", lambda e, m=m, mc=mc, cip=cip: e.activation(
                            out=zB[:, mc, cip * T:(cip + 1) * T], in_=pslot[m], func=AF.Silu),
                          reads=[d_slot[m]], writes=[d_z[mc]])
            return mms, evs
        return mk

    def conv_pieces(c):
        cip = c % 2
        par = (c // 2) % 2

        def mk(d0):
            def go():
                s0 = d0 % 8
                xds = [d_xin8[(d0 + i) % 8] for i in range(4)]
                E("act", lambda e, d0=d0, s0=s0: e.activation(
                    out=x_inb[:, s0:s0 + 4, 0:3], in_=tails[:, d0:d0 + 4, :], func=AF.Copy),
                  reads=[d_tails], writes=xds)
                for dc in range(d0, d0 + 4):
                    s8 = dc % 8
                    p = dc % 2
                    E("sq", lambda e, dc=dc, p=p: e.dma_start(out=key_db[p][:, 0:512], in_=convd_bf[dc]),
                      writes=[d_keydb[p]], dma=f"key{p}", extra_waits=SCR_TOTALS)
                    sl = 6 + (dc % 2)
                    for k in range(4):
                        E("pe", lambda e, dc=dc, k=k, p=p, sl=sl, s8=s8: e.matmul(
                            out=pslot[sl], lhsT=key_db[p][:, k * 128:(k + 1) * 128],
                            rhs=x_inb[:, s8, k:k + T],
                            start=(k == 0), stop=(k == 3)),
                          reads=[d_keydb[p], d_xin8[s8]], writes=[d_slot[sl]])
                    udst = (uB[:, dc, cip * T:(cip + 1) * T] if dc < 14
                            else uX[par][:, dc - 14, cip * T:(cip + 1) * T])
                    E("act", lambda e, dc=dc, sl=sl, udst=udst: e.activation(
                        out=udst, in_=pslot[sl], func=AF.Silu,
                        bias=cb_sb[:, dc:dc + 1]),
                      reads=[d_slot[sl], d_cb], writes=[d_u[dc]])
                E("act", lambda e, d0=d0, s0=s0: e.activation(
                    out=tails[:, d0:d0 + 4, :], in_=x_inb[:, s0:s0 + 4, T:T + 3], func=AF.Copy),
                  reads=xds, writes=[d_tails])
            return go
        return [(d0 + 3, mk(d0)) for d0 in (0, 4, 8)] + [(13, mk(12))]

    def xproj_pieces(c):
        cip = c % 2

        def p_xproj():
            S.tag = "xproj"
            px = pslot[5]
            par = (c // 2) % 2
            for kc in range(NDC):
                usrc = (uB[:, kc, cip * T:(cip + 1) * T] if kc < 14
                        else uX[par][:, kc - 14, cip * T:(cip + 1) * T])
                E("pe", lambda e, kc=kc, px=px, usrc=usrc: e.matmul(
                    out=px, lhsT=xproj_sb[:, kc * 128:(kc + 1) * 128],
                    rhs=usrc,
                    start=(kc == 0), stop=(kc == NDC - 1)),
                  reads=[d_u[kc], d_xpw], writes=[d_slot[5]])
            E("act", lambda e, px=px: e.activation(out=dtlr_sb[0:64, :], in_=px[64:128, :], func=AF.Copy),
              reads=[d_slot[5]], writes=[d_dtlr])
            E("act", lambda e, px=px: e.activation(out=Bsb[:], in_=px[0:16, :], func=AF.Copy),
              reads=[d_slot[5]], writes=[d_Bsb])
            E("act", lambda e, px=px: e.activation(out=bcs_sb[32:48, :], in_=px[32:48, :], func=AF.Copy),
              reads=[d_slot[5]], writes=[d_Csb])

        def p_bounce_out():
            q = c % 2
            E("pool", lambda e, q=q: e.dma_start(out=bscr[q], in_=Bsb[:]),
              reads=[d_Bsb], writes=[d_bscr[q]], dma=f"rbo{q}")
            E("pool", lambda e, q=q: e.dma_start(out=cscr[q], in_=Csb[:]),
              reads=[d_Csb], writes=[d_cscr[q]], dma=f"rco{q}")

        def p_bcastB():
            q = c % 2
            E("pool", lambda e, q=q, cip=cip: e.dma_start(
                out=BrepB[:, :, cip * T:(cip + 1) * T],
                in_=bscr[q].unsqueeze(0).broadcast_to([128, 16, T])),
              reads=[d_bscr[q]], writes=[d_Brep], dma=f"rbi{q}")

        def p_bcastC():
            q = c % 2
            E("pool", lambda e, q=q, cip=cip: e.dma_start(
                out=CrepB[:, :, cip * T:(cip + 1) * T],
                in_=cscr[q].unsqueeze(0).broadcast_to([128, 16, T])),
              reads=[d_cscr[q]], writes=[d_Crep], dma=f"rci{q}")

        pend = []

        def emit_dt_stt():
            mc, bk, lvd = pend.pop(0)
            E("dve", lambda e, mc=mc, cip=cip, bk=bk, lvd=lvd: e.scalar_tensor_tensor(
                out=deltaB[:, mc:mc + 2, cip * T:(cip + 1) * T],
                in0=pbank[bk][:, 0:2 * T], scalar=0.0, in1=smT[:, lvd:lvd + 2, :],
                op0=OP.max, op1=OP.add),
              reads=[d_slot[bk], d_sm[lvd], d_sm[lvd + 1], d_const],
              writes=[d_delta[mc], d_delta[mc + 1]])

        def mk_dt(m0):
            def go():
                S.tag = "dt"
                for mc in range(m0, m0 + 4, 2):
                    if len(pend) >= 2:
                        emit_dt_stt()
                    pr = (mc // 2) % 2
                    bk = 6 + pr
                    lvd = 2 * pr
                    full = pbank[bk][:, 0:2 * T]
                    lv = smT[:, lvd:lvd + 2, :].rearrange("p a b -> p (a b)")
                    E("pe", lambda e, mc=mc, bk=bk: e.matmul(
                        out=pbank[bk][:, 0:T], lhsT=wdt_sb[:, mc * 128:(mc + 1) * 128], rhs=dtlr_sb[:],
                        start=True, stop=True),
                      reads=[d_dtlr, d_wdtw], writes=[d_slot[bk]])
                    E("pe", lambda e, mc=mc, bk=bk: e.matmul(
                        out=pbank[bk][:, T:2 * T], lhsT=wdt_sb[:, (mc + 1) * 128:(mc + 2) * 128], rhs=dtlr_sb[:],
                        start=True, stop=True),
                      reads=[d_dtlr, d_wdtw], writes=[d_slot[bk]])
                    E("act", lambda e, bk=bk, lv=lv, full=full: e.activation(out=lv, in_=full, func=AF.Abs),
                      reads=[d_slot[bk]], writes=[d_sm[lvd], d_sm[lvd + 1]])
                    E("act", lambda e, lv=lv: e.activation(out=lv, in_=lv, func=AF.Exp, scale=-1.0),
                      writes=[d_sm[lvd], d_sm[lvd + 1]])
                    E("act", lambda e, lv=lv: e.activation(out=lv, in_=lv, func=AF.Ln, bias=1.0),
                      writes=[d_sm[lvd], d_sm[lvd + 1]])
                    pend.append((mc, bk, lvd))
                if m0 + 4 >= NDC - 4:
                    while pend:
                        emit_dt_stt()
            return go
        return ([(13, p_xproj), (13, p_bounce_out)]
                + [(13, mk_dt(m0)) for m0 in (0, 4, 8)] + [(15, mk_dt(12))]
                + [(15, p_bcastB), (15, p_bcastC)])

    def scan_dc_q(P, dc, qtr):
        S.tag = "scan"
        usrc = uB[:, dc, :] if dc < 14 else uX[P % 2][:, dc - 14, :]
        if qtr == 0:
            E("dve", lambda e, dc=dc, usrc=usrc: e.tensor_tensor(
                out=du_s[0][:], in0=deltaB[:, dc, :], in1=usrc, op=OP.mult),
              reads=[d_delta[dc], d_u[dc]], writes=[d_du[0]])
            E("dve", lambda e, dc=dc: e.tensor_tensor(
                out=Hbig[:], in0=du_s[0][:].unsqueeze(1).broadcast_to([128, NS, TS]),
                in1=BrepB[:], op=OP.mult),
              reads=[d_du[0], d_Brep], writes=[d_H])
        for n in range(4 * qtr, 4 * qtr + 4):
            ap_ = dA_s[n % 8]
            E("dve", lambda e, dc=dc, n=n, ap_=ap_: e.tensor_tensor_scan(
                out=Hbig[:, n, :], data0=ap_[:], data1=Hbig[:, n, :],
                initial=hcar[:, dc * NS + n:dc * NS + n + 1], op0=OP.mult, op1=OP.add),
              reads=[d_dA[n % 8], d_hcar], writes=[d_H])
        if qtr == 3:
            E("dve", lambda e, dc=dc: e.tensor_copy(
                out=hcar[:, dc * NS:(dc + 1) * NS], in_=Hbig[:, :, TS - 1:TS].squeeze(2)),
              reads=[d_H], writes=[d_hcar])
            E("dve", lambda e: e.tensor_tensor(out=Hbig[:], in0=Hbig[:], in1=CrepB[:], op=OP.mult),
              reads=[d_H, d_Crep], writes=[d_H])
            E("dve", lambda e: e.tensor_tensor(out=Hbig[:, 0:8, :], in0=Hbig[:, 0:8, :],
                                               in1=Hbig[:, 8:16, :], op=OP.add),
              reads=[d_H], writes=[d_H])
            E("dve", lambda e: e.tensor_tensor(out=Hbig[:, 0:4, :], in0=Hbig[:, 0:4, :],
                                               in1=Hbig[:, 4:8, :], op=OP.add),
              reads=[d_H], writes=[d_H])
            E("dve", lambda e: e.tensor_tensor(out=Hbig[:, 0:2, :], in0=Hbig[:, 0:2, :],
                                               in1=Hbig[:, 2:4, :], op=OP.add),
              reads=[d_H], writes=[d_H])
            E("dve", lambda e: e.tensor_tensor(out=y_s[:], in0=Hbig[:, 0, :],
                                               in1=Hbig[:, 1, :], op=OP.add),
              reads=[d_H], writes=[d_du[0]])
            E("dve", lambda e, dc=dc, usrc=usrc: e.scalar_tensor_tensor(
                out=y_s[:], in0=usrc, scalar=dsk_sb[:, dc:dc + 1], in1=y_s[:],
                op0=OP.mult, op1=OP.add),
              reads=[d_u[dc], d_dsk, d_du[0]], writes=[d_du[0]])
            E("dve", lambda e, dc=dc: e.tensor_tensor(
                out=ygB[:, dc, :], in0=y_s[:], in1=zB[:, dc, :], op=OP.mult),
              reads=[d_du[0], d_z[dc]], writes=[d_yg[dc]])
        for j in range(4):
            nn = 4 * qtr + 8 + j
            gg = P * NDC + dc + nn // NS
            n2 = nn % NS
            if gg >= NPAIR * NDC:
                continue
            d2 = gg % NDC
            ap_ = dA_s[n2 % 8]
            E("act", lambda e, d2=d2, n2=n2, ap_=ap_: e.activation(
                out=ap_[:], in_=deltaB[:, d2, :], func=AF.Exp,
                scale=A_sc[:, d2 * NS + n2:d2 * NS + n2 + 1]),
              reads=[d_delta[d2], d_Asc], writes=[d_dA[n2 % 8]])

    def wout_pieces(P):
        mm = []

        def mk_dc(dc):
            def go():
                S.tag = "wout"
                p = dc % 3
                E("sq", lambda e, dc=dc, p=p: e.dma_start(
                    out=win_db[p][:], in_=wout_bf[dc * 128:(dc + 1) * 128, :]),
                  writes=[d_windb[p]], dma=f"win{p}", extra_waits=SCR_TOTALS)
                for vc in range(8):
                    E("pe", lambda e, dc=dc, vc=vc, p=p: e.matmul(
                        out=pbank[vc][:, 0:512], lhsT=win_db[p][:, vc * 128:(vc + 1) * 128],
                        rhs=ygB[:, dc, :],
                        start=(dc == 0), stop=(dc == NDC - 1)),
                      reads=[d_windb[p], d_yg[dc]], writes=[d_slot[vc]])
            return go

        def mk_evac(h):
            def go():
                S.tag = "wout"
                c = 2 * P + h
                xb = xB[c % 4]
                dx = d_x[c % 4]
                for vc in range(8):
                    tr = sm[2 + vc % 2]
                    dtr = d_sm[2 + vc % 2]
                    E("act", lambda e, vc=vc, h=h, tr=tr: e.activation(
                        out=tr[:], in_=pbank[vc][:, h * T:(h + 1) * T], func=AF.Copy),
                      reads=[d_slot[vc]], writes=[dtr])
                    E("pool", lambda e, vc=vc, xb=xb, dx=dx, tr=tr: e.tensor_tensor(
                        out=xb[:, vc, :], in0=xb[:, vc, :], in1=tr[:], op=OP.add),
                      reads=[dtr, dx], writes=[dx])
            return go

        for dc in range(NDC):
            mm.append((0, mk_dc(dc)))
        ev = [(0, mk_evac(0)), (0, mk_evac(1))]
        return mm, ev

    def front_pieces(c):
        def p_dma():
            dma_x_in(c)

        def p_ln():
            S.tag = "ln"
            ln(c)

        def p_ln2():
            ln(c)
        wmk = win_pieces(c, 0)
        cps = conv_pieces(c)
        m00, e00 = wmk(0, 0)
        m01, e01 = wmk(0, 1)
        m10, e10 = wmk(1, 0)
        m11, _ = wmk(1, 1, inline_ev=True)
        kx = (lambda k: k) if c % 2 == 0 else (lambda k: 13)
        return ([(0, p_dma), (0, p_ln)] + ffn_pieces(c, 0.5) + [(0, p_ln2)]
                + [(kx(1), m00), (kx(2), e00), (kx(cps[0][0]), cps[0][1]), (kx(cps[1][0]), cps[1][1]),
                   (kx(7), m01), (kx(9), e01), (kx(cps[2][0]), cps[2][1]), (kx(cps[3][0]), cps[3][1]),
                   (9 if c % 2 == 0 else 10, m10), (10, e10), (15, m11)]
                + xproj_pieces(c))

    def tail_pieces(c):
        def p_ln():
            S.tag = "ln"
            ln(c)

        def p_out():
            dma_x_out(c)
        return [(0, p_ln)] + ffn_pieces(c, 0.5) + [(0, p_out)]

    for c in (0, 1):
        for _mdc, pc in front_pieces(c):
            pc()
    S.tag = "scan"
    for n in range(8):
        E("act", lambda e, n=n, ap_=dA_s[n % 8]: e.activation(
            out=ap_[:], in_=deltaB[:, 0, :], func=AF.Exp,
            scale=A_sc[:, n:n + 1]),
          reads=[d_delta[0], d_Asc], writes=[d_dA[n % 8]])
    for P in range(NPAIR):
        pieces = []
        wparts = {}
        if P >= 1:
            mm, ev = wout_pieces(P - 1)
            wparts[(0, 0)] = mm
            wparts[(0, 2)] = [ev[0]]
            wparts[(0, 3)] = [ev[1]]
            pieces += tail_pieces(2 * P - 2) + tail_pieces(2 * P - 1)
        if P + 1 <= NPAIR - 1:
            pieces += front_pieces(2 * P + 2) + front_pieces(2 * P + 3)
        pieces.sort(key=lambda t: t[0])
        n = len(pieces)
        k = (n + 4 * NDC - 1) // (4 * NDC)
        j = 0
        for dc in range(NDC):
            for qtr in range(4):
                scan_dc_q(P, dc, qtr)
                for _m, pc in wparts.pop((dc, qtr), ()):
                    pc()
                budget = k
                while j < n and (pieces[j][0] < dc or (pieces[j][0] == dc and qtr == 3)) and (budget > 0 or (dc == NDC - 1 and qtr == 3)):
                    pieces[j][1]()
                    j += 1
                    budget -= 1
        while j < n:
            pieces[j][1]()
            j += 1
    mm, ev = wout_pieces(NPAIR - 1)
    for _mdc, pc in mm + ev:
        pc()
    for c in (2 * NPAIR - 2, 2 * NPAIR - 1):
        for _mdc, pc in tail_pieces(c):
            pc()

    final_waits = [(s, S.cnt.get(s, 0)) for s in ("xout0", "xout1", "xout2", "xout3")
                   if S.cnt.get(s, 0) > 0]

    with nc.Block() as block:
        def make(eng_name):
            def body(eng):
                for fn, waits, sem_name, _tag in S.ops[eng_name]:
                    for (e2, c2) in waits:
                        eng.wait_ge(sems[e2], c2)
                    inst = fn(eng)
                    inst.then_inc(sems[sem_name], 16 if sem_name in DMA_SEMS else 1)
                if eng_name == "sq":
                    for (s, c2) in final_waits:
                        eng.wait_ge(sems[s], c2)
            return body
        block.tensor(make("pe"))
        block.scalar(make("act"))
        block.vector(make("dve"))
        block.gpsimd(make("pool"))
        block.sync(make("sq"))

    es.close()
    return nc


def _host_inputs(inputs, L=4096):
    W_in = np.ascontiguousarray(inputs["L0_attn_W_in"], dtype=np.float32)
    conv_w = np.asarray(inputs["L0_attn_conv_w"], dtype=np.float32)[:, 0, :]
    conv_b = np.asarray(inputs["L0_attn_conv_b"], dtype=np.float32)
    W_xp = np.asarray(inputs["L0_attn_W_xproj"], dtype=np.float32)
    W_dt0 = np.asarray(inputs["L0_attn_W_dt"], dtype=np.float32)
    b_dt0 = np.asarray(inputs["L0_attn_b_dt"], dtype=np.float32)
    W_dt = np.ascontiguousarray(np.concatenate([W_dt0, b_dt0[None, :]], axis=0))
    b_dt = np.asarray(inputs["L0_attn_b_dt"], dtype=np.float32)
    A_log = np.asarray(inputs["L0_attn_A_log"], dtype=np.float32)
    D_skip = np.asarray(inputs["L0_attn_D_skip"], dtype=np.float32)
    W_out = np.ascontiguousarray(inputs["L0_attn_W_out"], dtype=np.float32)
    selw = np.asarray(inputs["L0_ffn_expert_sel"], dtype=np.float32)
    keys = np.asarray(inputs["L0_ffn_keys"], dtype=np.float32)
    values = np.asarray(inputs["L0_ffn_values"], dtype=np.float32)

    keys_r = np.ascontiguousarray(
        keys.reshape(16, NFC, 128, 128).transpose(0, 2, 1, 3).reshape(16, 128, 1024))
    vals_r = np.ascontiguousarray(values)
    xproj_r = np.zeros((DI, 128), np.float32)
    xproj_r[:, 0:16] = W_xp[:, 64:80]
    xproj_r[:, 32:48] = W_xp[:, 80:96]
    xproj_r[:, 64:128] = W_xp[:, 0:64]
    selw_r = np.ascontiguousarray(selw.T)

    def di_layout(v):
        return np.ascontiguousarray(v.reshape(NDC, 128).T)
    alog_r = np.ascontiguousarray(
        A_log.reshape(NDC, 128, NS).transpose(1, 0, 2).reshape(128, NDC * NS))
    convd_r = np.zeros((NDC, 128, 4, 128), np.float32)
    idx = np.arange(128)
    for dc in range(NDC):
        for k in range(4):
            convd_r[dc, idx, k, idx] = conv_w[dc * 128 + idx, k]
    convd_r = np.ascontiguousarray(convd_r.reshape(NDC, 128, 512))
    shared = dict(
        w_in=W_in, keys_r=keys_r, vals_r=vals_r, wout_r=W_out,
        convd_r=convd_r, xproj_r=xproj_r, wdt_r=W_dt, selw_r=selw_r, alog_r=alog_r,
        dsk_r=di_layout(D_skip), cb_r=di_layout(conv_b), bdt_r=di_layout(b_dt),
    )
    x = np.asarray(inputs["L0_attn_x"], dtype=np.float32)
    xs = [np.ascontiguousarray(x[b, :L, :].T) for b in range(x.shape[0])]
    return shared, xs


def kernel(**inputs):
    L = 4096
    shared, xs = _host_inputs(inputs, L)
    nc = build_program(L)
    in_maps = [dict(shared, x_f=xs[b]) for b in range(8)]
    res = run_bass_kernel_spmd(nc, in_maps, core_ids=list(range(8)))
    out = np.empty((8, L, D), np.float32)
    for b in range(8):
        out[b] = np.asarray(res.results[b]["out_f"], dtype=np.float32).T
    return out


# version 56
# speedup vs baseline: 1.0225x; 1.0122x over previous
import contextlib
import numpy as np
import concourse.bass as bass
import concourse.mybir as mybir
from concourse.bass_utils import run_bass_kernel_spmd

F32 = mybir.dt.float32
F16 = mybir.dt.float16
BF16 = mybir.dt.bfloat16
AF = mybir.ActivationFunctionType
OP = mybir.AluOpType

D = 1024
DI = 2048
NS = 16
DTR = 64
T = 256
TS = 512
NFC = D // 128
NDC = DI // 128
EPS = 1e-5

ENGS = ("pe", "act", "dve", "pool", "sq")
DMA_SEMS = ("xin0", "xin1", "xin2", "xin3", "xout0", "xout1", "xout2", "xout3",
            "win0", "win1", "win2", "key0", "key1", "key2", "val0", "val1", "val2",
            "stgi0", "stgi1", "scr0", "scr1", "dir",
            "rso0", "rso1", "rsi0", "rsi1", "rbo0", "rbo1", "rbi0", "rbi1",
            "rco0", "rco1", "rci0", "rci1")


class Dep:
    __slots__ = ("w", "r")

    def __init__(self):
        self.w = None
        self.r = []


class Sched:

    def __init__(self):
        self.ops = {e: [] for e in ENGS}
        self.cnt = {}
        self.waited = {e: {} for e in ENGS}
        self.tag = "init"

    def emit(self, eng, fn, reads=(), writes=(), dma=None, extra_waits=()):
        need = {}

        def add(e, c):
            if c > need.get(e, 0):
                need[e] = c

        for d in reads:
            if d.w:
                add(*d.w)
        for d in writes:
            if d.w:
                add(*d.w)
            for (e, c) in d.r:
                add(e, c)
        for (e, c) in extra_waits:
            add(e, c)
        waits = []
        w = self.waited[eng]
        for e, c in need.items():
            if e == eng:
                c = min(c, self.cnt.get(eng, 0))
            if c > w.get(e, 0):
                waits.append((e, c))
                w[e] = c
        if dma is not None:
            self.cnt[dma] = self.cnt.get(dma, 0) + 16
            me = (dma, self.cnt[dma])
            self.ops[eng].append((fn, waits, dma, self.tag))
        else:
            self.cnt[eng] = self.cnt.get(eng, 0) + 1
            me = (eng, self.cnt[eng])
            self.ops[eng].append((fn, waits, eng, self.tag))
        for d in writes:
            d.w = me
            d.r = []
        for d in reads:
            d.r.append(me)


def build_program(L=4096):
    NPAIR = L // TS
    nc = bass.Bass("TRN2", target_bir_lowering=False, debug=False)

    x_in_d = nc.dram_tensor("x_f", [D, L], F32, kind="ExternalInput").ap()
    out_d = nc.dram_tensor("out_f", [D, L], F32, kind="ExternalOutput").ap()
    w_in_d = nc.dram_tensor("w_in", [D, 2 * DI], F32, kind="ExternalInput").ap()
    keys_d = nc.dram_tensor("keys_r", [16, 128, 1024], F32, kind="ExternalInput").ap()
    vals_d = nc.dram_tensor("vals_r", [16, 128, 1024], F32, kind="ExternalInput").ap()
    wout_d = nc.dram_tensor("wout_r", [DI, D], F32, kind="ExternalInput").ap()
    convd_d = nc.dram_tensor("convd_r", [NDC, 128, 512], F32, kind="ExternalInput").ap()
    xproj_d = nc.dram_tensor("xproj_r", [DI, 128], F32, kind="ExternalInput").ap()
    wdt_d = nc.dram_tensor("wdt_r", [DTR + 1, DI], F32, kind="ExternalInput").ap()
    selw_d = nc.dram_tensor("selw_r", [D, 16], F32, kind="ExternalInput").ap()
    alog_d = nc.dram_tensor("alog_r", [128, 256], F32, kind="ExternalInput").ap()
    dsk_d = nc.dram_tensor("dsk_r", [128, 16], F32, kind="ExternalInput").ap()
    cb_d = nc.dram_tensor("cb_r", [128, 16], F32, kind="ExternalInput").ap()
    bdt_d = nc.dram_tensor("bdt_r", [128, 16], F32, kind="ExternalInput").ap()

    win_bf = nc.dram_tensor("win_bf", [D, 2 * DI], BF16, kind="Internal").ap()
    keys_bf = nc.dram_tensor("keys_bf", [16, 128, 1024], BF16, kind="Internal").ap()
    vals_bf = nc.dram_tensor("vals_bf", [16, 128, 1024], BF16, kind="Internal").ap()
    wout_bf = nc.dram_tensor("wout_bf", [DI, D], BF16, kind="Internal").ap()
    convd_bf = nc.dram_tensor("convd_bf", [NDC, 128, 512], BF16, kind="Internal").ap()
    selscr = nc.dram_tensor("selscr", [2, 16, T], BF16, kind="Internal").ap()
    bscr = nc.dram_tensor("bscr", [2, 16, T], BF16, kind="Internal").ap()
    cscr = nc.dram_tensor("cscr", [2, 16, T], BF16, kind="Internal").ap()

    es = contextlib.ExitStack()

    def sb(name, shape, dt):
        return es.enter_context(nc.sbuf_tensor(name, shape, dt))

    xB = [sb(f"x{i}", [128, NFC, T], F32) for i in range(4)]
    xn2 = [sb(f"xn{i}", [128, NFC, T], BF16) for i in range(2)]
    x_inb = sb("x_inb", [128, 8, T + 3], BF16)
    tails = sb("tails", [128, NDC, 3], BF16)
    uB = sb("uB", [128, NDC, TS], BF16)
    zB = sb("zB", [128, NDC, TS], BF16)
    deltaB = sb("deltaB", [128, NDC, TS], BF16)
    ygB = sb("ygB", [128, NDC, TS], BF16)
    BrepB = sb("BrepB", [128, NS, TS], BF16)
    CrepB = sb("CrepB", [128, NS, TS], BF16)
    Hbig = sb("Hbig", [128, NS, TS], BF16)
    dA_s = [sb(f"dA{i}", [128, TS], F16) for i in range(8)]
    du_s = [sb("du0", [128, TS], BF16)]
    y_s = du_s[0]
    uX = [sb(f"uX{i}", [128, 2, TS], BF16) for i in range(2)]
    hcar = sb("hcar", [128, 256], BF16)
    A_sc = sb("A_sc", [128, 256], F32)
    dsk_sb = sb("dsk_sb", [128, 16], F32)
    cb_sb = sb("cb_sb", [128, 16], F32)
    smT = sb("smT", [128, 4, T], F32)
    sm = [smT[:, i, :] for i in range(4)]
    tres = sm[2]
    selrep = sb("selrep", [128, 16, T], BF16)
    xb16 = selrep[:, 0:2 * NFC:2, :]
    x2b = selrep[:, 1:2 * NFC:2, :]

    scr_sb = [sb(f"scr{i}", [128, T], BF16) for i in range(2)]
    dtlr_sb = sb("dtlr_sb", [DTR + 1, T], BF16)
    bcs_sb = sb("bcs_sb", [80, T], BF16)
    Bsb = bcs_sb[0:16, :]
    Csb = bcs_sb[32:48, :]
    sel_sb = bcs_sb[64:80, :]
    ones_bf = sb("ones_bf", [128, 128], BF16)
    cpool = sb("cpool", [128, 8], F32)
    xproj_sb = sb("xproj_sb", [128, 16 * 128], BF16)
    wdt_sb = sb("wdt_sb", [DTR + 1, DI], BF16)
    selw_sb = sb("selw_sb", [128, NFC * 16], BF16)
    win_db = [sb(f"windb{i}", [128, 1024], BF16) for i in range(3)]
    key_db = [sb(f"keydb{i}", [128, 1024], BF16) for i in range(2)]
    val_db = [sb(f"valdb{i}", [128, 1024], BF16) for i in range(2)]
    stg32 = [ygB[:, 0:2, :].bitcast(F32).rearrange("p a b -> p (a b)"),
             ygB[:, 2:4, :].bitcast(F32).rearrange("p a b -> p (a b)")]
    stgb = [Hbig[:, 0, :], CrepB[:, 0, :]]

    pbank = [es.enter_context(nc.psum_tensor(f"pb{i}", [128, 512], F32)) for i in range(8)]
    pslot = [pbank[i][:, 0:T] for i in range(8)]

    CONSTS = [EPS, 1.0, -1.0, -0.5, 1.0 / D, 0.5, 0.0, 2.0]
    for ci, cv in enumerate(CONSTS):
        nc.const_aps.register(F32, cv, cpool[:, ci:ci + 1])

    sems = {e: es.enter_context(nc.semaphore(f"sem_{e}")) for e in ENGS}
    for s in DMA_SEMS:
        sems[s] = es.enter_context(nc.semaphore(f"sem_{s}"))

    S = Sched()
    E = S.emit

    d_x = [Dep() for _ in range(4)]
    d_xn2 = [Dep(), Dep()]
    d_xin_body = Dep(); d_xin_tail = Dep()
    d_u = [Dep() for _ in range(NDC)]
    d_z = [Dep() for _ in range(NDC)]
    d_delta = [Dep() for _ in range(NDC)]
    d_yg = [Dep() for _ in range(NDC)]
    d_Brep = Dep(); d_Crep = Dep(); d_H = Dep()
    d_dA = [Dep() for _ in range(8)]
    d_du = [Dep()]
    d_hcar = Dep()
    d_sm = [Dep() for _ in range(4)]
    d_tres = d_sm[2]
    d_selrep = [Dep() for _ in range(16)]; d_selsb = Dep()
    d_scr = [Dep() for _ in range(2)]
    d_dtlr = Dep(); d_Bsb = Dep(); d_Csb = Dep()
    d_slot = [Dep() for _ in range(8)]
    d_windb = [Dep() for _ in range(3)]
    d_keydb = [Dep() for _ in range(2)]
    d_valdb = [Dep() for _ in range(2)]
    d_xin8 = [Dep() for _ in range(8)]
    d_tails = Dep()
    d_stg32 = [[d_yg[0], d_yg[1]], [d_yg[2], d_yg[3]]]
    d_stgb = [[d_H], [d_Crep]]
    d_const = Dep()
    d_dsk = Dep(); d_cb = Dep(); d_bdt = Dep(); d_Asc = Dep()
    d_xpw = Dep(); d_wdtw = Dep(); d_selww = Dep()
    d_selscr = [Dep(), Dep()]; d_bscr = [Dep(), Dep()]; d_cscr = [Dep(), Dep()]

    for ci, cv in enumerate(CONSTS):
        E("dve", lambda e, ci=ci, cv=cv: e.memset(cpool[:, ci:ci + 1], cv), writes=[d_const])
    E("dve", lambda e: e.memset(hcar[:], 0.0), writes=[d_hcar])
    E("dve", lambda e: e.memset(tails[:], 0.0), writes=[d_tails])
    E("dve", lambda e: e.memset(ones_bf[:], 1.0), writes=[d_const])
    E("dve", lambda e: e.memset(dtlr_sb[64:65, :], 1.0), writes=[d_const])
    for _e in ("pe", "act", "pool", "sq"):
        E(_e, lambda e: e.nop(), reads=[d_const])

    E("sq", lambda e: e.dma_start(out=dsk_sb[:], in_=dsk_d), writes=[d_dsk], dma="dir")
    E("sq", lambda e: e.dma_start(out=cb_sb[:], in_=cb_d), writes=[d_cb], dma="dir")
    E("sq", lambda e: e.dma_start(out=sm[0][:], in_=alog_d), writes=[d_sm[0]], dma="dir")
    dir_total = ("dir", S.cnt["dir"])
    for d in (d_dsk, d_cb, d_sm[0]):
        d.w = dir_total
    E("act", lambda e: e.activation(out=sm[1][:], in_=sm[0][:], func=AF.Exp),
      reads=[d_sm[0]], writes=[d_sm[1]])
    E("pool", lambda e: e.tensor_scalar_mul(out=A_sc[:], in0=sm[1][:], scalar1=-1.0),
      reads=[d_sm[1], d_const], writes=[d_Asc])

    rr = [0]

    def conv_round(src_ap, dst_sb=None, dst_sb_dep=None, dst_dram=None, nparts=128):
        s = rr[0] % 2
        rr[0] += 1
        wdt = src_ap.shape[-1]
        E("sq", lambda e, a=src_ap, s=s, p=nparts, w=wdt: e.dma_start(
            out=stg32[s][0:p, :w], in_=a),
          writes=d_stg32[s], dma=f"stgi{s}")
        if dst_sb is not None:
            E("dve", lambda e, d=dst_sb, s=s, p=nparts, w=wdt: e.tensor_copy(
                out=d, in_=stg32[s][0:p, :w]),
              reads=d_stg32[s], writes=[dst_sb_dep])
        else:
            E("dve", lambda e, s=s, p=nparts, w=wdt: e.tensor_copy(
                out=stgb[s][0:p, :w], in_=stg32[s][0:p, :w]),
              reads=d_stg32[s], writes=d_stgb[s])
            E("sq", lambda e, a=dst_dram, s=s, p=nparts, w=wdt: e.dma_start(
                out=a, in_=stgb[s][0:p, :w]),
              reads=d_stgb[s], dma=f"scr{s}")

    for kc in range(NFC):
        for j in range(8):
            conv_round(w_in_d[kc * 128:(kc + 1) * 128, j * 512:(j + 1) * 512],
                       dst_dram=win_bf[kc * 128:(kc + 1) * 128, j * 512:(j + 1) * 512])
    for ei in range(16):
        for j in range(2):
            conv_round(keys_d[ei, :, j * 512:(j + 1) * 512],
                       dst_dram=keys_bf[ei, :, j * 512:(j + 1) * 512])
    for ei in range(16):
        for j in range(2):
            conv_round(vals_d[ei, :, j * 512:(j + 1) * 512],
                       dst_dram=vals_bf[ei, :, j * 512:(j + 1) * 512])
    for kc in range(NDC):
        for j in range(2):
            conv_round(wout_d[kc * 128:(kc + 1) * 128, j * 512:(j + 1) * 512],
                       dst_dram=wout_bf[kc * 128:(kc + 1) * 128, j * 512:(j + 1) * 512])
    for dc in range(NDC):
        conv_round(convd_d[dc], dst_dram=convd_bf[dc])
    for kc in range(NDC):
        conv_round(xproj_d[kc * 128:(kc + 1) * 128, :],
                   dst_sb=xproj_sb[:, kc * 128:(kc + 1) * 128], dst_sb_dep=d_xpw)
    for j in range(4):
        conv_round(wdt_d[:, j * 512:(j + 1) * 512],
                   dst_sb=wdt_sb[:, j * 512:(j + 1) * 512], dst_sb_dep=d_wdtw, nparts=DTR + 1)
    for kc in range(NFC):
        conv_round(selw_d[kc * 128:(kc + 1) * 128, :],
                   dst_sb=selw_sb[:, kc * 16:(kc + 1) * 16], dst_sb_dep=d_selww)

    SCR_TOTALS = [("scr0", S.cnt.get("scr0", 0)), ("scr1", S.cnt.get("scr1", 0))]

    x_src = x_in_d.rearrange("(fc p) L -> p fc L", p=128)
    out_dst = out_d.rearrange("(fc p) L -> p fc L", p=128)

    def dma_x_in(c):
        xb = xB[c % 4]
        E("sq", lambda e, c=c, xb=xb: e.dma_start(out=xb[:], in_=x_src[:, :, c * T:(c + 1) * T]),
          writes=[d_x[c % 4]], dma=f"xin{c % 4}")

    def dma_x_out(c):
        xb = xB[c % 4]
        E("sq", lambda e, c=c, xb=xb: e.dma_start(out=out_dst[:, :, c * T:(c + 1) * T], in_=xb[:]),
          reads=[d_x[c % 4]], dma=f"xout{c % 4}")

    def ln(c):
        xb = xB[c % 4]
        dx = d_x[c % 4]
        E("pool", lambda e, xb=xb: e.tensor_copy(out=xb16, in_=xb[:]),
          reads=[dx], writes=d_selrep[0::2])
        E("act", lambda e, xb=xb: e.activation(out=x2b, in_=xb[:], func=AF.Square),
          reads=[dx], writes=d_selrep[1::2])
        psq = pbank[4][:, 0:512]
        for fc in range(NFC):
            E("pe", lambda e, fc=fc, psq=psq: e.matmul(
                out=psq, lhsT=ones_bf[:],
                rhs=selrep[:, 2 * fc:2 * fc + 2, :].rearrange("p a b -> p (a b)"),
                start=(fc == 0), stop=(fc == NFC - 1)),
              reads=[d_selrep[2 * fc], d_selrep[2 * fc + 1], d_const], writes=[d_slot[4]])
        E("act", lambda e, psq=psq: e.activation(out=sm[0][:], in_=psq[:, 0:T], func=AF.Copy, scale=1.0 / D),
          reads=[d_slot[4]], writes=[d_sm[0]])
        E("act", lambda e, psq=psq: e.activation(out=sm[1][:], in_=psq[:, T:2 * T], func=AF.Copy, scale=1.0 / D),
          reads=[d_slot[4]], writes=[d_sm[1]])
        E("pool", lambda e: e.tensor_tensor(out=sm[2][:], in0=sm[0][:], in1=sm[0][:], op=OP.mult),
          reads=[d_sm[0]], writes=[d_sm[2]])
        E("pool", lambda e: e.tensor_tensor(out=sm[1][:], in0=sm[1][:], in1=sm[2][:], op=OP.subtract),
          reads=[d_sm[2]], writes=[d_sm[1]])
        E("act", lambda e: e.activation(out=sm[1][:], in_=sm[1][:], func=AF.Ln, bias=EPS),
          writes=[d_sm[1]])
        E("act", lambda e: e.activation(out=sm[1][:], in_=sm[1][:], func=AF.Exp, scale=-0.5),
          writes=[d_sm[1]])
        E("pool", lambda e, xb=xb: e.tensor_tensor(
            out=xn2[c % 2][:], in0=xb[:],
            in1=sm[0][:].unsqueeze(1).broadcast_to([128, NFC, T]), op=OP.subtract),
          reads=[dx, d_sm[0]], writes=[d_xn2[c % 2]])
        E("pool", lambda e: e.tensor_tensor(
            out=xn2[c % 2][:], in0=xn2[c % 2][:],
            in1=sm[1][:].unsqueeze(1).broadcast_to([128, NFC, T]), op=OP.mult),
          reads=[d_sm[1]], writes=[d_xn2[c % 2]])

    def ffn_pieces(c, scale):
        xnc = xn2[c % 2]
        d_xnc = d_xn2[c % 2]
        xb = xB[c % 4]
        dx = d_x[c % 4]

        def p_sel():
            S.tag = "ffn_sel"
            psel = pslot[5]
            for kc in range(NFC):
                E("pe", lambda e, kc=kc, psel=psel: e.matmul(
                    out=psel[0:16, :], lhsT=selw_sb[:, kc * 16:(kc + 1) * 16], rhs=xnc[:, kc, :],
                    start=(kc == 0), stop=(kc == NFC - 1)),
                  reads=[d_xnc, d_selww], writes=[d_slot[5]])
            E("act", lambda e, psel=psel: e.activation(out=bcs_sb[64:80, :], in_=psel[0:16, :], func=AF.Sigmoid),
              reads=[d_slot[5]], writes=[d_selsb])
            q = c % 2
            E("pool", lambda e, q=q: e.dma_start(out=selscr[q], in_=sel_sb[:]),
              reads=[d_selsb], writes=[d_selscr[q]], dma=f"rso{q}")
            E("pool", lambda e, q=q: e.dma_start(
                out=selrep[:], in_=selscr[q].unsqueeze(0).broadcast_to([128, 16, T])),
              reads=[d_selscr[q]], writes=d_selrep, dma=f"rsi{q}")

        def mk_p1(e0):
            def go():
                S.tag = "ffn_p1"
                for ei in (e0, e0 + 1):
                    pk = ei % 2
                    p = ei % 2
                    sl = 6 + (ei % 2)
                    E("sq", lambda e, ei=ei, pk=pk: e.dma_start(out=key_db[pk][:], in_=keys_bf[ei]),
                      writes=[d_keydb[pk]], dma=f"key{pk}", extra_waits=SCR_TOTALS)
                    for kc in range(NFC):
                        E("pe", lambda e, kc=kc, sl=sl, pk=pk: e.matmul(
                            out=pslot[sl], lhsT=key_db[pk][:, kc * 128:(kc + 1) * 128], rhs=xnc[:, kc, :],
                            start=(kc == 0), stop=(kc == NFC - 1)),
                          reads=[d_keydb[pk], d_xnc], writes=[d_slot[sl]])
                    E("act", lambda e, sl=sl, p=p: e.activation(out=scr_sb[p][:], in_=pslot[sl], func=AF.Relu),
                      reads=[d_slot[sl]], writes=[d_scr[p]])
                    E("pool", lambda e, ei=ei, p=p: e.tensor_tensor(out=selrep[:, ei, :], in0=scr_sb[p][:],
                                                                    in1=selrep[:, ei, :], op=OP.mult),
                      reads=[d_scr[p]], writes=[d_selrep[ei]])
            return go

        def mk_p2(e0):
            def go():
                S.tag = "ffn_p2"
                for ei in (e0, e0 + 1):
                    p = ei % 2
                    E("sq", lambda e, ei=ei, p=p: e.dma_start(out=val_db[p][:], in_=vals_bf[ei]),
                      writes=[d_valdb[p]], dma=f"val{p}", extra_waits=SCR_TOTALS)
                    for vc in range(8):
                        E("pe", lambda e, ei=ei, vc=vc, p=p: e.matmul(
                            out=pslot[vc], lhsT=val_db[p][:, vc * 128:(vc + 1) * 128],
                            rhs=selrep[:, ei, :],
                            start=(ei == 0), stop=(ei == 15)),
                          reads=[d_valdb[p], d_selrep[ei]], writes=[d_slot[vc]])
            return go

        def p_evac():
            S.tag = "ffn_evac"
            for vc in range(8):
                tr = sm[2 + vc % 2]
                dtr = d_sm[2 + vc % 2]
                E("act", lambda e, vc=vc, scale=scale, tr=tr: e.activation(
                    out=tr[:], in_=pslot[vc], func=AF.Copy, scale=scale),
                  reads=[d_slot[vc]], writes=[dtr])
                E("pool", lambda e, vc=vc, xb=xb, tr=tr: e.tensor_tensor(
                    out=xb[:, vc, :], in0=xb[:, vc, :], in1=tr[:], op=OP.add),
                  reads=[dtr, dx], writes=[dx])

        return ([(0, p_sel)] + [(0, mk_p1(e0)) for e0 in range(0, 16, 2)]
                + [(0, mk_p2(e0)) for e0 in range(0, 16, 2)] + [(0, p_evac)])

    def win_pieces(c, half):
        cip = c % 2
        xnc = xn2[c % 2]
        d_xnc = d_xn2[c % 2]

        def mk(h, q, inline_ev=False):
            def mms():
                for kc in range(NFC):
                    p = (q * NFC + kc) % 3
                    cols = h * 2048 + q * 1024
                    E("sq", lambda e, kc=kc, p=p, cols=cols: e.dma_start(
                        out=win_db[p][:], in_=win_bf[kc * 128:(kc + 1) * 128, cols:cols + 1024]),
                      writes=[d_windb[p]], dma=f"win{p}", extra_waits=SCR_TOTALS)
                    for m in range(8):
                        E("pe", lambda e, kc=kc, m=m, p=p: e.matmul(
                            out=pslot[m], lhsT=win_db[p][:, m * 128:(m + 1) * 128], rhs=xnc[:, kc, :],
                            start=(kc == 0), stop=(kc == NFC - 1)),
                          reads=[d_windb[p], d_xnc], writes=[d_slot[m]])
                if inline_ev:
                    evs()

            def evs():
                for m in range(8):
                    mc = q * 8 + m
                    if h == 0:
                        E("act", lambda e, m=m, mc=mc: e.activation(
                            out=x_inb[:, mc % 8, 3:3 + T], in_=pslot[m], func=AF.Copy),
                          reads=[d_slot[m]], writes=[d_xin8[mc % 8]])
                    else:
                        E("act", lambda e, m=m, mc=mc, cip=cip: e.activation(
                            out=zB[:, mc, cip * T:(cip + 1) * T], in_=pslot[m], func=AF.Silu),
                          reads=[d_slot[m]], writes=[d_z[mc]])
            return mms, evs
        return mk

    def conv_pieces(c):
        cip = c % 2
        par = (c // 2) % 2

        def mk(d0):
            def go():
                s0 = d0 % 8
                xds = [d_xin8[(d0 + i) % 8] for i in range(4)]
                E("act", lambda e, d0=d0, s0=s0: e.activation(
                    out=x_inb[:, s0:s0 + 4, 0:3], in_=tails[:, d0:d0 + 4, :], func=AF.Copy),
                  reads=[d_tails], writes=xds)
                for dc in range(d0, d0 + 4):
                    s8 = dc % 8
                    p = (dc // 2) % 2
                    if dc % 2 == 0:
                        E("sq", lambda e, dc=dc, p=p: e.dma_start(
                            out=key_db[p][:].rearrange("q (a b) -> q a b", a=2),
                            in_=convd_bf[dc:dc + 2].rearrange("a q b -> q a b")),
                          writes=[d_keydb[p]], dma=f"key{p}", extra_waits=SCR_TOTALS)
                    base = (dc % 2) * 512
                    sl = 6 + (dc % 2)
                    for k in range(4):
                        E("pe", lambda e, dc=dc, k=k, p=p, sl=sl, s8=s8, base=base: e.matmul(
                            out=pslot[sl], lhsT=key_db[p][:, base + k * 128:base + (k + 1) * 128],
                            rhs=x_inb[:, s8, k:k + T],
                            start=(k == 0), stop=(k == 3)),
                          reads=[d_keydb[p], d_xin8[s8]], writes=[d_slot[sl]])
                    udst = (uB[:, dc, cip * T:(cip + 1) * T] if dc < 14
                            else uX[par][:, dc - 14, cip * T:(cip + 1) * T])
                    E("act", lambda e, dc=dc, sl=sl, udst=udst: e.activation(
                        out=udst, in_=pslot[sl], func=AF.Silu,
                        bias=cb_sb[:, dc:dc + 1]),
                      reads=[d_slot[sl], d_cb], writes=[d_u[dc]])
                E("act", lambda e, d0=d0, s0=s0: e.activation(
                    out=tails[:, d0:d0 + 4, :], in_=x_inb[:, s0:s0 + 4, T:T + 3], func=AF.Copy),
                  reads=xds, writes=[d_tails])
            return go
        return [(d0 + 3, mk(d0)) for d0 in (0, 4, 8)] + [(13, mk(12))]

    def xproj_pieces(c):
        cip = c % 2

        def p_xproj():
            S.tag = "xproj"
            px = pslot[5]
            par = (c // 2) % 2
            for kc in range(NDC):
                usrc = (uB[:, kc, cip * T:(cip + 1) * T] if kc < 14
                        else uX[par][:, kc - 14, cip * T:(cip + 1) * T])
                E("pe", lambda e, kc=kc, px=px, usrc=usrc: e.matmul(
                    out=px, lhsT=xproj_sb[:, kc * 128:(kc + 1) * 128],
                    rhs=usrc,
                    start=(kc == 0), stop=(kc == NDC - 1)),
                  reads=[d_u[kc], d_xpw], writes=[d_slot[5]])
            E("act", lambda e, px=px: e.activation(out=dtlr_sb[0:64, :], in_=px[64:128, :], func=AF.Copy),
              reads=[d_slot[5]], writes=[d_dtlr])
            E("act", lambda e, px=px: e.activation(out=Bsb[:], in_=px[0:16, :], func=AF.Copy),
              reads=[d_slot[5]], writes=[d_Bsb])
            E("act", lambda e, px=px: e.activation(out=bcs_sb[32:48, :], in_=px[32:48, :], func=AF.Copy),
              reads=[d_slot[5]], writes=[d_Csb])

        def p_bounce_out():
            q = c % 2
            E("pool", lambda e, q=q: e.dma_start(out=bscr[q], in_=Bsb[:]),
              reads=[d_Bsb], writes=[d_bscr[q]], dma=f"rbo{q}")
            E("pool", lambda e, q=q: e.dma_start(out=cscr[q], in_=Csb[:]),
              reads=[d_Csb], writes=[d_cscr[q]], dma=f"rco{q}")

        def p_bcastB():
            q = c % 2
            E("pool", lambda e, q=q, cip=cip: e.dma_start(
                out=BrepB[:, :, cip * T:(cip + 1) * T],
                in_=bscr[q].unsqueeze(0).broadcast_to([128, 16, T])),
              reads=[d_bscr[q]], writes=[d_Brep], dma=f"rbi{q}")

        def p_bcastC():
            q = c % 2
            E("pool", lambda e, q=q, cip=cip: e.dma_start(
                out=CrepB[:, :, cip * T:(cip + 1) * T],
                in_=cscr[q].unsqueeze(0).broadcast_to([128, 16, T])),
              reads=[d_cscr[q]], writes=[d_Crep], dma=f"rci{q}")

        pend = []

        def emit_dt_stt():
            mc, bk, lvd = pend.pop(0)
            E("dve", lambda e, mc=mc, cip=cip, bk=bk, lvd=lvd: e.scalar_tensor_tensor(
                out=deltaB[:, mc:mc + 2, cip * T:(cip + 1) * T],
                in0=pbank[bk][:, 0:2 * T], scalar=0.0, in1=smT[:, lvd:lvd + 2, :],
                op0=OP.max, op1=OP.add),
              reads=[d_slot[bk], d_sm[lvd], d_sm[lvd + 1], d_const],
              writes=[d_delta[mc], d_delta[mc + 1]])

        def mk_dt(m0):
            def go():
                S.tag = "dt"
                for mc in range(m0, m0 + 4, 2):
                    if len(pend) >= 2:
                        emit_dt_stt()
                    pr = (mc // 2) % 2
                    bk = 6 + pr
                    lvd = 2 * pr
                    full = pbank[bk][:, 0:2 * T]
                    lv = smT[:, lvd:lvd + 2, :].rearrange("p a b -> p (a b)")
                    E("pe", lambda e, mc=mc, bk=bk: e.matmul(
                        out=pbank[bk][:, 0:T], lhsT=wdt_sb[:, mc * 128:(mc + 1) * 128], rhs=dtlr_sb[:],
                        start=True, stop=True),
                      reads=[d_dtlr, d_wdtw], writes=[d_slot[bk]])
                    E("pe", lambda e, mc=mc, bk=bk: e.matmul(
                        out=pbank[bk][:, T:2 * T], lhsT=wdt_sb[:, (mc + 1) * 128:(mc + 2) * 128], rhs=dtlr_sb[:],
                        start=True, stop=True),
                      reads=[d_dtlr, d_wdtw], writes=[d_slot[bk]])
                    E("act", lambda e, bk=bk, lv=lv, full=full: e.activation(out=lv, in_=full, func=AF.Abs),
                      reads=[d_slot[bk]], writes=[d_sm[lvd], d_sm[lvd + 1]])
                    E("act", lambda e, lv=lv: e.activation(out=lv, in_=lv, func=AF.Exp, scale=-1.0),
                      writes=[d_sm[lvd], d_sm[lvd + 1]])
                    E("act", lambda e, lv=lv: e.activation(out=lv, in_=lv, func=AF.Ln, bias=1.0),
                      writes=[d_sm[lvd], d_sm[lvd + 1]])
                    pend.append((mc, bk, lvd))
                if m0 + 4 >= NDC - 4:
                    while pend:
                        emit_dt_stt()
            return go
        return ([(13, p_xproj), (13, p_bounce_out)]
                + [(13, mk_dt(m0)) for m0 in (0, 4, 8)] + [(15, mk_dt(12))]
                + [(15, p_bcastB), (15, p_bcastC)])

    def scan_dc_q(P, dc, qtr):
        S.tag = "scan"
        usrc = uB[:, dc, :] if dc < 14 else uX[P % 2][:, dc - 14, :]
        if qtr == 0:
            E("dve", lambda e, dc=dc, usrc=usrc: e.tensor_tensor(
                out=du_s[0][:], in0=deltaB[:, dc, :], in1=usrc, op=OP.mult),
              reads=[d_delta[dc], d_u[dc]], writes=[d_du[0]])
            E("dve", lambda e, dc=dc: e.tensor_tensor(
                out=Hbig[:], in0=du_s[0][:].unsqueeze(1).broadcast_to([128, NS, TS]),
                in1=BrepB[:], op=OP.mult),
              reads=[d_du[0], d_Brep], writes=[d_H])
        for n in range(4 * qtr, 4 * qtr + 4):
            ap_ = dA_s[n % 8]
            E("dve", lambda e, dc=dc, n=n, ap_=ap_: e.tensor_tensor_scan(
                out=Hbig[:, n, :], data0=ap_[:], data1=Hbig[:, n, :],
                initial=hcar[:, dc * NS + n:dc * NS + n + 1], op0=OP.mult, op1=OP.add),
              reads=[d_dA[n % 8], d_hcar], writes=[d_H])
        if qtr == 3:
            E("dve", lambda e, dc=dc: e.tensor_copy(
                out=hcar[:, dc * NS:(dc + 1) * NS], in_=Hbig[:, :, TS - 1:TS].squeeze(2)),
              reads=[d_H], writes=[d_hcar])
            E("dve", lambda e: e.tensor_tensor(out=Hbig[:], in0=Hbig[:], in1=CrepB[:], op=OP.mult),
              reads=[d_H, d_Crep], writes=[d_H])
            E("dve", lambda e: e.tensor_tensor(out=Hbig[:, 0:8, :], in0=Hbig[:, 0:8, :],
                                               in1=Hbig[:, 8:16, :], op=OP.add),
              reads=[d_H], writes=[d_H])
            E("dve", lambda e: e.tensor_tensor(out=Hbig[:, 0:4, :], in0=Hbig[:, 0:4, :],
                                               in1=Hbig[:, 4:8, :], op=OP.add),
              reads=[d_H], writes=[d_H])
            E("dve", lambda e: e.tensor_tensor(out=Hbig[:, 0:2, :], in0=Hbig[:, 0:2, :],
                                               in1=Hbig[:, 2:4, :], op=OP.add),
              reads=[d_H], writes=[d_H])
            E("dve", lambda e: e.tensor_tensor(out=y_s[:], in0=Hbig[:, 0, :],
                                               in1=Hbig[:, 1, :], op=OP.add),
              reads=[d_H], writes=[d_du[0]])
            E("dve", lambda e, dc=dc, usrc=usrc: e.scalar_tensor_tensor(
                out=y_s[:], in0=usrc, scalar=dsk_sb[:, dc:dc + 1], in1=y_s[:],
                op0=OP.mult, op1=OP.add),
              reads=[d_u[dc], d_dsk, d_du[0]], writes=[d_du[0]])
            E("dve", lambda e, dc=dc: e.tensor_tensor(
                out=ygB[:, dc, :], in0=y_s[:], in1=zB[:, dc, :], op=OP.mult),
              reads=[d_du[0], d_z[dc]], writes=[d_yg[dc]])
        for j in range(4):
            nn = 4 * qtr + 8 + j
            gg = P * NDC + dc + nn // NS
            n2 = nn % NS
            if gg >= NPAIR * NDC:
                continue
            d2 = gg % NDC
            ap_ = dA_s[n2 % 8]
            E("act", lambda e, d2=d2, n2=n2, ap_=ap_: e.activation(
                out=ap_[:], in_=deltaB[:, d2, :], func=AF.Exp,
                scale=A_sc[:, d2 * NS + n2:d2 * NS + n2 + 1]),
              reads=[d_delta[d2], d_Asc], writes=[d_dA[n2 % 8]])

    def wout_pieces(P):
        mm = []

        def mk_dc(dc):
            def go():
                S.tag = "wout"
                p = dc % 3
                E("sq", lambda e, dc=dc, p=p: e.dma_start(
                    out=win_db[p][:], in_=wout_bf[dc * 128:(dc + 1) * 128, :]),
                  writes=[d_windb[p]], dma=f"win{p}", extra_waits=SCR_TOTALS)
                for vc in range(8):
                    E("pe", lambda e, dc=dc, vc=vc, p=p: e.matmul(
                        out=pbank[vc][:, 0:512], lhsT=win_db[p][:, vc * 128:(vc + 1) * 128],
                        rhs=ygB[:, dc, :],
                        start=(dc == 0), stop=(dc == NDC - 1)),
                      reads=[d_windb[p], d_yg[dc]], writes=[d_slot[vc]])
            return go

        def mk_evac(h):
            def go():
                S.tag = "wout"
                c = 2 * P + h
                xb = xB[c % 4]
                dx = d_x[c % 4]
                for vc in range(8):
                    tr = sm[2 + vc % 2]
                    dtr = d_sm[2 + vc % 2]
                    E("act", lambda e, vc=vc, h=h, tr=tr: e.activation(
                        out=tr[:], in_=pbank[vc][:, h * T:(h + 1) * T], func=AF.Copy),
                      reads=[d_slot[vc]], writes=[dtr])
                    E("pool", lambda e, vc=vc, xb=xb, dx=dx, tr=tr: e.tensor_tensor(
                        out=xb[:, vc, :], in0=xb[:, vc, :], in1=tr[:], op=OP.add),
                      reads=[dtr, dx], writes=[dx])
            return go

        for dc in range(NDC):
            mm.append((0, mk_dc(dc)))
        ev = [(0, mk_evac(0)), (0, mk_evac(1))]
        return mm, ev

    def front_pieces(c):
        def p_dma():
            dma_x_in(c)

        def p_ln():
            S.tag = "ln"
            ln(c)

        def p_ln2():
            ln(c)
        wmk = win_pieces(c, 0)
        cps = conv_pieces(c)
        m00, e00 = wmk(0, 0)
        m01, e01 = wmk(0, 1)
        m10, e10 = wmk(1, 0)
        m11, _ = wmk(1, 1, inline_ev=True)
        kx = (lambda k: k) if c % 2 == 0 else (lambda k: 13)
        return ([(0, p_dma), (0, p_ln)] + ffn_pieces(c, 0.5) + [(0, p_ln2)]
                + [(kx(1), m00), (kx(2), e00), (kx(cps[0][0]), cps[0][1]), (kx(cps[1][0]), cps[1][1]),
                   (kx(7), m01), (kx(8), e01), (kx(cps[2][0]), cps[2][1]), (kx(cps[3][0]), cps[3][1]),
                   (8 if c % 2 == 0 else 10, m10), (10, e10), (15, m11)]
                + xproj_pieces(c))

    def tail_pieces(c):
        def p_ln():
            S.tag = "ln"
            ln(c)

        def p_out():
            dma_x_out(c)
        return [(0, p_ln)] + ffn_pieces(c, 0.5) + [(0, p_out)]

    for c in (0, 1):
        for _mdc, pc in front_pieces(c):
            pc()
    S.tag = "scan"
    for n in range(8):
        E("act", lambda e, n=n, ap_=dA_s[n % 8]: e.activation(
            out=ap_[:], in_=deltaB[:, 0, :], func=AF.Exp,
            scale=A_sc[:, n:n + 1]),
          reads=[d_delta[0], d_Asc], writes=[d_dA[n % 8]])
    for P in range(NPAIR):
        pieces = []
        wparts = {}
        if P >= 1:
            mm, ev = wout_pieces(P - 1)
            wparts[(0, 0)] = mm
            wparts[(0, 2)] = [ev[0]]
            wparts[(0, 3)] = [ev[1]]
            pieces += tail_pieces(2 * P - 2) + tail_pieces(2 * P - 1)
        if P + 1 <= NPAIR - 1:
            pieces += front_pieces(2 * P + 2) + front_pieces(2 * P + 3)
        pieces.sort(key=lambda t: t[0])
        n = len(pieces)
        k = (n + 4 * NDC - 1) // (4 * NDC)
        j = 0
        for dc in range(NDC):
            for qtr in range(4):
                scan_dc_q(P, dc, qtr)
                for _m, pc in wparts.pop((dc, qtr), ()):
                    pc()
                budget = k
                while j < n and (pieces[j][0] < dc or (pieces[j][0] == dc and qtr == 3)) and (budget > 0 or (dc == NDC - 1 and qtr == 3)):
                    pieces[j][1]()
                    j += 1
                    budget -= 1
        while j < n:
            pieces[j][1]()
            j += 1
    mm, ev = wout_pieces(NPAIR - 1)
    for _mdc, pc in mm + ev:
        pc()
    for c in (2 * NPAIR - 2, 2 * NPAIR - 1):
        for _mdc, pc in tail_pieces(c):
            pc()

    final_waits = [(s, S.cnt.get(s, 0)) for s in ("xout0", "xout1", "xout2", "xout3")
                   if S.cnt.get(s, 0) > 0]

    with nc.Block() as block:
        def make(eng_name):
            def body(eng):
                for fn, waits, sem_name, _tag in S.ops[eng_name]:
                    for (e2, c2) in waits:
                        eng.wait_ge(sems[e2], c2)
                    inst = fn(eng)
                    inst.then_inc(sems[sem_name], 16 if sem_name in DMA_SEMS else 1)
                if eng_name == "sq":
                    for (s, c2) in final_waits:
                        eng.wait_ge(sems[s], c2)
            return body
        block.tensor(make("pe"))
        block.scalar(make("act"))
        block.vector(make("dve"))
        block.gpsimd(make("pool"))
        block.sync(make("sq"))

    es.close()
    return nc


def _host_inputs(inputs, L=4096):
    W_in = np.ascontiguousarray(inputs["L0_attn_W_in"], dtype=np.float32)
    conv_w = np.asarray(inputs["L0_attn_conv_w"], dtype=np.float32)[:, 0, :]
    conv_b = np.asarray(inputs["L0_attn_conv_b"], dtype=np.float32)
    W_xp = np.asarray(inputs["L0_attn_W_xproj"], dtype=np.float32)
    W_dt0 = np.asarray(inputs["L0_attn_W_dt"], dtype=np.float32)
    b_dt0 = np.asarray(inputs["L0_attn_b_dt"], dtype=np.float32)
    W_dt = np.ascontiguousarray(np.concatenate([W_dt0, b_dt0[None, :]], axis=0))
    b_dt = np.asarray(inputs["L0_attn_b_dt"], dtype=np.float32)
    A_log = np.asarray(inputs["L0_attn_A_log"], dtype=np.float32)
    D_skip = np.asarray(inputs["L0_attn_D_skip"], dtype=np.float32)
    W_out = np.ascontiguousarray(inputs["L0_attn_W_out"], dtype=np.float32)
    selw = np.asarray(inputs["L0_ffn_expert_sel"], dtype=np.float32)
    keys = np.asarray(inputs["L0_ffn_keys"], dtype=np.float32)
    values = np.asarray(inputs["L0_ffn_values"], dtype=np.float32)

    keys_r = np.ascontiguousarray(
        keys.reshape(16, NFC, 128, 128).transpose(0, 2, 1, 3).reshape(16, 128, 1024))
    vals_r = np.ascontiguousarray(values)
    xproj_r = np.zeros((DI, 128), np.float32)
    xproj_r[:, 0:16] = W_xp[:, 64:80]
    xproj_r[:, 32:48] = W_xp[:, 80:96]
    xproj_r[:, 64:128] = W_xp[:, 0:64]
    selw_r = np.ascontiguousarray(selw.T)

    def di_layout(v):
        return np.ascontiguousarray(v.reshape(NDC, 128).T)
    alog_r = np.ascontiguousarray(
        A_log.reshape(NDC, 128, NS).transpose(1, 0, 2).reshape(128, NDC * NS))
    convd_r = np.zeros((NDC, 128, 4, 128), np.float32)
    idx = np.arange(128)
    for dc in range(NDC):
        for k in range(4):
            convd_r[dc, idx, k, idx] = conv_w[dc * 128 + idx, k]
    convd_r = np.ascontiguousarray(convd_r.reshape(NDC, 128, 512))
    shared = dict(
        w_in=W_in, keys_r=keys_r, vals_r=vals_r, wout_r=W_out,
        convd_r=convd_r, xproj_r=xproj_r, wdt_r=W_dt, selw_r=selw_r, alog_r=alog_r,
        dsk_r=di_layout(D_skip), cb_r=di_layout(conv_b), bdt_r=di_layout(b_dt),
    )
    x = np.asarray(inputs["L0_attn_x"], dtype=np.float32)
    xs = [np.ascontiguousarray(x[b, :L, :].T) for b in range(x.shape[0])]
    return shared, xs


def kernel(**inputs):
    L = 4096
    shared, xs = _host_inputs(inputs, L)
    nc = build_program(L)
    in_maps = [dict(shared, x_f=xs[b]) for b in range(8)]
    res = run_bass_kernel_spmd(nc, in_maps, core_ids=list(range(8)))
    out = np.empty((8, L, D), np.float32)
    for b in range(8):
        out[b] = np.asarray(res.results[b]["out_f"], dtype=np.float32).T
    return out


# version 57 (final)
# speedup vs baseline: 1.1491x; 1.1238x over previous
import contextlib
import numpy as np
import concourse.bass as bass
import concourse.mybir as mybir
from concourse.bass_utils import run_bass_kernel_spmd

F32 = mybir.dt.float32
F16 = mybir.dt.float16
BF16 = mybir.dt.bfloat16
AF = mybir.ActivationFunctionType
OP = mybir.AluOpType

D = 1024
DI = 2048
NS = 16
DTR = 64
T = 256
TS = 512
NFC = D // 128
NDC = DI // 128
EPS = 1e-5

ENGS = ("pe", "act", "dve", "pool", "sq")
DMA_SEMS = ("xin0", "xin1", "xin2", "xin3", "xout0", "xout1", "xout2", "xout3",
            "win0", "win1", "win2", "key0", "key1", "key2", "val0", "val1", "val2",
            "stgi0", "stgi1", "scr0", "scr1", "dir",
            "rso0", "rso1", "rsi0", "rsi1", "rbo0", "rbo1", "rbi0", "rbi1",
            "rco0", "rco1", "rci0", "rci1")


class Dep:
    __slots__ = ("w", "r")

    def __init__(self):
        self.w = None
        self.r = []


class Sched:

    def __init__(self):
        self.ops = {e: [] for e in ENGS}
        self.cnt = {}
        self.waited = {e: {} for e in ENGS}
        self.tag = "init"

    def emit(self, eng, fn, reads=(), writes=(), dma=None, extra_waits=()):
        need = {}

        def add(e, c):
            if c > need.get(e, 0):
                need[e] = c

        for d in reads:
            if d.w:
                add(*d.w)
        for d in writes:
            if d.w:
                add(*d.w)
            for (e, c) in d.r:
                add(e, c)
        for (e, c) in extra_waits:
            add(e, c)
        waits = []
        w = self.waited[eng]
        for e, c in need.items():
            if e == eng:
                if eng != "pool":
                    continue
                c = min(c, self.cnt.get(eng, 0))
            if c > w.get(e, 0):
                waits.append((e, c))
                w[e] = c
        if dma is not None:
            self.cnt[dma] = self.cnt.get(dma, 0) + 16
            me = (dma, self.cnt[dma])
            self.ops[eng].append((fn, waits, dma, self.tag))
        else:
            self.cnt[eng] = self.cnt.get(eng, 0) + 1
            me = (eng, self.cnt[eng])
            self.ops[eng].append((fn, waits, eng, self.tag))
        for d in writes:
            d.w = me
            d.r = []
        for d in reads:
            d.r.append(me)


def build_program(L=4096):
    NPAIR = L // TS
    nc = bass.Bass("TRN2", target_bir_lowering=False, debug=False)

    x_in_d = nc.dram_tensor("x_f", [D, L], F32, kind="ExternalInput").ap()
    out_d = nc.dram_tensor("out_f", [D, L], F32, kind="ExternalOutput").ap()
    w_in_d = nc.dram_tensor("w_in", [D, 2 * DI], F32, kind="ExternalInput").ap()
    keys_d = nc.dram_tensor("keys_r", [16, 128, 1024], F32, kind="ExternalInput").ap()
    vals_d = nc.dram_tensor("vals_r", [16, 128, 1024], F32, kind="ExternalInput").ap()
    wout_d = nc.dram_tensor("wout_r", [DI, D], F32, kind="ExternalInput").ap()
    convd_d = nc.dram_tensor("convd_r", [NDC, 128, 512], F32, kind="ExternalInput").ap()
    xproj_d = nc.dram_tensor("xproj_r", [DI, 128], F32, kind="ExternalInput").ap()
    wdt_d = nc.dram_tensor("wdt_r", [DTR + 1, DI], F32, kind="ExternalInput").ap()
    selw_d = nc.dram_tensor("selw_r", [D, 16], F32, kind="ExternalInput").ap()
    alog_d = nc.dram_tensor("alog_r", [128, 256], F32, kind="ExternalInput").ap()
    dsk_d = nc.dram_tensor("dsk_r", [128, 16], F32, kind="ExternalInput").ap()
    cb_d = nc.dram_tensor("cb_r", [128, 16], F32, kind="ExternalInput").ap()
    bdt_d = nc.dram_tensor("bdt_r", [128, 16], F32, kind="ExternalInput").ap()

    win_bf = nc.dram_tensor("win_bf", [D, 2 * DI], BF16, kind="Internal").ap()
    keys_bf = nc.dram_tensor("keys_bf", [16, 128, 1024], BF16, kind="Internal").ap()
    vals_bf = nc.dram_tensor("vals_bf", [16, 128, 1024], BF16, kind="Internal").ap()
    wout_bf = nc.dram_tensor("wout_bf", [DI, D], BF16, kind="Internal").ap()
    convd_bf = nc.dram_tensor("convd_bf", [NDC, 128, 512], BF16, kind="Internal").ap()
    selscr = nc.dram_tensor("selscr", [2, 16, T], BF16, kind="Internal").ap()
    bscr = nc.dram_tensor("bscr", [2, 16, T], BF16, kind="Internal").ap()
    cscr = nc.dram_tensor("cscr", [2, 16, T], BF16, kind="Internal").ap()

    es = contextlib.ExitStack()

    def sb(name, shape, dt):
        return es.enter_context(nc.sbuf_tensor(name, shape, dt))

    xB = [sb(f"x{i}", [128, NFC, T], F32) for i in range(4)]
    xn2 = [sb(f"xn{i}", [128, NFC, T], BF16) for i in range(2)]
    x_inb = sb("x_inb", [128, 8, T + 3], BF16)
    tails = sb("tails", [128, NDC, 3], BF16)
    uB = sb("uB", [128, NDC, TS], BF16)
    zB = sb("zB", [128, NDC, TS], BF16)
    deltaB = sb("deltaB", [128, NDC, TS], BF16)
    ygB = sb("ygB", [128, NDC, TS], BF16)
    BrepB = sb("BrepB", [128, NS, TS], BF16)
    CrepB = sb("CrepB", [128, NS, TS], BF16)
    Hbig = sb("Hbig", [128, NS, TS], BF16)
    dA_s = [sb(f"dA{i}", [128, TS], F16) for i in range(8)]
    du_s = [sb("du0", [128, TS], BF16)]
    y_s = du_s[0]
    uX = [sb(f"uX{i}", [128, 2, TS], BF16) for i in range(2)]
    hcar = sb("hcar", [128, 256], BF16)
    A_sc = sb("A_sc", [128, 256], F32)
    dsk_sb = sb("dsk_sb", [128, 16], F32)
    cb_sb = sb("cb_sb", [128, 16], F32)
    smT = sb("smT", [128, 4, T], F32)
    sm = [smT[:, i, :] for i in range(4)]
    tres = sm[2]
    selrep = sb("selrep", [128, 16, T], BF16)
    xb16 = selrep[:, 0:2 * NFC:2, :]
    x2b = selrep[:, 1:2 * NFC:2, :]

    scr_sb = [sb(f"scr{i}", [128, T], BF16) for i in range(2)]
    dtlr_sb = sb("dtlr_sb", [DTR + 1, T], BF16)
    bcs_sb = sb("bcs_sb", [80, T], BF16)
    Bsb = bcs_sb[0:16, :]
    Csb = bcs_sb[32:48, :]
    sel_sb = bcs_sb[64:80, :]
    ones_bf = sb("ones_bf", [128, 128], BF16)
    cpool = sb("cpool", [128, 8], F32)
    xproj_sb = sb("xproj_sb", [128, 16 * 128], BF16)
    wdt_sb = sb("wdt_sb", [DTR + 1, DI], BF16)
    selw_sb = sb("selw_sb", [128, NFC * 16], BF16)
    win_db = [sb(f"windb{i}", [128, 1024], BF16) for i in range(3)]
    key_db = [sb(f"keydb{i}", [128, 1024], BF16) for i in range(2)]
    val_db = [sb(f"valdb{i}", [128, 1024], BF16) for i in range(2)]
    stg32 = [ygB[:, 0:2, :].bitcast(F32).rearrange("p a b -> p (a b)"),
             ygB[:, 2:4, :].bitcast(F32).rearrange("p a b -> p (a b)")]
    stgb = [Hbig[:, 0, :], CrepB[:, 0, :]]

    pbank = [es.enter_context(nc.psum_tensor(f"pb{i}", [128, 512], F32)) for i in range(8)]
    pslot = [pbank[i][:, 0:T] for i in range(8)]

    CONSTS = [EPS, 1.0, -1.0, -0.5, 1.0 / D, 0.5, 0.0, 2.0]
    for ci, cv in enumerate(CONSTS):
        nc.const_aps.register(F32, cv, cpool[:, ci:ci + 1])

    sems = {e: es.enter_context(nc.semaphore(f"sem_{e}")) for e in ENGS}
    for s in DMA_SEMS:
        sems[s] = es.enter_context(nc.semaphore(f"sem_{s}"))

    S = Sched()
    E = S.emit

    d_x = [Dep() for _ in range(4)]
    d_xn2 = [Dep(), Dep()]
    d_xin_body = Dep(); d_xin_tail = Dep()
    d_u = [Dep() for _ in range(NDC)]
    d_z = [Dep() for _ in range(NDC)]
    d_delta = [Dep() for _ in range(NDC)]
    d_yg = [Dep() for _ in range(NDC)]
    d_Brep = Dep(); d_Crep = Dep(); d_H = Dep()
    d_dA = [Dep() for _ in range(8)]
    d_du = [Dep()]
    d_hcar = Dep()
    d_sm = [Dep() for _ in range(4)]
    d_tres = d_sm[2]
    d_selrep = [Dep() for _ in range(16)]; d_selsb = Dep()
    d_scr = [Dep() for _ in range(2)]
    d_dtlr = Dep(); d_Bsb = Dep(); d_Csb = Dep()
    d_slot = [Dep() for _ in range(8)]
    d_windb = [Dep() for _ in range(3)]
    d_keydb = [Dep() for _ in range(2)]
    d_valdb = [Dep() for _ in range(2)]
    d_xin8 = [Dep() for _ in range(8)]
    d_tails = Dep()
    d_stg32 = [[d_yg[0], d_yg[1]], [d_yg[2], d_yg[3]]]
    d_stgb = [[d_H], [d_Crep]]
    d_const = Dep()
    d_dsk = Dep(); d_cb = Dep(); d_bdt = Dep(); d_Asc = Dep()
    d_xpw = Dep(); d_wdtw = Dep(); d_selww = Dep()
    d_selscr = [Dep(), Dep()]; d_bscr = [Dep(), Dep()]; d_cscr = [Dep(), Dep()]

    for ci, cv in enumerate(CONSTS):
        E("dve", lambda e, ci=ci, cv=cv: e.memset(cpool[:, ci:ci + 1], cv), writes=[d_const])
    E("dve", lambda e: e.memset(hcar[:], 0.0), writes=[d_hcar])
    E("dve", lambda e: e.memset(tails[:], 0.0), writes=[d_tails])
    E("dve", lambda e: e.memset(ones_bf[:], 1.0), writes=[d_const])
    E("dve", lambda e: e.memset(dtlr_sb[64:65, :], 1.0), writes=[d_const])
    for _e in ("pe", "act", "pool", "sq"):
        E(_e, lambda e: e.nop(), reads=[d_const])

    E("sq", lambda e: e.dma_start(out=dsk_sb[:], in_=dsk_d), writes=[d_dsk], dma="dir")
    E("sq", lambda e: e.dma_start(out=cb_sb[:], in_=cb_d), writes=[d_cb], dma="dir")
    E("sq", lambda e: e.dma_start(out=sm[0][:], in_=alog_d), writes=[d_sm[0]], dma="dir")
    dir_total = ("dir", S.cnt["dir"])
    for d in (d_dsk, d_cb, d_sm[0]):
        d.w = dir_total
    E("act", lambda e: e.activation(out=sm[1][:], in_=sm[0][:], func=AF.Exp),
      reads=[d_sm[0]], writes=[d_sm[1]])
    E("pool", lambda e: e.tensor_scalar_mul(out=A_sc[:], in0=sm[1][:], scalar1=-1.0),
      reads=[d_sm[1], d_const], writes=[d_Asc])

    rr = [0]

    def conv_round(src_ap, dst_sb=None, dst_sb_dep=None, dst_dram=None, nparts=128):
        s = rr[0] % 2
        rr[0] += 1
        wdt = src_ap.shape[-1]
        E("sq", lambda e, a=src_ap, s=s, p=nparts, w=wdt: e.dma_start(
            out=stg32[s][0:p, :w], in_=a),
          writes=d_stg32[s], dma=f"stgi{s}")
        if dst_sb is not None:
            E("dve", lambda e, d=dst_sb, s=s, p=nparts, w=wdt: e.tensor_copy(
                out=d, in_=stg32[s][0:p, :w]),
              reads=d_stg32[s], writes=[dst_sb_dep])
        else:
            E("dve", lambda e, s=s, p=nparts, w=wdt: e.tensor_copy(
                out=stgb[s][0:p, :w], in_=stg32[s][0:p, :w]),
              reads=d_stg32[s], writes=d_stgb[s])
            E("sq", lambda e, a=dst_dram, s=s, p=nparts, w=wdt: e.dma_start(
                out=a, in_=stgb[s][0:p, :w]),
              reads=d_stgb[s], dma=f"scr{s}")

    for kc in range(NFC):
        for j in range(8):
            conv_round(w_in_d[kc * 128:(kc + 1) * 128, j * 512:(j + 1) * 512],
                       dst_dram=win_bf[kc * 128:(kc + 1) * 128, j * 512:(j + 1) * 512])
    for ei in range(16):
        for j in range(2):
            conv_round(keys_d[ei, :, j * 512:(j + 1) * 512],
                       dst_dram=keys_bf[ei, :, j * 512:(j + 1) * 512])
    for ei in range(16):
        for j in range(2):
            conv_round(vals_d[ei, :, j * 512:(j + 1) * 512],
                       dst_dram=vals_bf[ei, :, j * 512:(j + 1) * 512])
    for kc in range(NDC):
        for j in range(2):
            conv_round(wout_d[kc * 128:(kc + 1) * 128, j * 512:(j + 1) * 512],
                       dst_dram=wout_bf[kc * 128:(kc + 1) * 128, j * 512:(j + 1) * 512])
    for dc in range(NDC):
        conv_round(convd_d[dc], dst_dram=convd_bf[dc])
    for kc in range(NDC):
        conv_round(xproj_d[kc * 128:(kc + 1) * 128, :],
                   dst_sb=xproj_sb[:, kc * 128:(kc + 1) * 128], dst_sb_dep=d_xpw)
    for j in range(4):
        conv_round(wdt_d[:, j * 512:(j + 1) * 512],
                   dst_sb=wdt_sb[:, j * 512:(j + 1) * 512], dst_sb_dep=d_wdtw, nparts=DTR + 1)
    for kc in range(NFC):
        conv_round(selw_d[kc * 128:(kc + 1) * 128, :],
                   dst_sb=selw_sb[:, kc * 16:(kc + 1) * 16], dst_sb_dep=d_selww)

    SCR_TOTALS = [("scr0", S.cnt.get("scr0", 0)), ("scr1", S.cnt.get("scr1", 0))]

    x_src = x_in_d.rearrange("(fc p) L -> p fc L", p=128)
    out_dst = out_d.rearrange("(fc p) L -> p fc L", p=128)

    def dma_x_in(c):
        xb = xB[c % 4]
        E("sq", lambda e, c=c, xb=xb: e.dma_start(out=xb[:], in_=x_src[:, :, c * T:(c + 1) * T]),
          writes=[d_x[c % 4]], dma=f"xin{c % 4}")

    def dma_x_out(c):
        xb = xB[c % 4]
        E("sq", lambda e, c=c, xb=xb: e.dma_start(out=out_dst[:, :, c * T:(c + 1) * T], in_=xb[:]),
          reads=[d_x[c % 4]], dma=f"xout{c % 4}")

    def ln(c):
        xb = xB[c % 4]
        dx = d_x[c % 4]
        E("pool", lambda e, xb=xb: e.tensor_copy(out=xb16, in_=xb[:]),
          reads=[dx], writes=d_selrep[0::2])
        E("act", lambda e, xb=xb: e.activation(out=x2b, in_=xb[:], func=AF.Square),
          reads=[dx], writes=d_selrep[1::2])
        psq = pbank[4][:, 0:512]
        for fc in range(NFC):
            E("pe", lambda e, fc=fc, psq=psq: e.matmul(
                out=psq, lhsT=ones_bf[:],
                rhs=selrep[:, 2 * fc:2 * fc + 2, :].rearrange("p a b -> p (a b)"),
                start=(fc == 0), stop=(fc == NFC - 1)),
              reads=[d_selrep[2 * fc], d_selrep[2 * fc + 1], d_const], writes=[d_slot[4]])
        E("act", lambda e, psq=psq: e.activation(out=sm[0][:], in_=psq[:, 0:T], func=AF.Copy, scale=1.0 / D),
          reads=[d_slot[4]], writes=[d_sm[0]])
        E("act", lambda e, psq=psq: e.activation(out=sm[1][:], in_=psq[:, T:2 * T], func=AF.Copy, scale=1.0 / D),
          reads=[d_slot[4]], writes=[d_sm[1]])
        E("pool", lambda e: e.tensor_tensor(out=sm[2][:], in0=sm[0][:], in1=sm[0][:], op=OP.mult),
          reads=[d_sm[0]], writes=[d_sm[2]])
        E("pool", lambda e: e.tensor_tensor(out=sm[1][:], in0=sm[1][:], in1=sm[2][:], op=OP.subtract),
          reads=[d_sm[2]], writes=[d_sm[1]])
        E("act", lambda e: e.activation(out=sm[1][:], in_=sm[1][:], func=AF.Ln, bias=EPS),
          writes=[d_sm[1]])
        E("act", lambda e: e.activation(out=sm[1][:], in_=sm[1][:], func=AF.Exp, scale=-0.5),
          writes=[d_sm[1]])
        E("pool", lambda e, xb=xb: e.tensor_tensor(
            out=xn2[c % 2][:], in0=xb[:],
            in1=sm[0][:].unsqueeze(1).broadcast_to([128, NFC, T]), op=OP.subtract),
          reads=[dx, d_sm[0]], writes=[d_xn2[c % 2]])
        E("pool", lambda e: e.tensor_tensor(
            out=xn2[c % 2][:], in0=xn2[c % 2][:],
            in1=sm[1][:].unsqueeze(1).broadcast_to([128, NFC, T]), op=OP.mult),
          reads=[d_sm[1]], writes=[d_xn2[c % 2]])

    def ffn_pieces(c, scale):
        xnc = xn2[c % 2]
        d_xnc = d_xn2[c % 2]
        xb = xB[c % 4]
        dx = d_x[c % 4]

        def p_sel():
            S.tag = "ffn_sel"
            psel = pslot[5]
            for kc in range(NFC):
                E("pe", lambda e, kc=kc, psel=psel: e.matmul(
                    out=psel[0:16, :], lhsT=selw_sb[:, kc * 16:(kc + 1) * 16], rhs=xnc[:, kc, :],
                    start=(kc == 0), stop=(kc == NFC - 1)),
                  reads=[d_xnc, d_selww], writes=[d_slot[5]])
            E("act", lambda e, psel=psel: e.activation(out=bcs_sb[64:80, :], in_=psel[0:16, :], func=AF.Sigmoid),
              reads=[d_slot[5]], writes=[d_selsb])
            q = c % 2
            E("pool", lambda e, q=q: e.dma_start(out=selscr[q], in_=sel_sb[:]),
              reads=[d_selsb], writes=[d_selscr[q]], dma=f"rso{q}")
            E("pool", lambda e, q=q: e.dma_start(
                out=selrep[:], in_=selscr[q].unsqueeze(0).broadcast_to([128, 16, T])),
              reads=[d_selscr[q]], writes=d_selrep, dma=f"rsi{q}")

        def mk_p1(e0):
            def go():
                S.tag = "ffn_p1"
                for ei in (e0, e0 + 1):
                    pk = ei % 2
                    p = ei % 2
                    sl = 6 + (ei % 2)
                    E("sq", lambda e, ei=ei, pk=pk: e.dma_start(out=key_db[pk][:], in_=keys_bf[ei]),
                      writes=[d_keydb[pk]], dma=f"key{pk}", extra_waits=SCR_TOTALS)
                    for kc in range(NFC):
                        E("pe", lambda e, kc=kc, sl=sl, pk=pk: e.matmul(
                            out=pslot[sl], lhsT=key_db[pk][:, kc * 128:(kc + 1) * 128], rhs=xnc[:, kc, :],
                            start=(kc == 0), stop=(kc == NFC - 1)),
                          reads=[d_keydb[pk], d_xnc], writes=[d_slot[sl]])
                    E("act", lambda e, sl=sl, p=p: e.activation(out=scr_sb[p][:], in_=pslot[sl], func=AF.Relu),
                      reads=[d_slot[sl]], writes=[d_scr[p]])
                    E("pool", lambda e, ei=ei, p=p: e.tensor_tensor(out=selrep[:, ei, :], in0=scr_sb[p][:],
                                                                    in1=selrep[:, ei, :], op=OP.mult),
                      reads=[d_scr[p]], writes=[d_selrep[ei]])
            return go

        def mk_p2(e0):
            def go():
                S.tag = "ffn_p2"
                for ei in (e0, e0 + 1):
                    p = ei % 2
                    E("sq", lambda e, ei=ei, p=p: e.dma_start(out=val_db[p][:], in_=vals_bf[ei]),
                      writes=[d_valdb[p]], dma=f"val{p}", extra_waits=SCR_TOTALS)
                    for vc in range(8):
                        E("pe", lambda e, ei=ei, vc=vc, p=p: e.matmul(
                            out=pslot[vc], lhsT=val_db[p][:, vc * 128:(vc + 1) * 128],
                            rhs=selrep[:, ei, :],
                            start=(ei == 0), stop=(ei == 15)),
                          reads=[d_valdb[p], d_selrep[ei]], writes=[d_slot[vc]])
            return go

        def p_evac():
            S.tag = "ffn_evac"
            for vc in range(8):
                tr = sm[2 + vc % 2]
                dtr = d_sm[2 + vc % 2]
                E("act", lambda e, vc=vc, scale=scale, tr=tr: e.activation(
                    out=tr[:], in_=pslot[vc], func=AF.Copy, scale=scale),
                  reads=[d_slot[vc]], writes=[dtr])
                E("pool", lambda e, vc=vc, xb=xb, tr=tr: e.tensor_tensor(
                    out=xb[:, vc, :], in0=xb[:, vc, :], in1=tr[:], op=OP.add),
                  reads=[dtr, dx], writes=[dx])

        return ([(0, p_sel)] + [(0, mk_p1(e0)) for e0 in range(0, 16, 2)]
                + [(0, mk_p2(e0)) for e0 in range(0, 16, 2)] + [(0, p_evac)])

    def win_pieces(c, half):
        cip = c % 2
        xnc = xn2[c % 2]
        d_xnc = d_xn2[c % 2]

        def mk(h, q, inline_ev=False):
            def mms():
                for kc in range(NFC):
                    p = (q * NFC + kc) % 3
                    cols = h * 2048 + q * 1024
                    E("sq", lambda e, kc=kc, p=p, cols=cols: e.dma_start(
                        out=win_db[p][:], in_=win_bf[kc * 128:(kc + 1) * 128, cols:cols + 1024]),
                      writes=[d_windb[p]], dma=f"win{p}", extra_waits=SCR_TOTALS)
                    for m in range(8):
                        E("pe", lambda e, kc=kc, m=m, p=p: e.matmul(
                            out=pslot[m], lhsT=win_db[p][:, m * 128:(m + 1) * 128], rhs=xnc[:, kc, :],
                            start=(kc == 0), stop=(kc == NFC - 1)),
                          reads=[d_windb[p], d_xnc], writes=[d_slot[m]])
                if inline_ev:
                    evs()

            def evs():
                for m in range(8):
                    mc = q * 8 + m
                    if h == 0:
                        E("act", lambda e, m=m, mc=mc: e.activation(
                            out=x_inb[:, mc % 8, 3:3 + T], in_=pslot[m], func=AF.Copy),
                          reads=[d_slot[m]], writes=[d_xin8[mc % 8]])
                    else:
                        E("act", lambda e, m=m, mc=mc, cip=cip: e.activation(
                            out=zB[:, mc, cip * T:(cip + 1) * T], in_=pslot[m], func=AF.Silu),
                          reads=[d_slot[m]], writes=[d_z[mc]])
            return mms, evs
        return mk

    def conv_pieces(c):
        cip = c % 2
        par = (c // 2) % 2

        def mk(d0):
            def go():
                s0 = d0 % 8
                xds = [d_xin8[(d0 + i) % 8] for i in range(4)]
                E("act", lambda e, d0=d0, s0=s0: e.activation(
                    out=x_inb[:, s0:s0 + 4, 0:3], in_=tails[:, d0:d0 + 4, :], func=AF.Copy),
                  reads=[d_tails], writes=xds)
                for dc in range(d0, d0 + 4):
                    s8 = dc % 8
                    p = dc % 2
                    E("sq", lambda e, dc=dc, p=p: e.dma_start(out=key_db[p][:, 0:512], in_=convd_bf[dc]),
                      writes=[d_keydb[p]], dma=f"key{p}", extra_waits=SCR_TOTALS)
                    sl = 6 + (dc % 2)
                    for k in range(4):
                        E("pe", lambda e, dc=dc, k=k, p=p, sl=sl, s8=s8: e.matmul(
                            out=pslot[sl], lhsT=key_db[p][:, k * 128:(k + 1) * 128],
                            rhs=x_inb[:, s8, k:k + T],
                            start=(k == 0), stop=(k == 3)),
                          reads=[d_keydb[p], d_xin8[s8]], writes=[d_slot[sl]])
                    udst = (uB[:, dc, cip * T:(cip + 1) * T] if dc < 14
                            else uX[par][:, dc - 14, cip * T:(cip + 1) * T])
                    E("act", lambda e, dc=dc, sl=sl, udst=udst: e.activation(
                        out=udst, in_=pslot[sl], func=AF.Silu,
                        bias=cb_sb[:, dc:dc + 1]),
                      reads=[d_slot[sl], d_cb], writes=[d_u[dc]])
                E("act", lambda e, d0=d0, s0=s0: e.activation(
                    out=tails[:, d0:d0 + 4, :], in_=x_inb[:, s0:s0 + 4, T:T + 3], func=AF.Copy),
                  reads=xds, writes=[d_tails])
            return go
        return [(d0 + 3, mk(d0)) for d0 in (0, 4, 8)] + [(13, mk(12))]

    def xproj_pieces(c):
        cip = c % 2

        def p_xproj():
            S.tag = "xproj"
            px = pslot[5]
            par = (c // 2) % 2
            for kc in range(NDC):
                usrc = (uB[:, kc, cip * T:(cip + 1) * T] if kc < 14
                        else uX[par][:, kc - 14, cip * T:(cip + 1) * T])
                E("pe", lambda e, kc=kc, px=px, usrc=usrc: e.matmul(
                    out=px, lhsT=xproj_sb[:, kc * 128:(kc + 1) * 128],
                    rhs=usrc,
                    start=(kc == 0), stop=(kc == NDC - 1)),
                  reads=[d_u[kc], d_xpw], writes=[d_slot[5]])
            E("act", lambda e, px=px: e.activation(out=dtlr_sb[0:64, :], in_=px[64:128, :], func=AF.Copy),
              reads=[d_slot[5]], writes=[d_dtlr])
            E("act", lambda e, px=px: e.activation(out=Bsb[:], in_=px[0:16, :], func=AF.Copy),
              reads=[d_slot[5]], writes=[d_Bsb])
            E("act", lambda e, px=px: e.activation(out=bcs_sb[32:48, :], in_=px[32:48, :], func=AF.Copy),
              reads=[d_slot[5]], writes=[d_Csb])

        def p_bounce_out():
            q = c % 2
            E("pool", lambda e, q=q: e.dma_start(out=bscr[q], in_=Bsb[:]),
              reads=[d_Bsb], writes=[d_bscr[q]], dma=f"rbo{q}")
            E("pool", lambda e, q=q: e.dma_start(out=cscr[q], in_=Csb[:]),
              reads=[d_Csb], writes=[d_cscr[q]], dma=f"rco{q}")

        def p_bcastB():
            q = c % 2
            E("pool", lambda e, q=q, cip=cip: e.dma_start(
                out=BrepB[:, :, cip * T:(cip + 1) * T],
                in_=bscr[q].unsqueeze(0).broadcast_to([128, 16, T])),
              reads=[d_bscr[q]], writes=[d_Brep], dma=f"rbi{q}")

        def p_bcastC():
            q = c % 2
            E("pool", lambda e, q=q, cip=cip: e.dma_start(
                out=CrepB[:, :, cip * T:(cip + 1) * T],
                in_=cscr[q].unsqueeze(0).broadcast_to([128, 16, T])),
              reads=[d_cscr[q]], writes=[d_Crep], dma=f"rci{q}")

        pend = []

        def emit_dt_stt():
            mc, bk, lvd = pend.pop(0)
            E("dve", lambda e, mc=mc, cip=cip, bk=bk, lvd=lvd: e.scalar_tensor_tensor(
                out=deltaB[:, mc:mc + 2, cip * T:(cip + 1) * T],
                in0=pbank[bk][:, 0:2 * T], scalar=0.0, in1=smT[:, lvd:lvd + 2, :],
                op0=OP.max, op1=OP.add),
              reads=[d_slot[bk], d_sm[lvd], d_sm[lvd + 1], d_const],
              writes=[d_delta[mc], d_delta[mc + 1]])

        def mk_dt(m0):
            def go():
                S.tag = "dt"
                for mc in range(m0, m0 + 4, 2):
                    if len(pend) >= 2:
                        emit_dt_stt()
                    pr = (mc // 2) % 2
                    bk = 6 + pr
                    lvd = 2 * pr
                    full = pbank[bk][:, 0:2 * T]
                    lv = smT[:, lvd:lvd + 2, :].rearrange("p a b -> p (a b)")
                    E("pe", lambda e, mc=mc, bk=bk: e.matmul(
                        out=pbank[bk][:, 0:T], lhsT=wdt_sb[:, mc * 128:(mc + 1) * 128], rhs=dtlr_sb[:],
                        start=True, stop=True),
                      reads=[d_dtlr, d_wdtw], writes=[d_slot[bk]])
                    E("pe", lambda e, mc=mc, bk=bk: e.matmul(
                        out=pbank[bk][:, T:2 * T], lhsT=wdt_sb[:, (mc + 1) * 128:(mc + 2) * 128], rhs=dtlr_sb[:],
                        start=True, stop=True),
                      reads=[d_dtlr, d_wdtw], writes=[d_slot[bk]])
                    E("act", lambda e, bk=bk, lv=lv, full=full: e.activation(out=lv, in_=full, func=AF.Abs),
                      reads=[d_slot[bk]], writes=[d_sm[lvd], d_sm[lvd + 1]])
                    E("act", lambda e, lv=lv: e.activation(out=lv, in_=lv, func=AF.Exp, scale=-1.0),
                      writes=[d_sm[lvd], d_sm[lvd + 1]])
                    E("act", lambda e, lv=lv: e.activation(out=lv, in_=lv, func=AF.Ln, bias=1.0),
                      writes=[d_sm[lvd], d_sm[lvd + 1]])
                    pend.append((mc, bk, lvd))
                if m0 + 4 >= NDC - 4:
                    while pend:
                        emit_dt_stt()
            return go
        return ([(13, p_xproj), (13, p_bounce_out)]
                + [(13, mk_dt(m0)) for m0 in (0, 4, 8)] + [(15, mk_dt(12))]
                + [(15, p_bcastB), (15, p_bcastC)])

    def scan_dc_q(P, dc, qtr):
        S.tag = "scan"
        usrc = uB[:, dc, :] if dc < 14 else uX[P % 2][:, dc - 14, :]
        if qtr == 0:
            E("dve", lambda e, dc=dc, usrc=usrc: e.tensor_tensor(
                out=du_s[0][:], in0=deltaB[:, dc, :], in1=usrc, op=OP.mult),
              reads=[d_delta[dc], d_u[dc]], writes=[d_du[0]])
            E("dve", lambda e, dc=dc: e.tensor_tensor(
                out=Hbig[:], in0=du_s[0][:].unsqueeze(1).broadcast_to([128, NS, TS]),
                in1=BrepB[:], op=OP.mult),
              reads=[d_du[0], d_Brep], writes=[d_H])
        for n in range(4 * qtr, 4 * qtr + 4):
            ap_ = dA_s[n % 8]
            E("dve", lambda e, dc=dc, n=n, ap_=ap_: e.tensor_tensor_scan(
                out=Hbig[:, n, :], data0=ap_[:], data1=Hbig[:, n, :],
                initial=hcar[:, dc * NS + n:dc * NS + n + 1], op0=OP.mult, op1=OP.add),
              reads=[d_dA[n % 8], d_hcar], writes=[d_H])
        if qtr == 3:
            E("dve", lambda e, dc=dc: e.tensor_copy(
                out=hcar[:, dc * NS:(dc + 1) * NS], in_=Hbig[:, :, TS - 1:TS].squeeze(2)),
              reads=[d_H], writes=[d_hcar])
            E("dve", lambda e: e.tensor_tensor(out=Hbig[:], in0=Hbig[:], in1=CrepB[:], op=OP.mult),
              reads=[d_H, d_Crep], writes=[d_H])
            E("dve", lambda e: e.tensor_tensor(out=Hbig[:, 0:8, :], in0=Hbig[:, 0:8, :],
                                               in1=Hbig[:, 8:16, :], op=OP.add),
              reads=[d_H], writes=[d_H])
            E("dve", lambda e: e.tensor_tensor(out=Hbig[:, 0:4, :], in0=Hbig[:, 0:4, :],
                                               in1=Hbig[:, 4:8, :], op=OP.add),
              reads=[d_H], writes=[d_H])
            E("dve", lambda e: e.tensor_tensor(out=Hbig[:, 0:2, :], in0=Hbig[:, 0:2, :],
                                               in1=Hbig[:, 2:4, :], op=OP.add),
              reads=[d_H], writes=[d_H])
            E("dve", lambda e: e.tensor_tensor(out=y_s[:], in0=Hbig[:, 0, :],
                                               in1=Hbig[:, 1, :], op=OP.add),
              reads=[d_H], writes=[d_du[0]])
            E("dve", lambda e, dc=dc, usrc=usrc: e.scalar_tensor_tensor(
                out=y_s[:], in0=usrc, scalar=dsk_sb[:, dc:dc + 1], in1=y_s[:],
                op0=OP.mult, op1=OP.add),
              reads=[d_u[dc], d_dsk, d_du[0]], writes=[d_du[0]])
            E("dve", lambda e, dc=dc: e.tensor_tensor(
                out=ygB[:, dc, :], in0=y_s[:], in1=zB[:, dc, :], op=OP.mult),
              reads=[d_du[0], d_z[dc]], writes=[d_yg[dc]])
        for j in range(4):
            nn = 4 * qtr + 8 + j
            gg = P * NDC + dc + nn // NS
            n2 = nn % NS
            if gg >= NPAIR * NDC:
                continue
            d2 = gg % NDC
            ap_ = dA_s[n2 % 8]
            E("act", lambda e, d2=d2, n2=n2, ap_=ap_: e.activation(
                out=ap_[:], in_=deltaB[:, d2, :], func=AF.Exp,
                scale=A_sc[:, d2 * NS + n2:d2 * NS + n2 + 1]),
              reads=[d_delta[d2], d_Asc], writes=[d_dA[n2 % 8]])

    def wout_pieces(P):
        mm = []

        def mk_dc(dc):
            def go():
                S.tag = "wout"
                p = dc % 3
                E("sq", lambda e, dc=dc, p=p: e.dma_start(
                    out=win_db[p][:], in_=wout_bf[dc * 128:(dc + 1) * 128, :]),
                  writes=[d_windb[p]], dma=f"win{p}", extra_waits=SCR_TOTALS)
                for vc in range(8):
                    E("pe", lambda e, dc=dc, vc=vc, p=p: e.matmul(
                        out=pbank[vc][:, 0:512], lhsT=win_db[p][:, vc * 128:(vc + 1) * 128],
                        rhs=ygB[:, dc, :],
                        start=(dc == 0), stop=(dc == NDC - 1)),
                      reads=[d_windb[p], d_yg[dc]], writes=[d_slot[vc]])
            return go

        def mk_evac(h):
            def go():
                S.tag = "wout"
                c = 2 * P + h
                xb = xB[c % 4]
                dx = d_x[c % 4]
                for vc in range(8):
                    tr = sm[2 + vc % 2]
                    dtr = d_sm[2 + vc % 2]
                    E("act", lambda e, vc=vc, h=h, tr=tr: e.activation(
                        out=tr[:], in_=pbank[vc][:, h * T:(h + 1) * T], func=AF.Copy),
                      reads=[d_slot[vc]], writes=[dtr])
                    E("pool", lambda e, vc=vc, xb=xb, dx=dx, tr=tr: e.tensor_tensor(
                        out=xb[:, vc, :], in0=xb[:, vc, :], in1=tr[:], op=OP.add),
                      reads=[dtr, dx], writes=[dx])
            return go

        for dc in range(NDC):
            mm.append((0, mk_dc(dc)))
        ev = [(0, mk_evac(0)), (0, mk_evac(1))]
        return mm, ev

    def front_pieces(c):
        def p_dma():
            dma_x_in(c)

        def p_ln():
            S.tag = "ln"
            ln(c)

        def p_ln2():
            ln(c)
        wmk = win_pieces(c, 0)
        cps = conv_pieces(c)
        m00, e00 = wmk(0, 0)
        m01, e01 = wmk(0, 1)
        m10, e10 = wmk(1, 0)
        m11, _ = wmk(1, 1, inline_ev=True)
        kx = (lambda k: k) if c % 2 == 0 else (lambda k: 13)
        return ([(0, p_dma), (0, p_ln)] + ffn_pieces(c, 0.5) + [(0, p_ln2)]
                + [(kx(1), m00), (kx(2), e00), (kx(cps[0][0]), cps[0][1]), (kx(cps[1][0]), cps[1][1]),
                   (kx(7), m01), (kx(8), e01), (kx(cps[2][0]), cps[2][1]), (kx(cps[3][0]), cps[3][1]),
                   (8 if c % 2 == 0 else 10, m10), (10, e10), (15, m11)]
                + xproj_pieces(c))

    def tail_pieces(c):
        def p_ln():
            S.tag = "ln"
            ln(c)

        def p_out():
            dma_x_out(c)
        return [(0, p_ln)] + ffn_pieces(c, 0.5) + [(0, p_out)]

    for c in (0, 1):
        for _mdc, pc in front_pieces(c):
            pc()
    S.tag = "scan"
    for n in range(8):
        E("act", lambda e, n=n, ap_=dA_s[n % 8]: e.activation(
            out=ap_[:], in_=deltaB[:, 0, :], func=AF.Exp,
            scale=A_sc[:, n:n + 1]),
          reads=[d_delta[0], d_Asc], writes=[d_dA[n % 8]])
    for P in range(NPAIR):
        pieces = []
        wparts = {}
        if P >= 1:
            mm, ev = wout_pieces(P - 1)
            wparts[(0, 0)] = mm
            wparts[(0, 2)] = [ev[0]]
            wparts[(0, 3)] = [ev[1]]
            pieces += tail_pieces(2 * P - 2) + tail_pieces(2 * P - 1)
        if P + 1 <= NPAIR - 1:
            pieces += front_pieces(2 * P + 2) + front_pieces(2 * P + 3)
        pieces.sort(key=lambda t: t[0])
        n = len(pieces)
        k = (n + 4 * NDC - 1) // (4 * NDC)
        j = 0
        for dc in range(NDC):
            for qtr in range(4):
                scan_dc_q(P, dc, qtr)
                for _m, pc in wparts.pop((dc, qtr), ()):
                    pc()
                budget = k
                while j < n and (pieces[j][0] < dc or (pieces[j][0] == dc and qtr == 3)) and (budget > 0 or (dc == NDC - 1 and qtr == 3)):
                    pieces[j][1]()
                    j += 1
                    budget -= 1
        while j < n:
            pieces[j][1]()
            j += 1
    mm, ev = wout_pieces(NPAIR - 1)
    for _mdc, pc in mm + ev:
        pc()
    for c in (2 * NPAIR - 2, 2 * NPAIR - 1):
        for _mdc, pc in tail_pieces(c):
            pc()

    final_waits = [(s, S.cnt.get(s, 0)) for s in ("xout0", "xout1", "xout2", "xout3")
                   if S.cnt.get(s, 0) > 0]

    with nc.Block() as block:
        def make(eng_name):
            def body(eng):
                for fn, waits, sem_name, _tag in S.ops[eng_name]:
                    for (e2, c2) in waits:
                        eng.wait_ge(sems[e2], c2)
                    inst = fn(eng)
                    inst.then_inc(sems[sem_name], 16 if sem_name in DMA_SEMS else 1)
                if eng_name == "sq":
                    for (s, c2) in final_waits:
                        eng.wait_ge(sems[s], c2)
            return body
        block.tensor(make("pe"))
        block.scalar(make("act"))
        block.vector(make("dve"))
        block.gpsimd(make("pool"))
        block.sync(make("sq"))

    es.close()
    return nc


def _host_inputs(inputs, L=4096):
    W_in = np.ascontiguousarray(inputs["L0_attn_W_in"], dtype=np.float32)
    conv_w = np.asarray(inputs["L0_attn_conv_w"], dtype=np.float32)[:, 0, :]
    conv_b = np.asarray(inputs["L0_attn_conv_b"], dtype=np.float32)
    W_xp = np.asarray(inputs["L0_attn_W_xproj"], dtype=np.float32)
    W_dt0 = np.asarray(inputs["L0_attn_W_dt"], dtype=np.float32)
    b_dt0 = np.asarray(inputs["L0_attn_b_dt"], dtype=np.float32)
    W_dt = np.ascontiguousarray(np.concatenate([W_dt0, b_dt0[None, :]], axis=0))
    b_dt = np.asarray(inputs["L0_attn_b_dt"], dtype=np.float32)
    A_log = np.asarray(inputs["L0_attn_A_log"], dtype=np.float32)
    D_skip = np.asarray(inputs["L0_attn_D_skip"], dtype=np.float32)
    W_out = np.ascontiguousarray(inputs["L0_attn_W_out"], dtype=np.float32)
    selw = np.asarray(inputs["L0_ffn_expert_sel"], dtype=np.float32)
    keys = np.asarray(inputs["L0_ffn_keys"], dtype=np.float32)
    values = np.asarray(inputs["L0_ffn_values"], dtype=np.float32)

    keys_r = np.ascontiguousarray(
        keys.reshape(16, NFC, 128, 128).transpose(0, 2, 1, 3).reshape(16, 128, 1024))
    vals_r = np.ascontiguousarray(values)
    xproj_r = np.zeros((DI, 128), np.float32)
    xproj_r[:, 0:16] = W_xp[:, 64:80]
    xproj_r[:, 32:48] = W_xp[:, 80:96]
    xproj_r[:, 64:128] = W_xp[:, 0:64]
    selw_r = np.ascontiguousarray(selw.T)

    def di_layout(v):
        return np.ascontiguousarray(v.reshape(NDC, 128).T)
    alog_r = np.ascontiguousarray(
        A_log.reshape(NDC, 128, NS).transpose(1, 0, 2).reshape(128, NDC * NS))
    convd_r = np.zeros((NDC, 128, 4, 128), np.float32)
    idx = np.arange(128)
    for dc in range(NDC):
        for k in range(4):
            convd_r[dc, idx, k, idx] = conv_w[dc * 128 + idx, k]
    convd_r = np.ascontiguousarray(convd_r.reshape(NDC, 128, 512))
    shared = dict(
        w_in=W_in, keys_r=keys_r, vals_r=vals_r, wout_r=W_out,
        convd_r=convd_r, xproj_r=xproj_r, wdt_r=W_dt, selw_r=selw_r, alog_r=alog_r,
        dsk_r=di_layout(D_skip), cb_r=di_layout(conv_b), bdt_r=di_layout(b_dt),
    )
    x = np.asarray(inputs["L0_attn_x"], dtype=np.float32)
    xs = [np.ascontiguousarray(x[b, :L, :].T) for b in range(x.shape[0])]
    return shared, xs


def kernel(**inputs):
    L = 4096
    shared, xs = _host_inputs(inputs, L)
    nc = build_program(L)
    in_maps = [dict(shared, x_f=xs[b]) for b in range(8)]
    res = run_bass_kernel_spmd(nc, in_maps, core_ids=list(range(8)))
    out = np.empty((8, L, D), np.float32)
    for b in range(8):
        out[b] = np.asarray(res.results[b]["out_f"], dtype=np.float32).T
    return out


# version 58
# speedup vs baseline: 1.1492x; 1.0000x over previous
import contextlib
import numpy as np
import concourse.bass as bass
import concourse.mybir as mybir
from concourse.bass_utils import run_bass_kernel_spmd

F32 = mybir.dt.float32
F16 = mybir.dt.float16
BF16 = mybir.dt.bfloat16
AF = mybir.ActivationFunctionType
OP = mybir.AluOpType

D = 1024
DI = 2048
NS = 16
DTR = 64
T = 256
TS = 512
NFC = D // 128
NDC = DI // 128
EPS = 1e-5

ENGS = ("pe", "act", "dve", "pool", "sq")
DMA_SEMS = ("xin0", "xin1", "xin2", "xin3", "xout0", "xout1", "xout2", "xout3",
            "win0", "win1", "win2", "key0", "key1", "key2", "val0", "val1", "val2",
            "stgi0", "stgi1", "scr0", "scr1", "dir",
            "rso0", "rso1", "rsi0", "rsi1", "rbo0", "rbo1", "rbi0", "rbi1",
            "rco0", "rco1", "rci0", "rci1")


class Dep:
    __slots__ = ("w", "r")

    def __init__(self):
        self.w = None
        self.r = []


class Sched:

    def __init__(self):
        self.ops = {e: [] for e in ENGS}
        self.cnt = {}
        self.waited = {e: {} for e in ENGS}
        self.tag = "init"

    def emit(self, eng, fn, reads=(), writes=(), dma=None, extra_waits=()):
        need = {}

        def add(e, c):
            if c > need.get(e, 0):
                need[e] = c

        for d in reads:
            if d.w:
                add(*d.w)
        for d in writes:
            if d.w:
                add(*d.w)
            for (e, c) in d.r:
                add(e, c)
        for (e, c) in extra_waits:
            add(e, c)
        waits = []
        w = self.waited[eng]
        for e, c in need.items():
            if e == eng:
                if eng != "pool":
                    continue
                c = min(c, self.cnt.get(eng, 0))
            if c > w.get(e, 0):
                waits.append((e, c))
                w[e] = c
        if dma is not None:
            self.cnt[dma] = self.cnt.get(dma, 0) + 16
            me = (dma, self.cnt[dma])
            self.ops[eng].append((fn, waits, dma, self.tag))
        else:
            self.cnt[eng] = self.cnt.get(eng, 0) + 1
            me = (eng, self.cnt[eng])
            self.ops[eng].append((fn, waits, eng, self.tag))
        for d in writes:
            d.w = me
            d.r = []
        for d in reads:
            d.r.append(me)


def build_program(L=4096):
    NPAIR = L // TS
    nc = bass.Bass("TRN2", target_bir_lowering=False, debug=False)

    x_in_d = nc.dram_tensor("x_f", [D, L], F32, kind="ExternalInput").ap()
    out_d = nc.dram_tensor("out_f", [D, L], F32, kind="ExternalOutput").ap()
    w_in_d = nc.dram_tensor("w_in", [D, 2 * DI], F32, kind="ExternalInput").ap()
    keys_d = nc.dram_tensor("keys_r", [16, 128, 1024], F32, kind="ExternalInput").ap()
    vals_d = nc.dram_tensor("vals_r", [16, 128, 1024], F32, kind="ExternalInput").ap()
    wout_d = nc.dram_tensor("wout_r", [DI, D], F32, kind="ExternalInput").ap()
    convd_d = nc.dram_tensor("convd_r", [NDC, 128, 512], F32, kind="ExternalInput").ap()
    xproj_d = nc.dram_tensor("xproj_r", [DI, 128], F32, kind="ExternalInput").ap()
    wdt_d = nc.dram_tensor("wdt_r", [DTR + 1, DI], F32, kind="ExternalInput").ap()
    selw_d = nc.dram_tensor("selw_r", [D, 16], F32, kind="ExternalInput").ap()
    alog_d = nc.dram_tensor("alog_r", [128, 256], F32, kind="ExternalInput").ap()
    dsk_d = nc.dram_tensor("dsk_r", [128, 16], F32, kind="ExternalInput").ap()
    cb_d = nc.dram_tensor("cb_r", [128, 16], F32, kind="ExternalInput").ap()
    bdt_d = nc.dram_tensor("bdt_r", [128, 16], F32, kind="ExternalInput").ap()

    win_bf = nc.dram_tensor("win_bf", [D, 2 * DI], BF16, kind="Internal").ap()
    keys_bf = nc.dram_tensor("keys_bf", [16, 128, 1024], BF16, kind="Internal").ap()
    vals_bf = nc.dram_tensor("vals_bf", [16, 128, 1024], BF16, kind="Internal").ap()
    wout_bf = nc.dram_tensor("wout_bf", [DI, D], BF16, kind="Internal").ap()
    convd_bf = nc.dram_tensor("convd_bf", [NDC, 128, 512], BF16, kind="Internal").ap()
    selscr = nc.dram_tensor("selscr", [2, 16, T], BF16, kind="Internal").ap()
    bscr = nc.dram_tensor("bscr", [2, 16, T], BF16, kind="Internal").ap()
    cscr = nc.dram_tensor("cscr", [2, 16, T], BF16, kind="Internal").ap()

    es = contextlib.ExitStack()

    def sb(name, shape, dt):
        return es.enter_context(nc.sbuf_tensor(name, shape, dt))

    xB = [sb(f"x{i}", [128, NFC, T], F32) for i in range(4)]
    xn2 = [sb(f"xn{i}", [128, NFC, T], BF16) for i in range(2)]
    x_inb = sb("x_inb", [128, 8, T + 3], BF16)
    tails = sb("tails", [128, NDC, 3], BF16)
    uB = sb("uB", [128, NDC, TS], BF16)
    zB = sb("zB", [128, NDC, TS], BF16)
    deltaB = sb("deltaB", [128, NDC, TS], BF16)
    ygB = sb("ygB", [128, NDC, TS], BF16)
    BrepB = sb("BrepB", [128, NS, TS], BF16)
    CrepB = sb("CrepB", [128, NS, TS], BF16)
    Hbig = sb("Hbig", [128, NS, TS], BF16)
    dA_s = [sb(f"dA{i}", [128, TS], F16) for i in range(8)]
    du_s = [sb("du0", [128, TS], BF16)]
    y_s = du_s[0]
    uX = [sb(f"uX{i}", [128, 2, TS], BF16) for i in range(2)]
    hcar = sb("hcar", [128, 256], BF16)
    A_sc = sb("A_sc", [128, 256], F32)
    dsk_sb = sb("dsk_sb", [128, 16], F32)
    cb_sb = sb("cb_sb", [128, 16], F32)
    smT = sb("smT", [128, 4, T], F32)
    sm = [smT[:, i, :] for i in range(4)]
    tres = sm[2]
    selrep = sb("selrep", [128, 16, T], BF16)
    xb16 = selrep[:, 0:2 * NFC:2, :]
    x2b = selrep[:, 1:2 * NFC:2, :]

    scr_sb = [sb(f"scr{i}", [128, T], BF16) for i in range(2)]
    dtlr_sb = sb("dtlr_sb", [DTR + 1, T], BF16)
    bcs_sb = sb("bcs_sb", [80, T], BF16)
    Bsb = bcs_sb[0:16, :]
    Csb = bcs_sb[32:48, :]
    sel_sb = bcs_sb[64:80, :]
    ones_bf = sb("ones_bf", [128, 128], BF16)
    cpool = sb("cpool", [128, 8], F32)
    xproj_sb = sb("xproj_sb", [128, 16 * 128], BF16)
    wdt_sb = sb("wdt_sb", [DTR + 1, DI], BF16)
    selw_sb = sb("selw_sb", [128, NFC * 16], BF16)
    win_db = [sb(f"windb{i}", [128, 1024], BF16) for i in range(3)]
    key_db = [sb(f"keydb{i}", [128, 1024], BF16) for i in range(2)]
    val_db = [sb(f"valdb{i}", [128, 1024], BF16) for i in range(2)]
    stg32 = [ygB[:, 0:2, :].bitcast(F32).rearrange("p a b -> p (a b)"),
             ygB[:, 2:4, :].bitcast(F32).rearrange("p a b -> p (a b)")]
    stgb = [Hbig[:, 0, :], CrepB[:, 0, :]]

    pbank = [es.enter_context(nc.psum_tensor(f"pb{i}", [128, 512], F32)) for i in range(8)]
    pslot = [pbank[i][:, 0:T] for i in range(8)]

    CONSTS = [EPS, 1.0, -1.0, -0.5, 1.0 / D, 0.5, 0.0, 2.0]
    for ci, cv in enumerate(CONSTS):
        nc.const_aps.register(F32, cv, cpool[:, ci:ci + 1])

    sems = {e: es.enter_context(nc.semaphore(f"sem_{e}")) for e in ENGS}
    for s in DMA_SEMS:
        sems[s] = es.enter_context(nc.semaphore(f"sem_{s}"))

    S = Sched()
    E = S.emit

    d_x = [Dep() for _ in range(4)]
    d_xn2 = [Dep(), Dep()]
    d_xin_body = Dep(); d_xin_tail = Dep()
    d_u = [Dep() for _ in range(NDC)]
    d_z = [Dep() for _ in range(NDC)]
    d_delta = [Dep() for _ in range(NDC)]
    d_yg = [Dep() for _ in range(NDC)]
    d_Brep = Dep(); d_Crep = Dep(); d_H = Dep()
    d_dA = [Dep() for _ in range(8)]
    d_du = [Dep()]
    d_hcar = Dep()
    d_sm = [Dep() for _ in range(4)]
    d_tres = d_sm[2]
    d_selrep = [Dep() for _ in range(16)]; d_selsb = Dep()
    d_scr = [Dep() for _ in range(2)]
    d_dtlr = Dep(); d_Bsb = Dep(); d_Csb = Dep()
    d_slot = [Dep() for _ in range(8)]
    d_windb = [Dep() for _ in range(3)]
    d_keydb = [Dep() for _ in range(2)]
    d_valdb = [Dep() for _ in range(2)]
    d_xin8 = [Dep() for _ in range(8)]
    d_tails = Dep()
    d_stg32 = [[d_yg[0], d_yg[1]], [d_yg[2], d_yg[3]]]
    d_stgb = [[d_H], [d_Crep]]
    d_const = Dep()
    d_dsk = Dep(); d_cb = Dep(); d_bdt = Dep(); d_Asc = Dep()
    d_xpw = Dep(); d_wdtw = Dep(); d_selww = Dep()
    d_selscr = [Dep(), Dep()]; d_bscr = [Dep(), Dep()]; d_cscr = [Dep(), Dep()]

    for ci, cv in enumerate(CONSTS):
        E("dve", lambda e, ci=ci, cv=cv: e.memset(cpool[:, ci:ci + 1], cv), writes=[d_const])
    E("dve", lambda e: e.memset(hcar[:], 0.0), writes=[d_hcar])
    E("dve", lambda e: e.memset(tails[:], 0.0), writes=[d_tails])
    E("dve", lambda e: e.memset(ones_bf[:], 1.0), writes=[d_const])
    E("dve", lambda e: e.memset(dtlr_sb[64:65, :], 1.0), writes=[d_const])
    for _e in ("pe", "act", "pool", "sq"):
        E(_e, lambda e: e.nop(), reads=[d_const])

    E("sq", lambda e: e.dma_start(out=dsk_sb[:], in_=dsk_d), writes=[d_dsk], dma="dir")
    E("sq", lambda e: e.dma_start(out=cb_sb[:], in_=cb_d), writes=[d_cb], dma="dir")
    E("sq", lambda e: e.dma_start(out=sm[0][:], in_=alog_d), writes=[d_sm[0]], dma="dir")
    dir_total = ("dir", S.cnt["dir"])
    for d in (d_dsk, d_cb, d_sm[0]):
        d.w = dir_total
    E("act", lambda e: e.activation(out=sm[1][:], in_=sm[0][:], func=AF.Exp),
      reads=[d_sm[0]], writes=[d_sm[1]])
    E("pool", lambda e: e.tensor_scalar_mul(out=A_sc[:], in0=sm[1][:], scalar1=-1.0),
      reads=[d_sm[1], d_const], writes=[d_Asc])

    rr = [0]

    def conv_round(src_ap, dst_sb=None, dst_sb_dep=None, dst_dram=None, nparts=128):
        s = rr[0] % 2
        rr[0] += 1
        wdt = src_ap.shape[-1]
        E("sq", lambda e, a=src_ap, s=s, p=nparts, w=wdt: e.dma_start(
            out=stg32[s][0:p, :w], in_=a),
          writes=d_stg32[s], dma=f"stgi{s}")
        if dst_sb is not None:
            E("dve", lambda e, d=dst_sb, s=s, p=nparts, w=wdt: e.tensor_copy(
                out=d, in_=stg32[s][0:p, :w]),
              reads=d_stg32[s], writes=[dst_sb_dep])
        else:
            E("dve", lambda e, s=s, p=nparts, w=wdt: e.tensor_copy(
                out=stgb[s][0:p, :w], in_=stg32[s][0:p, :w]),
              reads=d_stg32[s], writes=d_stgb[s])
            E("sq", lambda e, a=dst_dram, s=s, p=nparts, w=wdt: e.dma_start(
                out=a, in_=stgb[s][0:p, :w]),
              reads=d_stgb[s], dma=f"scr{s}")

    for kc in range(NFC):
        for j in range(8):
            conv_round(w_in_d[kc * 128:(kc + 1) * 128, j * 512:(j + 1) * 512],
                       dst_dram=win_bf[kc * 128:(kc + 1) * 128, j * 512:(j + 1) * 512])
    for ei in range(16):
        for j in range(2):
            conv_round(keys_d[ei, :, j * 512:(j + 1) * 512],
                       dst_dram=keys_bf[ei, :, j * 512:(j + 1) * 512])
    for ei in range(16):
        for j in range(2):
            conv_round(vals_d[ei, :, j * 512:(j + 1) * 512],
                       dst_dram=vals_bf[ei, :, j * 512:(j + 1) * 512])
    for kc in range(NDC):
        for j in range(2):
            conv_round(wout_d[kc * 128:(kc + 1) * 128, j * 512:(j + 1) * 512],
                       dst_dram=wout_bf[kc * 128:(kc + 1) * 128, j * 512:(j + 1) * 512])
    for dc in range(NDC):
        conv_round(convd_d[dc], dst_dram=convd_bf[dc])
    for kc in range(NDC):
        conv_round(xproj_d[kc * 128:(kc + 1) * 128, :],
                   dst_sb=xproj_sb[:, kc * 128:(kc + 1) * 128], dst_sb_dep=d_xpw)
    for j in range(4):
        conv_round(wdt_d[:, j * 512:(j + 1) * 512],
                   dst_sb=wdt_sb[:, j * 512:(j + 1) * 512], dst_sb_dep=d_wdtw, nparts=DTR + 1)
    for kc in range(NFC):
        conv_round(selw_d[kc * 128:(kc + 1) * 128, :],
                   dst_sb=selw_sb[:, kc * 16:(kc + 1) * 16], dst_sb_dep=d_selww)

    SCR_TOTALS = [("scr0", S.cnt.get("scr0", 0)), ("scr1", S.cnt.get("scr1", 0))]

    x_src = x_in_d.rearrange("(fc p) L -> p fc L", p=128)
    out_dst = out_d.rearrange("(fc p) L -> p fc L", p=128)

    def dma_x_in(c):
        xb = xB[c % 4]
        E("sq", lambda e, c=c, xb=xb: e.dma_start(out=xb[:], in_=x_src[:, :, c * T:(c + 1) * T]),
          writes=[d_x[c % 4]], dma=f"xin{c % 4}")

    def dma_x_out(c):
        xb = xB[c % 4]
        E("sq", lambda e, c=c, xb=xb: e.dma_start(out=out_dst[:, :, c * T:(c + 1) * T], in_=xb[:]),
          reads=[d_x[c % 4]], dma=f"xout{c % 4}")

    def ln(c):
        xb = xB[c % 4]
        dx = d_x[c % 4]
        E("pool", lambda e, xb=xb: e.tensor_copy(out=xb16, in_=xb[:]),
          reads=[dx], writes=d_selrep[0::2])
        E("act", lambda e, xb=xb: e.activation(out=x2b, in_=xb[:], func=AF.Square),
          reads=[dx], writes=d_selrep[1::2])
        psq = pbank[4][:, 0:512]
        for fc in range(NFC):
            E("pe", lambda e, fc=fc, psq=psq: e.matmul(
                out=psq, lhsT=ones_bf[:],
                rhs=selrep[:, 2 * fc:2 * fc + 2, :].rearrange("p a b -> p (a b)"),
                start=(fc == 0), stop=(fc == NFC - 1)),
              reads=[d_selrep[2 * fc], d_selrep[2 * fc + 1], d_const], writes=[d_slot[4]])
        E("act", lambda e, psq=psq: e.activation(out=sm[0][:], in_=psq[:, 0:T], func=AF.Copy, scale=1.0 / D),
          reads=[d_slot[4]], writes=[d_sm[0]])
        E("act", lambda e, psq=psq: e.activation(out=sm[1][:], in_=psq[:, T:2 * T], func=AF.Copy, scale=1.0 / D),
          reads=[d_slot[4]], writes=[d_sm[1]])
        E("pool", lambda e: e.tensor_tensor(out=sm[2][:], in0=sm[0][:], in1=sm[0][:], op=OP.mult),
          reads=[d_sm[0]], writes=[d_sm[2]])
        E("pool", lambda e: e.tensor_tensor(out=sm[1][:], in0=sm[1][:], in1=sm[2][:], op=OP.subtract),
          reads=[d_sm[2]], writes=[d_sm[1]])
        E("act", lambda e: e.activation(out=sm[1][:], in_=sm[1][:], func=AF.Ln, bias=EPS),
          writes=[d_sm[1]])
        E("act", lambda e: e.activation(out=sm[1][:], in_=sm[1][:], func=AF.Exp, scale=-0.5),
          writes=[d_sm[1]])
        E("pool", lambda e, xb=xb: e.tensor_tensor(
            out=xn2[c % 2][:], in0=xb[:],
            in1=sm[0][:].unsqueeze(1).broadcast_to([128, NFC, T]), op=OP.subtract),
          reads=[dx, d_sm[0]], writes=[d_xn2[c % 2]])
        E("pool", lambda e: e.tensor_tensor(
            out=xn2[c % 2][:], in0=xn2[c % 2][:],
            in1=sm[1][:].unsqueeze(1).broadcast_to([128, NFC, T]), op=OP.mult),
          reads=[d_sm[1]], writes=[d_xn2[c % 2]])

    def ffn_pieces(c, scale):
        xnc = xn2[c % 2]
        d_xnc = d_xn2[c % 2]
        xb = xB[c % 4]
        dx = d_x[c % 4]

        def p_sel():
            S.tag = "ffn_sel"
            psel = pslot[5]
            for kc in range(NFC):
                E("pe", lambda e, kc=kc, psel=psel: e.matmul(
                    out=psel[0:16, :], lhsT=selw_sb[:, kc * 16:(kc + 1) * 16], rhs=xnc[:, kc, :],
                    start=(kc == 0), stop=(kc == NFC - 1)),
                  reads=[d_xnc, d_selww], writes=[d_slot[5]])
            E("act", lambda e, psel=psel: e.activation(out=bcs_sb[64:80, :], in_=psel[0:16, :], func=AF.Sigmoid),
              reads=[d_slot[5]], writes=[d_selsb])
            q = c % 2
            E("pool", lambda e, q=q: e.dma_start(out=selscr[q], in_=sel_sb[:]),
              reads=[d_selsb], writes=[d_selscr[q]], dma=f"rso{q}")
            E("pool", lambda e, q=q: e.dma_start(
                out=selrep[:], in_=selscr[q].unsqueeze(0).broadcast_to([128, 16, T])),
              reads=[d_selscr[q]], writes=d_selrep, dma=f"rsi{q}")

        def mk_p1(e0):
            def go():
                S.tag = "ffn_p1"
                for ei in (e0, e0 + 1):
                    pk = ei % 2
                    p = ei % 2
                    sl = 6 + (ei % 2)
                    E("sq", lambda e, ei=ei, pk=pk: e.dma_start(out=key_db[pk][:], in_=keys_bf[ei]),
                      writes=[d_keydb[pk]], dma=f"key{pk}", extra_waits=SCR_TOTALS)
                    for kc in range(NFC):
                        E("pe", lambda e, kc=kc, sl=sl, pk=pk: e.matmul(
                            out=pslot[sl], lhsT=key_db[pk][:, kc * 128:(kc + 1) * 128], rhs=xnc[:, kc, :],
                            start=(kc == 0), stop=(kc == NFC - 1)),
                          reads=[d_keydb[pk], d_xnc], writes=[d_slot[sl]])
                    E("act", lambda e, sl=sl, p=p: e.activation(out=scr_sb[p][:], in_=pslot[sl], func=AF.Relu),
                      reads=[d_slot[sl]], writes=[d_scr[p]])
                    E("pool", lambda e, ei=ei, p=p: e.tensor_tensor(out=selrep[:, ei, :], in0=scr_sb[p][:],
                                                                    in1=selrep[:, ei, :], op=OP.mult),
                      reads=[d_scr[p]], writes=[d_selrep[ei]])
            return go

        def mk_p2(e0):
            def go():
                S.tag = "ffn_p2"
                for ei in (e0, e0 + 1):
                    p = ei % 2
                    E("sq", lambda e, ei=ei, p=p: e.dma_start(out=val_db[p][:], in_=vals_bf[ei]),
                      writes=[d_valdb[p]], dma=f"val{p}", extra_waits=SCR_TOTALS)
                    for vc in range(8):
                        E("pe", lambda e, ei=ei, vc=vc, p=p: e.matmul(
                            out=pslot[vc], lhsT=val_db[p][:, vc * 128:(vc + 1) * 128],
                            rhs=selrep[:, ei, :],
                            start=(ei == 0), stop=(ei == 15)),
                          reads=[d_valdb[p], d_selrep[ei]], writes=[d_slot[vc]])
            return go

        def p_evac():
            S.tag = "ffn_evac"
            for vc in range(8):
                tr = sm[2 + vc % 2]
                dtr = d_sm[2 + vc % 2]
                E("act", lambda e, vc=vc, scale=scale, tr=tr: e.activation(
                    out=tr[:], in_=pslot[vc], func=AF.Copy, scale=scale),
                  reads=[d_slot[vc]], writes=[dtr])
                E("pool", lambda e, vc=vc, xb=xb, tr=tr: e.tensor_tensor(
                    out=xb[:, vc, :], in0=xb[:, vc, :], in1=tr[:], op=OP.add),
                  reads=[dtr, dx], writes=[dx])

        return ([(0, p_sel)] + [(0, mk_p1(e0)) for e0 in range(0, 16, 2)]
                + [(0, mk_p2(e0)) for e0 in range(0, 16, 2)] + [(0, p_evac)])

    def win_pieces(c, half):
        cip = c % 2
        xnc = xn2[c % 2]
        d_xnc = d_xn2[c % 2]

        def mk(h, q, inline_ev=False):
            def mms():
                for kc in range(NFC):
                    p = (q * NFC + kc) % 3
                    cols = h * 2048 + q * 1024
                    E("sq", lambda e, kc=kc, p=p, cols=cols: e.dma_start(
                        out=win_db[p][:], in_=win_bf[kc * 128:(kc + 1) * 128, cols:cols + 1024]),
                      writes=[d_windb[p]], dma=f"win{p}", extra_waits=SCR_TOTALS)
                    for m in range(8):
                        E("pe", lambda e, kc=kc, m=m, p=p: e.matmul(
                            out=pslot[m], lhsT=win_db[p][:, m * 128:(m + 1) * 128], rhs=xnc[:, kc, :],
                            start=(kc == 0), stop=(kc == NFC - 1)),
                          reads=[d_windb[p], d_xnc], writes=[d_slot[m]])
                if inline_ev:
                    evs()

            def evs():
                for m in range(8):
                    mc = q * 8 + m
                    if h == 0:
                        E("act", lambda e, m=m, mc=mc: e.activation(
                            out=x_inb[:, mc % 8, 3:3 + T], in_=pslot[m], func=AF.Copy),
                          reads=[d_slot[m]], writes=[d_xin8[mc % 8]])
                    else:
                        E("act", lambda e, m=m, mc=mc, cip=cip: e.activation(
                            out=zB[:, mc, cip * T:(cip + 1) * T], in_=pslot[m], func=AF.Silu),
                          reads=[d_slot[m]], writes=[d_z[mc]])
            return mms, evs
        return mk

    def conv_pieces(c):
        cip = c % 2
        par = (c // 2) % 2

        def mk(d0):
            def go():
                s0 = d0 % 8
                xds = [d_xin8[(d0 + i) % 8] for i in range(4)]
                E("act", lambda e, d0=d0, s0=s0: e.activation(
                    out=x_inb[:, s0:s0 + 4, 0:3], in_=tails[:, d0:d0 + 4, :], func=AF.Copy),
                  reads=[d_tails], writes=xds)
                for dc in range(d0, d0 + 4):
                    s8 = dc % 8
                    p = dc % 2
                    E("sq", lambda e, dc=dc, p=p: e.dma_start(out=key_db[p][:, 0:512], in_=convd_bf[dc]),
                      writes=[d_keydb[p]], dma=f"key{p}", extra_waits=SCR_TOTALS)
                    sl = 6 + (dc % 2)
                    for k in range(4):
                        E("pe", lambda e, dc=dc, k=k, p=p, sl=sl, s8=s8: e.matmul(
                            out=pslot[sl], lhsT=key_db[p][:, k * 128:(k + 1) * 128],
                            rhs=x_inb[:, s8, k:k + T],
                            start=(k == 0), stop=(k == 3)),
                          reads=[d_keydb[p], d_xin8[s8]], writes=[d_slot[sl]])
                    udst = (uB[:, dc, cip * T:(cip + 1) * T] if dc < 14
                            else uX[par][:, dc - 14, cip * T:(cip + 1) * T])
                    E("act", lambda e, dc=dc, sl=sl, udst=udst: e.activation(
                        out=udst, in_=pslot[sl], func=AF.Silu,
                        bias=cb_sb[:, dc:dc + 1]),
                      reads=[d_slot[sl], d_cb], writes=[d_u[dc]])
                E("act", lambda e, d0=d0, s0=s0: e.activation(
                    out=tails[:, d0:d0 + 4, :], in_=x_inb[:, s0:s0 + 4, T:T + 3], func=AF.Copy),
                  reads=xds, writes=[d_tails])
            return go
        return [(d0 + 3, mk(d0)) for d0 in (0, 4, 8)] + [(13, mk(12))]

    def xproj_pieces(c):
        cip = c % 2

        def p_xproj():
            S.tag = "xproj"
            px = pslot[5]
            par = (c // 2) % 2
            for kc in range(NDC):
                usrc = (uB[:, kc, cip * T:(cip + 1) * T] if kc < 14
                        else uX[par][:, kc - 14, cip * T:(cip + 1) * T])
                E("pe", lambda e, kc=kc, px=px, usrc=usrc: e.matmul(
                    out=px, lhsT=xproj_sb[:, kc * 128:(kc + 1) * 128],
                    rhs=usrc,
                    start=(kc == 0), stop=(kc == NDC - 1)),
                  reads=[d_u[kc], d_xpw], writes=[d_slot[5]])
            E("act", lambda e, px=px: e.activation(out=dtlr_sb[0:64, :], in_=px[64:128, :], func=AF.Copy),
              reads=[d_slot[5]], writes=[d_dtlr])
            E("act", lambda e, px=px: e.activation(out=Bsb[:], in_=px[0:16, :], func=AF.Copy),
              reads=[d_slot[5]], writes=[d_Bsb])
            E("act", lambda e, px=px: e.activation(out=bcs_sb[32:48, :], in_=px[32:48, :], func=AF.Copy),
              reads=[d_slot[5]], writes=[d_Csb])

        def p_bounce_out():
            q = c % 2
            E("pool", lambda e, q=q: e.dma_start(out=bscr[q], in_=Bsb[:]),
              reads=[d_Bsb], writes=[d_bscr[q]], dma=f"rbo{q}")
            E("pool", lambda e, q=q: e.dma_start(out=cscr[q], in_=Csb[:]),
              reads=[d_Csb], writes=[d_cscr[q]], dma=f"rco{q}")

        def p_bcastB():
            q = c % 2
            E("pool", lambda e, q=q, cip=cip: e.dma_start(
                out=BrepB[:, :, cip * T:(cip + 1) * T],
                in_=bscr[q].unsqueeze(0).broadcast_to([128, 16, T])),
              reads=[d_bscr[q]], writes=[d_Brep], dma=f"rbi{q}")

        def p_bcastC():
            q = c % 2
            E("pool", lambda e, q=q, cip=cip: e.dma_start(
                out=CrepB[:, :, cip * T:(cip + 1) * T],
                in_=cscr[q].unsqueeze(0).broadcast_to([128, 16, T])),
              reads=[d_cscr[q]], writes=[d_Crep], dma=f"rci{q}")

        pend = []

        def emit_dt_stt():
            mc, bk, lvd = pend.pop(0)
            E("dve", lambda e, mc=mc, cip=cip, bk=bk, lvd=lvd: e.scalar_tensor_tensor(
                out=deltaB[:, mc:mc + 2, cip * T:(cip + 1) * T],
                in0=pbank[bk][:, 0:2 * T], scalar=0.0, in1=smT[:, lvd:lvd + 2, :],
                op0=OP.max, op1=OP.add),
              reads=[d_slot[bk], d_sm[lvd], d_sm[lvd + 1], d_const],
              writes=[d_delta[mc], d_delta[mc + 1]])

        def mk_dt(m0):
            def go():
                S.tag = "dt"
                for mc in range(m0, m0 + 4, 2):
                    if len(pend) >= 2:
                        emit_dt_stt()
                    pr = (mc // 2) % 2
                    bk = 6 + pr
                    lvd = 2 * pr
                    full = pbank[bk][:, 0:2 * T]
                    lv = smT[:, lvd:lvd + 2, :].rearrange("p a b -> p (a b)")
                    E("pe", lambda e, mc=mc, bk=bk: e.matmul(
                        out=pbank[bk][:, 0:T], lhsT=wdt_sb[:, mc * 128:(mc + 1) * 128], rhs=dtlr_sb[:],
                        start=True, stop=True),
                      reads=[d_dtlr, d_wdtw], writes=[d_slot[bk]])
                    E("pe", lambda e, mc=mc, bk=bk: e.matmul(
                        out=pbank[bk][:, T:2 * T], lhsT=wdt_sb[:, (mc + 1) * 128:(mc + 2) * 128], rhs=dtlr_sb[:],
                        start=True, stop=True),
                      reads=[d_dtlr, d_wdtw], writes=[d_slot[bk]])
                    E("act", lambda e, bk=bk, lv=lv, full=full: e.activation(out=lv, in_=full, func=AF.Abs),
                      reads=[d_slot[bk]], writes=[d_sm[lvd], d_sm[lvd + 1]])
                    E("act", lambda e, lv=lv: e.activation(out=lv, in_=lv, func=AF.Exp, scale=-1.0),
                      writes=[d_sm[lvd], d_sm[lvd + 1]])
                    E("act", lambda e, lv=lv: e.activation(out=lv, in_=lv, func=AF.Ln, bias=1.0),
                      writes=[d_sm[lvd], d_sm[lvd + 1]])
                    pend.append((mc, bk, lvd))
                if m0 + 4 >= NDC - 4:
                    while pend:
                        emit_dt_stt()
            return go
        return ([(13, p_xproj), (13, p_bounce_out)]
                + [(13, mk_dt(m0)) for m0 in (0, 4, 8)] + [(15, mk_dt(12))]
                + [(15, p_bcastB), (15, p_bcastC)])

    def scan_dc_q(P, dc, qtr):
        S.tag = "scan"
        usrc = uB[:, dc, :] if dc < 14 else uX[P % 2][:, dc - 14, :]
        if qtr == 0:
            E("dve", lambda e, dc=dc, usrc=usrc: e.tensor_tensor(
                out=du_s[0][:], in0=deltaB[:, dc, :], in1=usrc, op=OP.mult),
              reads=[d_delta[dc], d_u[dc]], writes=[d_du[0]])
            E("dve", lambda e, dc=dc: e.tensor_tensor(
                out=Hbig[:], in0=du_s[0][:].unsqueeze(1).broadcast_to([128, NS, TS]),
                in1=BrepB[:], op=OP.mult),
              reads=[d_du[0], d_Brep], writes=[d_H])
        for n in range(4 * qtr, 4 * qtr + 4):
            ap_ = dA_s[n % 8]
            E("dve", lambda e, dc=dc, n=n, ap_=ap_: e.tensor_tensor_scan(
                out=Hbig[:, n, :], data0=ap_[:], data1=Hbig[:, n, :],
                initial=hcar[:, dc * NS + n:dc * NS + n + 1], op0=OP.mult, op1=OP.add),
              reads=[d_dA[n % 8], d_hcar], writes=[d_H])
        if qtr == 3:
            E("dve", lambda e, dc=dc: e.tensor_copy(
                out=hcar[:, dc * NS:(dc + 1) * NS], in_=Hbig[:, :, TS - 1:TS].squeeze(2)),
              reads=[d_H], writes=[d_hcar])
            E("dve", lambda e: e.tensor_tensor(out=Hbig[:], in0=Hbig[:], in1=CrepB[:], op=OP.mult),
              reads=[d_H, d_Crep], writes=[d_H])
            E("dve", lambda e: e.tensor_tensor(out=Hbig[:, 0:8, :], in0=Hbig[:, 0:8, :],
                                               in1=Hbig[:, 8:16, :], op=OP.add),
              reads=[d_H], writes=[d_H])
            E("dve", lambda e: e.tensor_tensor(out=Hbig[:, 0:4, :], in0=Hbig[:, 0:4, :],
                                               in1=Hbig[:, 4:8, :], op=OP.add),
              reads=[d_H], writes=[d_H])
            E("dve", lambda e: e.tensor_tensor(out=Hbig[:, 0:2, :], in0=Hbig[:, 0:2, :],
                                               in1=Hbig[:, 2:4, :], op=OP.add),
              reads=[d_H], writes=[d_H])
            E("dve", lambda e: e.tensor_tensor(out=y_s[:], in0=Hbig[:, 0, :],
                                               in1=Hbig[:, 1, :], op=OP.add),
              reads=[d_H], writes=[d_du[0]])
            E("dve", lambda e, dc=dc, usrc=usrc: e.scalar_tensor_tensor(
                out=y_s[:], in0=usrc, scalar=dsk_sb[:, dc:dc + 1], in1=y_s[:],
                op0=OP.mult, op1=OP.add),
              reads=[d_u[dc], d_dsk, d_du[0]], writes=[d_du[0]])
            E("dve", lambda e, dc=dc: e.tensor_tensor(
                out=ygB[:, dc, :], in0=y_s[:], in1=zB[:, dc, :], op=OP.mult),
              reads=[d_du[0], d_z[dc]], writes=[d_yg[dc]])
        if qtr % 2 == 0:
            jr = ()
        else:
            jr = range(8)
        for j in jr:
            nn = 4 * (qtr - 1) + 8 + j
            gg = P * NDC + dc + nn // NS
            n2 = nn % NS
            if gg >= NPAIR * NDC:
                continue
            d2 = gg % NDC
            ap_ = dA_s[n2 % 8]
            E("act", lambda e, d2=d2, n2=n2, ap_=ap_: e.activation(
                out=ap_[:], in_=deltaB[:, d2, :], func=AF.Exp,
                scale=A_sc[:, d2 * NS + n2:d2 * NS + n2 + 1]),
              reads=[d_delta[d2], d_Asc], writes=[d_dA[n2 % 8]])

    def wout_pieces(P):
        mm = []

        def mk_dc(dc):
            def go():
                S.tag = "wout"
                p = dc % 3
                E("sq", lambda e, dc=dc, p=p: e.dma_start(
                    out=win_db[p][:], in_=wout_bf[dc * 128:(dc + 1) * 128, :]),
                  writes=[d_windb[p]], dma=f"win{p}", extra_waits=SCR_TOTALS)
                for vc in range(8):
                    E("pe", lambda e, dc=dc, vc=vc, p=p: e.matmul(
                        out=pbank[vc][:, 0:512], lhsT=win_db[p][:, vc * 128:(vc + 1) * 128],
                        rhs=ygB[:, dc, :],
                        start=(dc == 0), stop=(dc == NDC - 1)),
                      reads=[d_windb[p], d_yg[dc]], writes=[d_slot[vc]])
            return go

        def mk_evac(h):
            def go():
                S.tag = "wout"
                c = 2 * P + h
                xb = xB[c % 4]
                dx = d_x[c % 4]
                for vc in range(8):
                    tr = sm[2 + vc % 2]
                    dtr = d_sm[2 + vc % 2]
                    E("act", lambda e, vc=vc, h=h, tr=tr: e.activation(
                        out=tr[:], in_=pbank[vc][:, h * T:(h + 1) * T], func=AF.Copy),
                      reads=[d_slot[vc]], writes=[dtr])
                    E("pool", lambda e, vc=vc, xb=xb, dx=dx, tr=tr: e.tensor_tensor(
                        out=xb[:, vc, :], in0=xb[:, vc, :], in1=tr[:], op=OP.add),
                      reads=[dtr, dx], writes=[dx])
            return go

        for dc in range(NDC):
            mm.append((0, mk_dc(dc)))
        ev = [(0, mk_evac(0)), (0, mk_evac(1))]
        return mm, ev

    def front_pieces(c):
        def p_dma():
            dma_x_in(c)

        def p_ln():
            S.tag = "ln"
            ln(c)

        def p_ln2():
            ln(c)
        wmk = win_pieces(c, 0)
        cps = conv_pieces(c)
        m00, e00 = wmk(0, 0)
        m01, e01 = wmk(0, 1)
        m10, e10 = wmk(1, 0)
        m11, _ = wmk(1, 1, inline_ev=True)
        kx = (lambda k: k) if c % 2 == 0 else (lambda k: 13)
        return ([(0, p_dma), (0, p_ln)] + ffn_pieces(c, 0.5) + [(0, p_ln2)]
                + [(kx(1), m00), (kx(2), e00), (kx(cps[0][0]), cps[0][1]), (kx(cps[1][0]), cps[1][1]),
                   (kx(7), m01), (kx(8), e01), (kx(cps[2][0]), cps[2][1]), (kx(cps[3][0]), cps[3][1]),
                   (8 if c % 2 == 0 else 10, m10), (10, e10), (15, m11)]
                + xproj_pieces(c))

    def tail_pieces(c):
        def p_ln():
            S.tag = "ln"
            ln(c)

        def p_out():
            dma_x_out(c)
        return [(0, p_ln)] + ffn_pieces(c, 0.5) + [(0, p_out)]

    for c in (0, 1):
        for _mdc, pc in front_pieces(c):
            pc()
    S.tag = "scan"
    for n in range(8):
        E("act", lambda e, n=n, ap_=dA_s[n % 8]: e.activation(
            out=ap_[:], in_=deltaB[:, 0, :], func=AF.Exp,
            scale=A_sc[:, n:n + 1]),
          reads=[d_delta[0], d_Asc], writes=[d_dA[n % 8]])
    for P in range(NPAIR):
        pieces = []
        wparts = {}
        if P >= 1:
            mm, ev = wout_pieces(P - 1)
            wparts[(0, 0)] = mm
            wparts[(0, 2)] = [ev[0]]
            wparts[(0, 3)] = [ev[1]]
            pieces += tail_pieces(2 * P - 2) + tail_pieces(2 * P - 1)
        if P + 1 <= NPAIR - 1:
            pieces += front_pieces(2 * P + 2) + front_pieces(2 * P + 3)
        pieces.sort(key=lambda t: t[0])
        n = len(pieces)
        k = (n + 4 * NDC - 1) // (4 * NDC)
        j = 0
        for dc in range(NDC):
            for qtr in range(4):
                scan_dc_q(P, dc, qtr)
                for _m, pc in wparts.pop((dc, qtr), ()):
                    pc()
                budget = k
                while j < n and (pieces[j][0] < dc or (pieces[j][0] == dc and qtr == 3)) and (budget > 0 or (dc == NDC - 1 and qtr == 3)):
                    pieces[j][1]()
                    j += 1
                    budget -= 1
        while j < n:
            pieces[j][1]()
            j += 1
    mm, ev = wout_pieces(NPAIR - 1)
    for _mdc, pc in mm + ev:
        pc()
    for c in (2 * NPAIR - 2, 2 * NPAIR - 1):
        for _mdc, pc in tail_pieces(c):
            pc()

    final_waits = [(s, S.cnt.get(s, 0)) for s in ("xout0", "xout1", "xout2", "xout3")
                   if S.cnt.get(s, 0) > 0]

    with nc.Block() as block:
        def make(eng_name):
            def body(eng):
                for fn, waits, sem_name, _tag in S.ops[eng_name]:
                    for (e2, c2) in waits:
                        eng.wait_ge(sems[e2], c2)
                    inst = fn(eng)
                    inst.then_inc(sems[sem_name], 16 if sem_name in DMA_SEMS else 1)
                if eng_name == "sq":
                    for (s, c2) in final_waits:
                        eng.wait_ge(sems[s], c2)
            return body
        block.tensor(make("pe"))
        block.scalar(make("act"))
        block.vector(make("dve"))
        block.gpsimd(make("pool"))
        block.sync(make("sq"))

    es.close()
    return nc


def _host_inputs(inputs, L=4096):
    W_in = np.ascontiguousarray(inputs["L0_attn_W_in"], dtype=np.float32)
    conv_w = np.asarray(inputs["L0_attn_conv_w"], dtype=np.float32)[:, 0, :]
    conv_b = np.asarray(inputs["L0_attn_conv_b"], dtype=np.float32)
    W_xp = np.asarray(inputs["L0_attn_W_xproj"], dtype=np.float32)
    W_dt0 = np.asarray(inputs["L0_attn_W_dt"], dtype=np.float32)
    b_dt0 = np.asarray(inputs["L0_attn_b_dt"], dtype=np.float32)
    W_dt = np.ascontiguousarray(np.concatenate([W_dt0, b_dt0[None, :]], axis=0))
    b_dt = np.asarray(inputs["L0_attn_b_dt"], dtype=np.float32)
    A_log = np.asarray(inputs["L0_attn_A_log"], dtype=np.float32)
    D_skip = np.asarray(inputs["L0_attn_D_skip"], dtype=np.float32)
    W_out = np.ascontiguousarray(inputs["L0_attn_W_out"], dtype=np.float32)
    selw = np.asarray(inputs["L0_ffn_expert_sel"], dtype=np.float32)
    keys = np.asarray(inputs["L0_ffn_keys"], dtype=np.float32)
    values = np.asarray(inputs["L0_ffn_values"], dtype=np.float32)

    keys_r = np.ascontiguousarray(
        keys.reshape(16, NFC, 128, 128).transpose(0, 2, 1, 3).reshape(16, 128, 1024))
    vals_r = np.ascontiguousarray(values)
    xproj_r = np.zeros((DI, 128), np.float32)
    xproj_r[:, 0:16] = W_xp[:, 64:80]
    xproj_r[:, 32:48] = W_xp[:, 80:96]
    xproj_r[:, 64:128] = W_xp[:, 0:64]
    selw_r = np.ascontiguousarray(selw.T)

    def di_layout(v):
        return np.ascontiguousarray(v.reshape(NDC, 128).T)
    alog_r = np.ascontiguousarray(
        A_log.reshape(NDC, 128, NS).transpose(1, 0, 2).reshape(128, NDC * NS))
    convd_r = np.zeros((NDC, 128, 4, 128), np.float32)
    idx = np.arange(128)
    for dc in range(NDC):
        for k in range(4):
            convd_r[dc, idx, k, idx] = conv_w[dc * 128 + idx, k]
    convd_r = np.ascontiguousarray(convd_r.reshape(NDC, 128, 512))
    shared = dict(
        w_in=W_in, keys_r=keys_r, vals_r=vals_r, wout_r=W_out,
        convd_r=convd_r, xproj_r=xproj_r, wdt_r=W_dt, selw_r=selw_r, alog_r=alog_r,
        dsk_r=di_layout(D_skip), cb_r=di_layout(conv_b), bdt_r=di_layout(b_dt),
    )
    x = np.asarray(inputs["L0_attn_x"], dtype=np.float32)
    xs = [np.ascontiguousarray(x[b, :L, :].T) for b in range(x.shape[0])]
    return shared, xs


def kernel(**inputs):
    L = 4096
    shared, xs = _host_inputs(inputs, L)
    nc = build_program(L)
    in_maps = [dict(shared, x_f=xs[b]) for b in range(8)]
    res = run_bass_kernel_spmd(nc, in_maps, core_ids=list(range(8)))
    out = np.empty((8, L, D), np.float32)
    for b in range(8):
        out[b] = np.asarray(res.results[b]["out_f"], dtype=np.float32).T
    return out


# version 59
# speedup vs baseline: 1.1727x; 1.0205x over previous
import contextlib
import numpy as np
import concourse.bass as bass
import concourse.mybir as mybir
from concourse.bass_utils import run_bass_kernel_spmd

F32 = mybir.dt.float32
F16 = mybir.dt.float16
BF16 = mybir.dt.bfloat16
AF = mybir.ActivationFunctionType
OP = mybir.AluOpType

D = 1024
DI = 2048
NS = 16
DTR = 64
T = 256
TS = 512
NFC = D // 128
NDC = DI // 128
EPS = 1e-5

ENGS = ("pe", "act", "dve", "pool", "sq")
DMA_SEMS = ("xin0", "xin1", "xin2", "xin3", "xout0", "xout1", "xout2", "xout3",
            "win0", "win1", "win2", "key0", "key1", "key2", "val0", "val1", "val2",
            "stgi0", "stgi1", "scr0", "scr1", "dir",
            "rso0", "rso1", "rsi0", "rsi1", "rbo0", "rbo1", "rbi0", "rbi1",
            "rco0", "rco1", "rci0", "rci1")


class Dep:
    __slots__ = ("w", "r")

    def __init__(self):
        self.w = None
        self.r = []


class Sched:

    def __init__(self):
        self.ops = {e: [] for e in ENGS}
        self.cnt = {}
        self.waited = {e: {} for e in ENGS}
        self.clocks = {}
        self.tag = "init"

    def emit(self, eng, fn, reads=(), writes=(), dma=None, extra_waits=()):
        need = {}

        def add(e, c):
            if c > need.get(e, 0):
                need[e] = c

        for d in reads:
            if d.w:
                add(*d.w)
        for d in writes:
            if d.w:
                add(*d.w)
            for (e, c) in d.r:
                add(e, c)
        for (e, c) in extra_waits:
            add(e, c)
        waits = []
        vc = self.waited[eng]
        newvc = dict(vc)
        for e, c in need.items():
            pc = self.clocks.get((e, c))
            if pc:
                for e2, c2 in pc.items():
                    if c2 > newvc.get(e2, 0):
                        newvc[e2] = c2
            if c > newvc.get(e, 0) and not (e == eng and eng != "pool"):
                newvc[e] = c
            ec = c if e != eng else min(c, self.cnt.get(eng, 0))
            if e == eng:
                if eng != "pool":
                    continue
            if ec > vc.get(e, 0):
                waits.append((e, ec))
                newvc[e] = max(newvc.get(e, 0), ec)
        self.waited[eng] = newvc
        if dma is not None:
            self.cnt[dma] = self.cnt.get(dma, 0) + 16
            me = (dma, self.cnt[dma])
            self.ops[eng].append((fn, waits, dma, self.tag))
            self.clocks[me] = dict(self.waited[eng])
        else:
            self.cnt[eng] = self.cnt.get(eng, 0) + 1
            me = (eng, self.cnt[eng])
            self.ops[eng].append((fn, waits, eng, self.tag))
            snap = dict(self.waited[eng])
            snap[eng] = self.cnt[eng]
            self.clocks[me] = snap
        for d in writes:
            d.w = me
            d.r = []
        for d in reads:
            d.r.append(me)


def build_program(L=4096):
    NPAIR = L // TS
    nc = bass.Bass("TRN2", target_bir_lowering=False, debug=False)

    x_in_d = nc.dram_tensor("x_f", [D, L], F32, kind="ExternalInput").ap()
    out_d = nc.dram_tensor("out_f", [D, L], F32, kind="ExternalOutput").ap()
    w_in_d = nc.dram_tensor("w_in", [D, 2 * DI], F32, kind="ExternalInput").ap()
    keys_d = nc.dram_tensor("keys_r", [16, 128, 1024], F32, kind="ExternalInput").ap()
    vals_d = nc.dram_tensor("vals_r", [16, 128, 1024], F32, kind="ExternalInput").ap()
    wout_d = nc.dram_tensor("wout_r", [DI, D], F32, kind="ExternalInput").ap()
    convd_d = nc.dram_tensor("convd_r", [NDC, 128, 512], F32, kind="ExternalInput").ap()
    xproj_d = nc.dram_tensor("xproj_r", [DI, 128], F32, kind="ExternalInput").ap()
    wdt_d = nc.dram_tensor("wdt_r", [DTR + 1, DI], F32, kind="ExternalInput").ap()
    selw_d = nc.dram_tensor("selw_r", [D, 16], F32, kind="ExternalInput").ap()
    alog_d = nc.dram_tensor("alog_r", [128, 256], F32, kind="ExternalInput").ap()
    dsk_d = nc.dram_tensor("dsk_r", [128, 16], F32, kind="ExternalInput").ap()
    cb_d = nc.dram_tensor("cb_r", [128, 16], F32, kind="ExternalInput").ap()
    bdt_d = nc.dram_tensor("bdt_r", [128, 16], F32, kind="ExternalInput").ap()

    win_bf = nc.dram_tensor("win_bf", [D, 2 * DI], BF16, kind="Internal").ap()
    keys_bf = nc.dram_tensor("keys_bf", [16, 128, 1024], BF16, kind="Internal").ap()
    vals_bf = nc.dram_tensor("vals_bf", [16, 128, 1024], BF16, kind="Internal").ap()
    wout_bf = nc.dram_tensor("wout_bf", [DI, D], BF16, kind="Internal").ap()
    convd_bf = nc.dram_tensor("convd_bf", [NDC, 128, 512], BF16, kind="Internal").ap()
    selscr = nc.dram_tensor("selscr", [2, 16, T], BF16, kind="Internal").ap()
    bscr = nc.dram_tensor("bscr", [2, 16, T], BF16, kind="Internal").ap()
    cscr = nc.dram_tensor("cscr", [2, 16, T], BF16, kind="Internal").ap()

    es = contextlib.ExitStack()

    def sb(name, shape, dt):
        return es.enter_context(nc.sbuf_tensor(name, shape, dt))

    xB = [sb(f"x{i}", [128, NFC, T], F32) for i in range(4)]
    xn2 = [sb(f"xn{i}", [128, NFC, T], BF16) for i in range(2)]
    x_inb = sb("x_inb", [128, 8, T + 3], BF16)
    tails = sb("tails", [128, NDC, 3], BF16)
    uB = sb("uB", [128, NDC, TS], BF16)
    zB = sb("zB", [128, NDC, TS], BF16)
    deltaB = sb("deltaB", [128, NDC, TS], BF16)
    ygB = sb("ygB", [128, NDC, TS], BF16)
    BrepB = sb("BrepB", [128, NS, TS], BF16)
    CrepB = sb("CrepB", [128, NS, TS], BF16)
    Hbig = sb("Hbig", [128, NS, TS], BF16)
    dA_s = [sb(f"dA{i}", [128, TS], F16) for i in range(8)]
    du_s = [sb("du0", [128, TS], BF16)]
    y_s = du_s[0]
    uX = [sb(f"uX{i}", [128, 2, TS], BF16) for i in range(2)]
    hcar = sb("hcar", [128, 256], BF16)
    A_sc = sb("A_sc", [128, 256], F32)
    dsk_sb = sb("dsk_sb", [128, 16], F32)
    cb_sb = sb("cb_sb", [128, 16], F32)
    smT = sb("smT", [128, 4, T], F32)
    sm = [smT[:, i, :] for i in range(4)]
    tres = sm[2]
    selrep = sb("selrep", [128, 16, T], BF16)
    xb16 = selrep[:, 0:2 * NFC:2, :]
    x2b = selrep[:, 1:2 * NFC:2, :]

    scr_sb = [sb(f"scr{i}", [128, T], BF16) for i in range(2)]
    dtlr_sb = sb("dtlr_sb", [DTR + 1, T], BF16)
    bcs_sb = sb("bcs_sb", [80, T], BF16)
    Bsb = bcs_sb[0:16, :]
    Csb = bcs_sb[32:48, :]
    sel_sb = bcs_sb[64:80, :]
    ones_bf = sb("ones_bf", [128, 128], BF16)
    cpool = sb("cpool", [128, 8], F32)
    xproj_sb = sb("xproj_sb", [128, 16 * 128], BF16)
    wdt_sb = sb("wdt_sb", [DTR + 1, DI], BF16)
    selw_sb = sb("selw_sb", [128, NFC * 16], BF16)
    win_db = [sb(f"windb{i}", [128, 1024], BF16) for i in range(3)]
    key_db = [sb(f"keydb{i}", [128, 1024], BF16) for i in range(2)]
    val_db = [sb(f"valdb{i}", [128, 1024], BF16) for i in range(2)]
    stg32 = [ygB[:, 0:2, :].bitcast(F32).rearrange("p a b -> p (a b)"),
             ygB[:, 2:4, :].bitcast(F32).rearrange("p a b -> p (a b)")]
    stgb = [Hbig[:, 0, :], CrepB[:, 0, :]]

    pbank = [es.enter_context(nc.psum_tensor(f"pb{i}", [128, 512], F32)) for i in range(8)]
    pslot = [pbank[i][:, 0:T] for i in range(8)]

    CONSTS = [EPS, 1.0, -1.0, -0.5, 1.0 / D, 0.5, 0.0, 2.0]
    for ci, cv in enumerate(CONSTS):
        nc.const_aps.register(F32, cv, cpool[:, ci:ci + 1])

    sems = {e: es.enter_context(nc.semaphore(f"sem_{e}")) for e in ENGS}
    for s in DMA_SEMS:
        sems[s] = es.enter_context(nc.semaphore(f"sem_{s}"))

    S = Sched()
    E = S.emit

    d_x = [Dep() for _ in range(4)]
    d_xn2 = [Dep(), Dep()]
    d_xin_body = Dep(); d_xin_tail = Dep()
    d_u = [Dep() for _ in range(NDC)]
    d_z = [Dep() for _ in range(NDC)]
    d_delta = [Dep() for _ in range(NDC)]
    d_yg = [Dep() for _ in range(NDC)]
    d_Brep = Dep(); d_Crep = Dep(); d_H = Dep()
    d_dA = [Dep() for _ in range(8)]
    d_du = [Dep()]
    d_hcar = Dep()
    d_sm = [Dep() for _ in range(4)]
    d_tres = d_sm[2]
    d_selrep = [Dep() for _ in range(16)]; d_selsb = Dep()
    d_scr = [Dep() for _ in range(2)]
    d_dtlr = Dep(); d_Bsb = Dep(); d_Csb = Dep()
    d_slot = [Dep() for _ in range(8)]
    d_windb = [Dep() for _ in range(3)]
    d_keydb = [Dep() for _ in range(2)]
    d_valdb = [Dep() for _ in range(2)]
    d_xin8 = [Dep() for _ in range(8)]
    d_tails = Dep()
    d_stg32 = [[d_yg[0], d_yg[1]], [d_yg[2], d_yg[3]]]
    d_stgb = [[d_H], [d_Crep]]
    d_const = Dep()
    d_dsk = Dep(); d_cb = Dep(); d_bdt = Dep(); d_Asc = Dep()
    d_xpw = Dep(); d_wdtw = Dep(); d_selww = Dep()
    d_selscr = [Dep(), Dep()]; d_bscr = [Dep(), Dep()]; d_cscr = [Dep(), Dep()]

    for ci, cv in enumerate(CONSTS):
        E("dve", lambda e, ci=ci, cv=cv: e.memset(cpool[:, ci:ci + 1], cv), writes=[d_const])
    E("dve", lambda e: e.memset(hcar[:], 0.0), writes=[d_hcar])
    E("dve", lambda e: e.memset(tails[:], 0.0), writes=[d_tails])
    E("dve", lambda e: e.memset(ones_bf[:], 1.0), writes=[d_const])
    E("dve", lambda e: e.memset(dtlr_sb[64:65, :], 1.0), writes=[d_const])
    for _e in ("pe", "act", "pool", "sq"):
        E(_e, lambda e: e.nop(), reads=[d_const])

    E("sq", lambda e: e.dma_start(out=dsk_sb[:], in_=dsk_d), writes=[d_dsk], dma="dir")
    E("sq", lambda e: e.dma_start(out=cb_sb[:], in_=cb_d), writes=[d_cb], dma="dir")
    E("sq", lambda e: e.dma_start(out=sm[0][:], in_=alog_d), writes=[d_sm[0]], dma="dir")
    dir_total = ("dir", S.cnt["dir"])
    for d in (d_dsk, d_cb, d_sm[0]):
        d.w = dir_total
    E("act", lambda e: e.activation(out=sm[1][:], in_=sm[0][:], func=AF.Exp),
      reads=[d_sm[0]], writes=[d_sm[1]])
    E("pool", lambda e: e.tensor_scalar_mul(out=A_sc[:], in0=sm[1][:], scalar1=-1.0),
      reads=[d_sm[1], d_const], writes=[d_Asc])

    rr = [0]

    def conv_round(src_ap, dst_sb=None, dst_sb_dep=None, dst_dram=None, nparts=128):
        s = rr[0] % 2
        rr[0] += 1
        wdt = src_ap.shape[-1]
        E("sq", lambda e, a=src_ap, s=s, p=nparts, w=wdt: e.dma_start(
            out=stg32[s][0:p, :w], in_=a),
          writes=d_stg32[s], dma=f"stgi{s}")
        if dst_sb is not None:
            E("dve", lambda e, d=dst_sb, s=s, p=nparts, w=wdt: e.tensor_copy(
                out=d, in_=stg32[s][0:p, :w]),
              reads=d_stg32[s], writes=[dst_sb_dep])
        else:
            E("dve", lambda e, s=s, p=nparts, w=wdt: e.tensor_copy(
                out=stgb[s][0:p, :w], in_=stg32[s][0:p, :w]),
              reads=d_stg32[s], writes=d_stgb[s])
            E("sq", lambda e, a=dst_dram, s=s, p=nparts, w=wdt: e.dma_start(
                out=a, in_=stgb[s][0:p, :w]),
              reads=d_stgb[s], dma=f"scr{s}")

    for kc in range(NFC):
        for j in range(8):
            conv_round(w_in_d[kc * 128:(kc + 1) * 128, j * 512:(j + 1) * 512],
                       dst_dram=win_bf[kc * 128:(kc + 1) * 128, j * 512:(j + 1) * 512])
    for ei in range(16):
        for j in range(2):
            conv_round(keys_d[ei, :, j * 512:(j + 1) * 512],
                       dst_dram=keys_bf[ei, :, j * 512:(j + 1) * 512])
    for ei in range(16):
        for j in range(2):
            conv_round(vals_d[ei, :, j * 512:(j + 1) * 512],
                       dst_dram=vals_bf[ei, :, j * 512:(j + 1) * 512])
    for kc in range(NDC):
        for j in range(2):
            conv_round(wout_d[kc * 128:(kc + 1) * 128, j * 512:(j + 1) * 512],
                       dst_dram=wout_bf[kc * 128:(kc + 1) * 128, j * 512:(j + 1) * 512])
    for dc in range(NDC):
        conv_round(convd_d[dc], dst_dram=convd_bf[dc])
    for kc in range(NDC):
        conv_round(xproj_d[kc * 128:(kc + 1) * 128, :],
                   dst_sb=xproj_sb[:, kc * 128:(kc + 1) * 128], dst_sb_dep=d_xpw)
    for j in range(4):
        conv_round(wdt_d[:, j * 512:(j + 1) * 512],
                   dst_sb=wdt_sb[:, j * 512:(j + 1) * 512], dst_sb_dep=d_wdtw, nparts=DTR + 1)
    for kc in range(NFC):
        conv_round(selw_d[kc * 128:(kc + 1) * 128, :],
                   dst_sb=selw_sb[:, kc * 16:(kc + 1) * 16], dst_sb_dep=d_selww)

    SCR_TOTALS = [("scr0", S.cnt.get("scr0", 0)), ("scr1", S.cnt.get("scr1", 0))]

    x_src = x_in_d.rearrange("(fc p) L -> p fc L", p=128)
    out_dst = out_d.rearrange("(fc p) L -> p fc L", p=128)

    def dma_x_in(c):
        xb = xB[c % 4]
        E("sq", lambda e, c=c, xb=xb: e.dma_start(out=xb[:], in_=x_src[:, :, c * T:(c + 1) * T]),
          writes=[d_x[c % 4]], dma=f"xin{c % 4}")

    def dma_x_out(c):
        xb = xB[c % 4]
        E("sq", lambda e, c=c, xb=xb: e.dma_start(out=out_dst[:, :, c * T:(c + 1) * T], in_=xb[:]),
          reads=[d_x[c % 4]], dma=f"xout{c % 4}")

    def ln(c):
        xb = xB[c % 4]
        dx = d_x[c % 4]
        E("pool", lambda e, xb=xb: e.tensor_copy(out=xb16, in_=xb[:]),
          reads=[dx], writes=d_selrep[0::2])
        E("act", lambda e, xb=xb: e.activation(out=x2b, in_=xb[:], func=AF.Square),
          reads=[dx], writes=d_selrep[1::2])
        psq = pbank[4][:, 0:512]
        for fc in range(NFC):
            E("pe", lambda e, fc=fc, psq=psq: e.matmul(
                out=psq, lhsT=ones_bf[:],
                rhs=selrep[:, 2 * fc:2 * fc + 2, :].rearrange("p a b -> p (a b)"),
                start=(fc == 0), stop=(fc == NFC - 1)),
              reads=[d_selrep[2 * fc], d_selrep[2 * fc + 1], d_const], writes=[d_slot[4]])
        E("act", lambda e, psq=psq: e.activation(out=sm[0][:], in_=psq[:, 0:T], func=AF.Copy, scale=1.0 / D),
          reads=[d_slot[4]], writes=[d_sm[0]])
        E("act", lambda e, psq=psq: e.activation(out=sm[1][:], in_=psq[:, T:2 * T], func=AF.Copy, scale=1.0 / D),
          reads=[d_slot[4]], writes=[d_sm[1]])
        E("pool", lambda e: e.tensor_tensor(out=sm[2][:], in0=sm[0][:], in1=sm[0][:], op=OP.mult),
          reads=[d_sm[0]], writes=[d_sm[2]])
        E("pool", lambda e: e.tensor_tensor(out=sm[1][:], in0=sm[1][:], in1=sm[2][:], op=OP.subtract),
          reads=[d_sm[2]], writes=[d_sm[1]])
        E("act", lambda e: e.activation(out=sm[1][:], in_=sm[1][:], func=AF.Ln, bias=EPS),
          writes=[d_sm[1]])
        E("act", lambda e: e.activation(out=sm[1][:], in_=sm[1][:], func=AF.Exp, scale=-0.5),
          writes=[d_sm[1]])
        E("pool", lambda e, xb=xb: e.tensor_tensor(
            out=xn2[c % 2][:], in0=xb[:],
            in1=sm[0][:].unsqueeze(1).broadcast_to([128, NFC, T]), op=OP.subtract),
          reads=[dx, d_sm[0]], writes=[d_xn2[c % 2]])
        E("pool", lambda e: e.tensor_tensor(
            out=xn2[c % 2][:], in0=xn2[c % 2][:],
            in1=sm[1][:].unsqueeze(1).broadcast_to([128, NFC, T]), op=OP.mult),
          reads=[d_sm[1]], writes=[d_xn2[c % 2]])

    def ffn_pieces(c, scale):
        xnc = xn2[c % 2]
        d_xnc = d_xn2[c % 2]
        xb = xB[c % 4]
        dx = d_x[c % 4]

        def p_sel():
            S.tag = "ffn_sel"
            psel = pslot[5]
            for kc in range(NFC):
                E("pe", lambda e, kc=kc, psel=psel: e.matmul(
                    out=psel[0:16, :], lhsT=selw_sb[:, kc * 16:(kc + 1) * 16], rhs=xnc[:, kc, :],
                    start=(kc == 0), stop=(kc == NFC - 1)),
                  reads=[d_xnc, d_selww], writes=[d_slot[5]])
            E("act", lambda e, psel=psel: e.activation(out=bcs_sb[64:80, :], in_=psel[0:16, :], func=AF.Sigmoid),
              reads=[d_slot[5]], writes=[d_selsb])
            q = c % 2
            E("pool", lambda e, q=q: e.dma_start(out=selscr[q], in_=sel_sb[:]),
              reads=[d_selsb], writes=[d_selscr[q]], dma=f"rso{q}")
            E("pool", lambda e, q=q: e.dma_start(
                out=selrep[:], in_=selscr[q].unsqueeze(0).broadcast_to([128, 16, T])),
              reads=[d_selscr[q]], writes=d_selrep, dma=f"rsi{q}")

        def mk_p1(e0):
            def go():
                S.tag = "ffn_p1"
                for ei in (e0, e0 + 1):
                    pk = ei % 2
                    p = ei % 2
                    sl = 6 + (ei % 2)
                    E("sq", lambda e, ei=ei, pk=pk: e.dma_start(out=key_db[pk][:], in_=keys_bf[ei]),
                      writes=[d_keydb[pk]], dma=f"key{pk}", extra_waits=SCR_TOTALS)
                    for kc in range(NFC):
                        E("pe", lambda e, kc=kc, sl=sl, pk=pk: e.matmul(
                            out=pslot[sl], lhsT=key_db[pk][:, kc * 128:(kc + 1) * 128], rhs=xnc[:, kc, :],
                            start=(kc == 0), stop=(kc == NFC - 1)),
                          reads=[d_keydb[pk], d_xnc], writes=[d_slot[sl]])
                    E("act", lambda e, sl=sl, p=p: e.activation(out=scr_sb[p][:], in_=pslot[sl], func=AF.Relu),
                      reads=[d_slot[sl]], writes=[d_scr[p]])
                    E("pool", lambda e, ei=ei, p=p: e.tensor_tensor(out=selrep[:, ei, :], in0=scr_sb[p][:],
                                                                    in1=selrep[:, ei, :], op=OP.mult),
                      reads=[d_scr[p]], writes=[d_selrep[ei]])
            return go

        def mk_p2(e0):
            def go():
                S.tag = "ffn_p2"
                for ei in (e0, e0 + 1):
                    p = ei % 2
                    E("sq", lambda e, ei=ei, p=p: e.dma_start(out=val_db[p][:], in_=vals_bf[ei]),
                      writes=[d_valdb[p]], dma=f"val{p}", extra_waits=SCR_TOTALS)
                    for vc in range(8):
                        E("pe", lambda e, ei=ei, vc=vc, p=p: e.matmul(
                            out=pslot[vc], lhsT=val_db[p][:, vc * 128:(vc + 1) * 128],
                            rhs=selrep[:, ei, :],
                            start=(ei == 0), stop=(ei == 15)),
                          reads=[d_valdb[p], d_selrep[ei]], writes=[d_slot[vc]])
            return go

        def p_evac():
            S.tag = "ffn_evac"
            for vc in range(8):
                tr = sm[2 + vc % 2]
                dtr = d_sm[2 + vc % 2]
                E("act", lambda e, vc=vc, scale=scale, tr=tr: e.activation(
                    out=tr[:], in_=pslot[vc], func=AF.Copy, scale=scale),
                  reads=[d_slot[vc]], writes=[dtr])
                E("pool", lambda e, vc=vc, xb=xb, tr=tr: e.tensor_tensor(
                    out=xb[:, vc, :], in0=xb[:, vc, :], in1=tr[:], op=OP.add),
                  reads=[dtr, dx], writes=[dx])

        return ([(0, p_sel)] + [(0, mk_p1(e0)) for e0 in range(0, 16, 2)]
                + [(0, mk_p2(e0)) for e0 in range(0, 16, 2)] + [(0, p_evac)])

    def win_pieces(c, half):
        cip = c % 2
        xnc = xn2[c % 2]
        d_xnc = d_xn2[c % 2]

        def mk(h, q, inline_ev=False):
            def mms():
                for kc in range(NFC):
                    p = (q * NFC + kc) % 3
                    cols = h * 2048 + q * 1024
                    E("sq", lambda e, kc=kc, p=p, cols=cols: e.dma_start(
                        out=win_db[p][:], in_=win_bf[kc * 128:(kc + 1) * 128, cols:cols + 1024]),
                      writes=[d_windb[p]], dma=f"win{p}", extra_waits=SCR_TOTALS)
                    for m in range(8):
                        E("pe", lambda e, kc=kc, m=m, p=p: e.matmul(
                            out=pslot[m], lhsT=win_db[p][:, m * 128:(m + 1) * 128], rhs=xnc[:, kc, :],
                            start=(kc == 0), stop=(kc == NFC - 1)),
                          reads=[d_windb[p], d_xnc], writes=[d_slot[m]])
                if inline_ev:
                    evs()

            def evs():
                for m in range(8):
                    mc = q * 8 + m
                    if h == 0:
                        E("act", lambda e, m=m, mc=mc: e.activation(
                            out=x_inb[:, mc % 8, 3:3 + T], in_=pslot[m], func=AF.Copy),
                          reads=[d_slot[m]], writes=[d_xin8[mc % 8]])
                    else:
                        E("act", lambda e, m=m, mc=mc, cip=cip: e.activation(
                            out=zB[:, mc, cip * T:(cip + 1) * T], in_=pslot[m], func=AF.Silu),
                          reads=[d_slot[m]], writes=[d_z[mc]])
            return mms, evs
        return mk

    def conv_pieces(c):
        cip = c % 2
        par = (c // 2) % 2

        def mk(d0):
            def go():
                s0 = d0 % 8
                xds = [d_xin8[(d0 + i) % 8] for i in range(4)]
                E("act", lambda e, d0=d0, s0=s0: e.activation(
                    out=x_inb[:, s0:s0 + 4, 0:3], in_=tails[:, d0:d0 + 4, :], func=AF.Copy),
                  reads=[d_tails], writes=xds)
                for dc in range(d0, d0 + 4):
                    s8 = dc % 8
                    p = dc % 2
                    E("sq", lambda e, dc=dc, p=p: e.dma_start(out=key_db[p][:, 0:512], in_=convd_bf[dc]),
                      writes=[d_keydb[p]], dma=f"key{p}", extra_waits=SCR_TOTALS)
                    sl = 6 + (dc % 2)
                    for k in range(4):
                        E("pe", lambda e, dc=dc, k=k, p=p, sl=sl, s8=s8: e.matmul(
                            out=pslot[sl], lhsT=key_db[p][:, k * 128:(k + 1) * 128],
                            rhs=x_inb[:, s8, k:k + T],
                            start=(k == 0), stop=(k == 3)),
                          reads=[d_keydb[p], d_xin8[s8]], writes=[d_slot[sl]])
                    udst = (uB[:, dc, cip * T:(cip + 1) * T] if dc < 14
                            else uX[par][:, dc - 14, cip * T:(cip + 1) * T])
                    E("act", lambda e, dc=dc, sl=sl, udst=udst: e.activation(
                        out=udst, in_=pslot[sl], func=AF.Silu,
                        bias=cb_sb[:, dc:dc + 1]),
                      reads=[d_slot[sl], d_cb], writes=[d_u[dc]])
                E("act", lambda e, d0=d0, s0=s0: e.activation(
                    out=tails[:, d0:d0 + 4, :], in_=x_inb[:, s0:s0 + 4, T:T + 3], func=AF.Copy),
                  reads=xds, writes=[d_tails])
            return go
        return [(d0 + 3, mk(d0)) for d0 in (0, 4, 8)] + [(13, mk(12))]

    def xproj_pieces(c):
        cip = c % 2

        def p_xproj():
            S.tag = "xproj"
            px = pslot[5]
            par = (c // 2) % 2
            for kc in range(NDC):
                usrc = (uB[:, kc, cip * T:(cip + 1) * T] if kc < 14
                        else uX[par][:, kc - 14, cip * T:(cip + 1) * T])
                E("pe", lambda e, kc=kc, px=px, usrc=usrc: e.matmul(
                    out=px, lhsT=xproj_sb[:, kc * 128:(kc + 1) * 128],
                    rhs=usrc,
                    start=(kc == 0), stop=(kc == NDC - 1)),
                  reads=[d_u[kc], d_xpw], writes=[d_slot[5]])
            E("act", lambda e, px=px: e.activation(out=dtlr_sb[0:64, :], in_=px[64:128, :], func=AF.Copy),
              reads=[d_slot[5]], writes=[d_dtlr])
            E("act", lambda e, px=px: e.activation(out=Bsb[:], in_=px[0:16, :], func=AF.Copy),
              reads=[d_slot[5]], writes=[d_Bsb])
            E("act", lambda e, px=px: e.activation(out=bcs_sb[32:48, :], in_=px[32:48, :], func=AF.Copy),
              reads=[d_slot[5]], writes=[d_Csb])

        def p_bounce_out():
            q = c % 2
            E("pool", lambda e, q=q: e.dma_start(out=bscr[q], in_=Bsb[:]),
              reads=[d_Bsb], writes=[d_bscr[q]], dma=f"rbo{q}")
            E("pool", lambda e, q=q: e.dma_start(out=cscr[q], in_=Csb[:]),
              reads=[d_Csb], writes=[d_cscr[q]], dma=f"rco{q}")

        def p_bcastB():
            q = c % 2
            E("pool", lambda e, q=q, cip=cip: e.dma_start(
                out=BrepB[:, :, cip * T:(cip + 1) * T],
                in_=bscr[q].unsqueeze(0).broadcast_to([128, 16, T])),
              reads=[d_bscr[q]], writes=[d_Brep], dma=f"rbi{q}")

        def p_bcastC():
            q = c % 2
            E("pool", lambda e, q=q, cip=cip: e.dma_start(
                out=CrepB[:, :, cip * T:(cip + 1) * T],
                in_=cscr[q].unsqueeze(0).broadcast_to([128, 16, T])),
              reads=[d_cscr[q]], writes=[d_Crep], dma=f"rci{q}")

        pend = []

        def emit_dt_stt():
            mc, bk, lvd = pend.pop(0)
            E("dve", lambda e, mc=mc, cip=cip, bk=bk, lvd=lvd: e.scalar_tensor_tensor(
                out=deltaB[:, mc:mc + 2, cip * T:(cip + 1) * T],
                in0=pbank[bk][:, 0:2 * T], scalar=0.0, in1=smT[:, lvd:lvd + 2, :],
                op0=OP.max, op1=OP.add),
              reads=[d_slot[bk], d_sm[lvd], d_sm[lvd + 1], d_const],
              writes=[d_delta[mc], d_delta[mc + 1]])

        def mk_dt(m0):
            def go():
                S.tag = "dt"
                for mc in range(m0, m0 + 4, 2):
                    if len(pend) >= 2:
                        emit_dt_stt()
                    pr = (mc // 2) % 2
                    bk = 6 + pr
                    lvd = 2 * pr
                    full = pbank[bk][:, 0:2 * T]
                    lv = smT[:, lvd:lvd + 2, :].rearrange("p a b -> p (a b)")
                    E("pe", lambda e, mc=mc, bk=bk: e.matmul(
                        out=pbank[bk][:, 0:T], lhsT=wdt_sb[:, mc * 128:(mc + 1) * 128], rhs=dtlr_sb[:],
                        start=True, stop=True),
                      reads=[d_dtlr, d_wdtw], writes=[d_slot[bk]])
                    E("pe", lambda e, mc=mc, bk=bk: e.matmul(
                        out=pbank[bk][:, T:2 * T], lhsT=wdt_sb[:, (mc + 1) * 128:(mc + 2) * 128], rhs=dtlr_sb[:],
                        start=True, stop=True),
                      reads=[d_dtlr, d_wdtw], writes=[d_slot[bk]])
                    E("act", lambda e, bk=bk, lv=lv, full=full: e.activation(out=lv, in_=full, func=AF.Abs),
                      reads=[d_slot[bk]], writes=[d_sm[lvd], d_sm[lvd + 1]])
                    E("act", lambda e, lv=lv: e.activation(out=lv, in_=lv, func=AF.Exp, scale=-1.0),
                      writes=[d_sm[lvd], d_sm[lvd + 1]])
                    E("act", lambda e, lv=lv: e.activation(out=lv, in_=lv, func=AF.Ln, bias=1.0),
                      writes=[d_sm[lvd], d_sm[lvd + 1]])
                    pend.append((mc, bk, lvd))
                if m0 + 4 >= NDC - 4:
                    while pend:
                        emit_dt_stt()
            return go
        return ([(13, p_xproj), (13, p_bounce_out)]
                + [(13, mk_dt(m0)) for m0 in (0, 4, 8)] + [(15, mk_dt(12))]
                + [(15, p_bcastB), (15, p_bcastC)])

    def scan_dc_q(P, dc, qtr):
        S.tag = "scan"
        usrc = uB[:, dc, :] if dc < 14 else uX[P % 2][:, dc - 14, :]
        if qtr == 0:
            E("dve", lambda e, dc=dc, usrc=usrc: e.tensor_tensor(
                out=du_s[0][:], in0=deltaB[:, dc, :], in1=usrc, op=OP.mult),
              reads=[d_delta[dc], d_u[dc]], writes=[d_du[0]])
            E("dve", lambda e, dc=dc: e.tensor_tensor(
                out=Hbig[:], in0=du_s[0][:].unsqueeze(1).broadcast_to([128, NS, TS]),
                in1=BrepB[:], op=OP.mult),
              reads=[d_du[0], d_Brep], writes=[d_H])
        for n in range(4 * qtr, 4 * qtr + 4):
            ap_ = dA_s[n % 8]
            E("dve", lambda e, dc=dc, n=n, ap_=ap_: e.tensor_tensor_scan(
                out=Hbig[:, n, :], data0=ap_[:], data1=Hbig[:, n, :],
                initial=hcar[:, dc * NS + n:dc * NS + n + 1], op0=OP.mult, op1=OP.add),
              reads=[d_dA[n % 8], d_hcar], writes=[d_H])
        if qtr == 3:
            E("dve", lambda e, dc=dc: e.tensor_copy(
                out=hcar[:, dc * NS:(dc + 1) * NS], in_=Hbig[:, :, TS - 1:TS].squeeze(2)),
              reads=[d_H], writes=[d_hcar])
            E("dve", lambda e: e.tensor_tensor(out=Hbig[:], in0=Hbig[:], in1=CrepB[:], op=OP.mult),
              reads=[d_H, d_Crep], writes=[d_H])
            E("dve", lambda e: e.tensor_tensor(out=Hbig[:, 0:8, :], in0=Hbig[:, 0:8, :],
                                               in1=Hbig[:, 8:16, :], op=OP.add),
              reads=[d_H], writes=[d_H])
            E("dve", lambda e: e.tensor_tensor(out=Hbig[:, 0:4, :], in0=Hbig[:, 0:4, :],
                                               in1=Hbig[:, 4:8, :], op=OP.add),
              reads=[d_H], writes=[d_H])
            E("dve", lambda e: e.tensor_tensor(out=Hbig[:, 0:2, :], in0=Hbig[:, 0:2, :],
                                               in1=Hbig[:, 2:4, :], op=OP.add),
              reads=[d_H], writes=[d_H])
            E("dve", lambda e: e.tensor_tensor(out=y_s[:], in0=Hbig[:, 0, :],
                                               in1=Hbig[:, 1, :], op=OP.add),
              reads=[d_H], writes=[d_du[0]])
            E("dve", lambda e, dc=dc, usrc=usrc: e.scalar_tensor_tensor(
                out=y_s[:], in0=usrc, scalar=dsk_sb[:, dc:dc + 1], in1=y_s[:],
                op0=OP.mult, op1=OP.add),
              reads=[d_u[dc], d_dsk, d_du[0]], writes=[d_du[0]])
            E("dve", lambda e, dc=dc: e.tensor_tensor(
                out=ygB[:, dc, :], in0=y_s[:], in1=zB[:, dc, :], op=OP.mult),
              reads=[d_du[0], d_z[dc]], writes=[d_yg[dc]])
        for j in range(4):
            nn = 4 * qtr + 8 + j
            gg = P * NDC + dc + nn // NS
            n2 = nn % NS
            if gg >= NPAIR * NDC:
                continue
            d2 = gg % NDC
            ap_ = dA_s[n2 % 8]
            E("act", lambda e, d2=d2, n2=n2, ap_=ap_: e.activation(
                out=ap_[:], in_=deltaB[:, d2, :], func=AF.Exp,
                scale=A_sc[:, d2 * NS + n2:d2 * NS + n2 + 1]),
              reads=[d_delta[d2], d_Asc], writes=[d_dA[n2 % 8]])

    def wout_pieces(P):
        mm = []

        def mk_dc(dc):
            def go():
                S.tag = "wout"
                p = dc % 3
                E("sq", lambda e, dc=dc, p=p: e.dma_start(
                    out=win_db[p][:], in_=wout_bf[dc * 128:(dc + 1) * 128, :]),
                  writes=[d_windb[p]], dma=f"win{p}", extra_waits=SCR_TOTALS)
                for vc in range(8):
                    E("pe", lambda e, dc=dc, vc=vc, p=p: e.matmul(
                        out=pbank[vc][:, 0:512], lhsT=win_db[p][:, vc * 128:(vc + 1) * 128],
                        rhs=ygB[:, dc, :],
                        start=(dc == 0), stop=(dc == NDC - 1)),
                      reads=[d_windb[p], d_yg[dc]], writes=[d_slot[vc]])
            return go

        def mk_evac(h):
            def go():
                S.tag = "wout"
                c = 2 * P + h
                xb = xB[c % 4]
                dx = d_x[c % 4]
                for vc in range(8):
                    tr = sm[2 + vc % 2]
                    dtr = d_sm[2 + vc % 2]
                    E("act", lambda e, vc=vc, h=h, tr=tr: e.activation(
                        out=tr[:], in_=pbank[vc][:, h * T:(h + 1) * T], func=AF.Copy),
                      reads=[d_slot[vc]], writes=[dtr])
                    E("pool", lambda e, vc=vc, xb=xb, dx=dx, tr=tr: e.tensor_tensor(
                        out=xb[:, vc, :], in0=xb[:, vc, :], in1=tr[:], op=OP.add),
                      reads=[dtr, dx], writes=[dx])
            return go

        for dc in range(NDC):
            mm.append((0, mk_dc(dc)))
        ev = [(0, mk_evac(0)), (0, mk_evac(1))]
        return mm, ev

    def front_pieces(c):
        def p_dma():
            dma_x_in(c)

        def p_ln():
            S.tag = "ln"
            ln(c)

        def p_ln2():
            ln(c)
        wmk = win_pieces(c, 0)
        cps = conv_pieces(c)
        m00, e00 = wmk(0, 0)
        m01, e01 = wmk(0, 1)
        m10, e10 = wmk(1, 0)
        m11, _ = wmk(1, 1, inline_ev=True)
        kx = (lambda k: k) if c % 2 == 0 else (lambda k: 13)
        return ([(0, p_dma), (0, p_ln)] + ffn_pieces(c, 0.5) + [(0, p_ln2)]
                + [(kx(1), m00), (kx(2), e00), (kx(cps[0][0]), cps[0][1]), (kx(cps[1][0]), cps[1][1]),
                   (kx(7), m01), (kx(8), e01), (kx(cps[2][0]), cps[2][1]), (kx(cps[3][0]), cps[3][1]),
                   (8 if c % 2 == 0 else 10, m10), (10, e10), (15, m11)]
                + xproj_pieces(c))

    def tail_pieces(c):
        def p_ln():
            S.tag = "ln"
            ln(c)

        def p_out():
            dma_x_out(c)
        return [(0, p_ln)] + ffn_pieces(c, 0.5) + [(0, p_out)]

    for c in (0, 1):
        for _mdc, pc in front_pieces(c):
            pc()
    S.tag = "scan"
    for n in range(8):
        E("act", lambda e, n=n, ap_=dA_s[n % 8]: e.activation(
            out=ap_[:], in_=deltaB[:, 0, :], func=AF.Exp,
            scale=A_sc[:, n:n + 1]),
          reads=[d_delta[0], d_Asc], writes=[d_dA[n % 8]])
    for P in range(NPAIR):
        pieces = []
        wparts = {}
        if P >= 1:
            mm, ev = wout_pieces(P - 1)
            wparts[(0, 0)] = mm
            wparts[(0, 2)] = [ev[0]]
            wparts[(0, 3)] = [ev[1]]
            pieces += tail_pieces(2 * P - 2) + tail_pieces(2 * P - 1)
        if P + 1 <= NPAIR - 1:
            pieces += front_pieces(2 * P + 2) + front_pieces(2 * P + 3)
        pieces.sort(key=lambda t: t[0])
        n = len(pieces)
        k = (n + 4 * NDC - 1) // (4 * NDC)
        j = 0
        for dc in range(NDC):
            for qtr in range(4):
                scan_dc_q(P, dc, qtr)
                for _m, pc in wparts.pop((dc, qtr), ()):
                    pc()
                budget = k
                while j < n and (pieces[j][0] < dc or (pieces[j][0] == dc and qtr == 3)) and (budget > 0 or (dc == NDC - 1 and qtr == 3)):
                    pieces[j][1]()
                    j += 1
                    budget -= 1
        while j < n:
            pieces[j][1]()
            j += 1
    mm, ev = wout_pieces(NPAIR - 1)
    for _mdc, pc in mm + ev:
        pc()
    for c in (2 * NPAIR - 2, 2 * NPAIR - 1):
        for _mdc, pc in tail_pieces(c):
            pc()

    final_waits = [(s, S.cnt.get(s, 0)) for s in ("xout0", "xout1", "xout2", "xout3")
                   if S.cnt.get(s, 0) > 0]

    with nc.Block() as block:
        def make(eng_name):
            def body(eng):
                for fn, waits, sem_name, _tag in S.ops[eng_name]:
                    for (e2, c2) in waits:
                        eng.wait_ge(sems[e2], c2)
                    inst = fn(eng)
                    inst.then_inc(sems[sem_name], 16 if sem_name in DMA_SEMS else 1)
                if eng_name == "sq":
                    for (s, c2) in final_waits:
                        eng.wait_ge(sems[s], c2)
            return body
        block.tensor(make("pe"))
        block.scalar(make("act"))
        block.vector(make("dve"))
        block.gpsimd(make("pool"))
        block.sync(make("sq"))

    es.close()
    return nc


def _host_inputs(inputs, L=4096):
    W_in = np.ascontiguousarray(inputs["L0_attn_W_in"], dtype=np.float32)
    conv_w = np.asarray(inputs["L0_attn_conv_w"], dtype=np.float32)[:, 0, :]
    conv_b = np.asarray(inputs["L0_attn_conv_b"], dtype=np.float32)
    W_xp = np.asarray(inputs["L0_attn_W_xproj"], dtype=np.float32)
    W_dt0 = np.asarray(inputs["L0_attn_W_dt"], dtype=np.float32)
    b_dt0 = np.asarray(inputs["L0_attn_b_dt"], dtype=np.float32)
    W_dt = np.ascontiguousarray(np.concatenate([W_dt0, b_dt0[None, :]], axis=0))
    b_dt = np.asarray(inputs["L0_attn_b_dt"], dtype=np.float32)
    A_log = np.asarray(inputs["L0_attn_A_log"], dtype=np.float32)
    D_skip = np.asarray(inputs["L0_attn_D_skip"], dtype=np.float32)
    W_out = np.ascontiguousarray(inputs["L0_attn_W_out"], dtype=np.float32)
    selw = np.asarray(inputs["L0_ffn_expert_sel"], dtype=np.float32)
    keys = np.asarray(inputs["L0_ffn_keys"], dtype=np.float32)
    values = np.asarray(inputs["L0_ffn_values"], dtype=np.float32)

    keys_r = np.ascontiguousarray(
        keys.reshape(16, NFC, 128, 128).transpose(0, 2, 1, 3).reshape(16, 128, 1024))
    vals_r = np.ascontiguousarray(values)
    xproj_r = np.zeros((DI, 128), np.float32)
    xproj_r[:, 0:16] = W_xp[:, 64:80]
    xproj_r[:, 32:48] = W_xp[:, 80:96]
    xproj_r[:, 64:128] = W_xp[:, 0:64]
    selw_r = np.ascontiguousarray(selw.T)

    def di_layout(v):
        return np.ascontiguousarray(v.reshape(NDC, 128).T)
    alog_r = np.ascontiguousarray(
        A_log.reshape(NDC, 128, NS).transpose(1, 0, 2).reshape(128, NDC * NS))
    convd_r = np.zeros((NDC, 128, 4, 128), np.float32)
    idx = np.arange(128)
    for dc in range(NDC):
        for k in range(4):
            convd_r[dc, idx, k, idx] = conv_w[dc * 128 + idx, k]
    convd_r = np.ascontiguousarray(convd_r.reshape(NDC, 128, 512))
    shared = dict(
        w_in=W_in, keys_r=keys_r, vals_r=vals_r, wout_r=W_out,
        convd_r=convd_r, xproj_r=xproj_r, wdt_r=W_dt, selw_r=selw_r, alog_r=alog_r,
        dsk_r=di_layout(D_skip), cb_r=di_layout(conv_b), bdt_r=di_layout(b_dt),
    )
    x = np.asarray(inputs["L0_attn_x"], dtype=np.float32)
    xs = [np.ascontiguousarray(x[b, :L, :].T) for b in range(x.shape[0])]
    return shared, xs


def kernel(**inputs):
    L = 4096
    shared, xs = _host_inputs(inputs, L)
    nc = build_program(L)
    in_maps = [dict(shared, x_f=xs[b]) for b in range(8)]
    res = run_bass_kernel_spmd(nc, in_maps, core_ids=list(range(8)))
    out = np.empty((8, L, D), np.float32)
    for b in range(8):
        out[b] = np.asarray(res.results[b]["out_f"], dtype=np.float32).T
    return out
